# version 03
# baseline (speedup 1.0000x reference)
import numpy as np
from contextlib import ExitStack

import concourse.bass as bass
import concourse.mybir as mybir
from concourse.bass_utils import run_bass_kernel_spmd
from concourse.aot_env import allow_ici_all_to_all

F32 = mybir.dt.float32
BF16 = mybir.dt.bfloat16
AF = mybir.ActivationFunctionType
ALU = mybir.AluOpType
AX = mybir.AxisListType

NCORES = 8
D = 2048
HID = 8192
TOKC = 2048
TTOT = 16384
HSL = 256
NST = 16
RNK = 64
SC = 512
NSC = TTOT // SC
FT = 512
NFT = TOKC // FT
PT = 256
NPT = TOKC // PT

EPS1 = 1e-5
EPS2 = 1e-6
EPS3 = 1e-6
QEPS = 1e-6

GAMMA_SCALE = None


class Op:
    __slots__ = ("eng", "fn", "deps", "sem_key", "inc", "need_inc", "tag")

    def __init__(self, eng, fn, deps, sem_key, inc, tag):
        self.eng, self.fn, self.deps = eng, fn, deps
        self.sem_key, self.inc, self.tag = sem_key, inc, tag
        self.need_inc = False


class Prog:
    ENGINES = ("v", "s", "t", "g", "y")

    def __init__(self):
        self.streams = {e: [] for e in self.ENGINES}
        self.all = []
        self._ncc = 0

    def op(self, eng, fn, deps=(), tag=""):
        o = Op(eng, fn, [d for d in deps if d is not None], eng, 1, tag)
        self.streams[eng].append(o)
        self.all.append(o)
        return o

    def dma(self, eng, stream, out, in_, deps=(), tag="", slot=0):
        def fn(E, out=out, in_=in_):
            return E.dma_start(out=out, in_=in_)
        o = Op(eng, fn, [d for d in deps if d is not None],
               "q" + eng + stream + str(slot), 16, tag)
        self.streams[eng].append(o)
        self.all.append(o)
        return o

    def collective(self, fn, deps=(), tag="cc"):
        key = "cc%d" % self._ncc
        self._ncc += 1
        o = Op("g", fn, [d for d in deps if d is not None], key, 1, tag)
        self.streams["g"].append(o)
        self.all.append(o)
        return o

    def finalize(self):
        import os
        strict = os.environ.get("KSTRICT", "1") == "1"
        pos = {}
        for e in self.ENGINES:
            for i, o in enumerate(self.streams[e]):
                pos[id(o)] = (e, i)
        if not strict:
            for o in self.all:
                oe, oi = pos[id(o)]
                o.deps = [d for d in o.deps
                          if not (d.sem_key == d.eng == oe and pos[id(d)][1] < oi)]
        else:
            for e in self.ENGINES:
                st = self.streams[e]
                for i in range(1, len(st)):
                    st[i].deps.append(st[i - 1])
        for o in self.all:
            for d in o.deps:
                d.need_inc = True
        for o in self.all:
            if o.inc == 16 or o.sem_key.startswith("cc"):
                o.need_inc = True
        counters = {}
        self.val = {}
        for e in self.ENGINES:
            for o in self.streams[e]:
                if o.need_inc:
                    counters[o.sem_key] = counters.get(o.sem_key, 0) + o.inc
                    self.val[o] = counters[o.sem_key]
        self.sem_keys = sorted(counters)

    def check(self):
        ptr = {e: 0 for e in self.ENGINES}
        sem = {k: 0 for k in self.sem_keys}
        done, total, progress = 0, len(self.all), True
        while progress:
            progress = False
            for e in self.ENGINES:
                while ptr[e] < len(self.streams[e]):
                    o = self.streams[e][ptr[e]]
                    if all(sem[d.sem_key] >= self.val[d] for d in o.deps):
                        if o.need_inc:
                            sem[o.sem_key] += o.inc
                        ptr[e] += 1
                        done += 1
                        progress = True
                    else:
                        break
        if done != total:
            stuck = []
            for e in self.ENGINES:
                if ptr[e] < len(self.streams[e]):
                    o = self.streams[e][ptr[e]]
                    why = [(d.tag, d.eng, self.val[d], sem[d.sem_key])
                           for d in o.deps if sem[d.sem_key] < self.val[d]]
                    stuck.append((e, o.tag, why[:2]))
            raise RuntimeError(f"deadlock {done}/{total}: {stuck}")

    def emit(self, nc, stack):
        self.finalize()
        self.check()
        sems = {k: stack.enter_context(nc.semaphore("sm_" + k))
                for k in self.sem_keys}
        block = stack.enter_context(nc.Block())
        prog = self

        def section(ekey):
            def run(E):
                waited = {}
                for o in prog.streams[ekey]:
                    for d in o.deps:
                        v = prog.val[d]
                        if waited.get(d.sem_key, -1) < v:
                            E.wait_ge(sems[d.sem_key], v)
                            waited[d.sem_key] = v
                    inst = o.fn(E)
                    if o.need_inc:
                        inst.then_inc(sems[o.sem_key], o.inc)
            return run

        block.vector(section("v"))
        block.scalar(section("s"))
        block.tensor(section("t"))
        block.gpsimd(section("g"))
        block.sync(section("y"))


class Carver:
    def __init__(self, tensor, nbytes, start=0):
        self.t, self.nbytes, self.cur = tensor, nbytes, start

    def take(self, rows, cols, dt):
        esz = 4 if dt == F32 else 2
        start = (self.cur + 3) // 4 * 4
        self.cur = start + cols * esz
        assert self.cur <= self.nbytes, f"arena overflow {self.cur}>{self.nbytes}"
        if dt == F32:
            return self.t.ap()[0:rows, start // 4:start // 4 + cols]
        return self.t.ap().bitcast(BF16)[0:rows, start // 2:start // 2 + cols]


def build_program():
    nc = bass.Bass("TRN2", target_bir_lowering=False, debug=False)
    P = Prog()

    x_in = nc.dram_tensor("x", [TOKC, D], F32, kind="ExternalInput").ap()
    w1s_in = nc.dram_tensor("w1s", [HID // NCORES, D], F32, kind="ExternalInput").ap()
    w2s_in = nc.dram_tensor("w2s", [D // NCORES, HID], F32, kind="ExternalInput").ap()
    wbct_in = nc.dram_tensor("wbct", [D, 96], F32, kind="ExternalInput").ap()
    wdtoutT_in = nc.dram_tensor("wdtoutT", [RNK, HSL], F32, kind="ExternalInput").ap()
    bdts_in = nc.dram_tensor("bdts", [128, 2], F32, kind="ExternalInput").ap()
    dsl_in = nc.dram_tensor("dsl", [128, 2], F32, kind="ExternalInput").ap()
    alogs_in = nc.dram_tensor("alogs", [128, 32], F32, kind="ExternalInput").ap()
    ident_in = nc.dram_tensor("ident", [128, 128], F32, kind="ExternalInput").ap()
    identb_in = nc.dram_tensor("identb", [128, 128], BF16, kind="ExternalInput").ap()
    seln_in = nc.dram_tensor("seln", [NST, NST * 128], BF16, kind="ExternalInput").ap()
    onesb_in = nc.dram_tensor("onesb", [128, 1], BF16, kind="ExternalInput").ap()
    ones1_in = nc.dram_tensor("ones1", [1, 128], F32, kind="ExternalInput").ap()
    fconst_in = nc.dram_tensor("fconst", [128, 4], F32, kind="ExternalInput").ap()
    out_ext = nc.dram_tensor("out", [TOKC, D], F32, kind="ExternalOutput").ap()

    ar_in = nc.dram_tensor("ar_in", [1, 2], F32)
    ar_out = nc.dram_tensor("ar_out", [1, 2], F32)
    w1ag_in = nc.dram_tensor("w1ag_in", [D, HID // NCORES], BF16)
    w1ag_out = nc.dram_tensor("w1ag_out", [NCORES * D, HID // NCORES], BF16)
    w2ag_in = nc.dram_tensor("w2ag_in", [HID, D // NCORES], BF16)
    w2ag_out = nc.dram_tensor("w2ag_out", [NCORES * HID, D // NCORES], BF16)
    dbag_in = nc.dram_tensor("dbag_in", [96, TOKC], F32)
    dbag_out = nc.dram_tensor("dbag_out", [NCORES * 96, TOKC], F32)
    ua_in = nc.dram_tensor("ua_in", [D, TOKC], BF16)
    ua_out = nc.dram_tensor("ua_out", [D, TOKC], BF16)
    sa_in = nc.dram_tensor("sa_in", [D, TOKC], BF16)
    sa_out = nc.dram_tensor("sa_out", [D, TOKC], BF16)
    xh1sp = nc.dram_tensor("xh1sp", [128, 16 * TOKC], BF16)
    gsp = nc.dram_tensor("gsp", [128, 16 * TOKC], BF16)

    stack = ExitStack()

    def sbuf(name, shape, dt=F32):
        return stack.enter_context(nc.sbuf_tensor(name, shape, dt))

    wbct = sbuf("s_wbct", [128, 16 * 96])
    wdtoutT = sbuf("s_wdtoutT", [RNK, HSL])
    bdts = sbuf("s_bdts", [128, 2])
    dsl = sbuf("s_dsl", [128, 2])
    alogs = sbuf("s_alogs", [128, 32])
    aneg = sbuf("aneg", [128, 32])
    ident = sbuf("s_ident", [128, 128])
    identb = sbuf("s_identb", [128, 128], BF16)
    seln = sbuf("s_seln", [NST, NST * 128], BF16)
    onesb = sbuf("s_onesb", [128, 1], BF16)
    ones1 = sbuf("s_ones1", [1, 128])
    fconst = sbuf("s_fconst", [128, 4])
    gam = sbuf("gam", [128, 6])
    onescol = sbuf("s_onescol", [128, 1])
    grow = sbuf("grow", [1, 16])
    tpart = sbuf("tpart", [128, 16])
    spack = sbuf("spack", [128, 8])
    srow = sbuf("srow", [8, 128])
    sdl = sbuf("sdl", [RNK, SC])
    sbc = sbuf("sbc", [NST, 2 * SC])
    sbcb = sbuf("sbcb", [NST, 2 * SC], BF16)
    sdt = sbuf("sdt", [128, 2 * SC])
    sdta = sbuf("sdta", [128, 2 * SC])
    sdtb2 = sbuf("sdtb2", [128, 2 * SC])
    sdtu = sbuf("sdtu", [128, 2 * SC], BF16)
    suc = sbuf("suc", [128, 2 * SC], BF16)
    sab = [sbuf(f"sab{i}", [128, SC]) for i in range(2)]
    sbb = sbuf("sbb", [128, SC], BF16)
    sbrep = [sbuf(f"sbrep{i}", [128, SC], BF16) for i in range(2)]
    sxb = [sbuf(f"sxb{i}", [128, SC]) for i in range(2)]
    cx = [sbuf(f"cx{i}", [128, SC * NST], BF16) for i in range(2)]
    yred = sbuf("yred", [128, 2 * SC])
    sout = sbuf("sout", [128, 2 * SC], BF16)
    scarry = sbuf("scarry", [128, 32])
    zcol = sbuf("zcol", [128, 1])

    ARENA_BYTES = 122368
    arena_t = sbuf("arena", [128, ARENA_BYTES // 4])
    At = Carver(arena_t, ARENA_BYTES)
    twa = At.take(128, 4096, F32)
    twb = At.take(128, 4096, F32)
    twm1 = At.take(128, 2048, F32)
    twm2 = At.take(128, 2048, F32)
    twh = At.take(128, 4096, BF16)
    tstage = At.take(128, 32 * 128, BF16)
    Ap = Carver(arena_t, ARENA_BYTES, start=At.cur)
    xtm = Ap.take(128, 2 * D, F32)
    xfm = Ap.take(128, 16 * PT, F32)
    ubuf = Ap.take(128, 16 * PT, BF16)
    xh1b = Ap.take(128, 16 * PT, BF16)
    mrep = Ap.take(128, PT, F32)
    r1rep = Ap.take(128, PT, F32)
    rCrep = Ap.take(128, PT, F32)
    srow2 = Ap.take(1, 3 * PT, F32)
    stat6 = Ap.take(128, 24, F32)
    statv = Ap.take(128, 2 * 6, F32)
    sdtbc = Ap.take(96, PT, F32)
    Af = Carver(arena_t, ARENA_BYTES)
    g1 = Af.take(128, 64 * FT, BF16)
    xh1t = Af.take(128, 16 * FT, BF16)
    o2n = xh1t
    w1buf = [Af.take(128, 16 * 128, BF16) for _ in range(2)]
    w2buf = [Af.take(128, 16 * 128, BF16) for _ in range(2)]
    sqc = [Af.take(128, FT, BF16) for _ in range(2)]
    strow = Af.take(1, 6 * FT, F32)
    m3rep = Af.take(128, FT, BF16)
    r3rep = Af.take(128, FT, BF16)
    Az = Carver(arena_t, ARENA_BYTES)
    fs = Az.take(128, 16 * FT, BF16)
    fg = Az.take(128, 16 * FT, BF16)
    fsg = Az.take(128, 16 * FT, BF16)
    fsgtm = Az.take(128, 4 * D, BF16)
    fxt = Az.take(128, 4 * D, F32)

    pb = [stack.enter_context(nc.psum_tensor(f"pb{i}", [128, 512], F32))
          for i in range(8)]
    pbb = [p.ap().bitcast(BF16) for p in pb]
    bank_free = {i: [] for i in range(8)}

    def bank_use(i, deps):
        return list(deps) + bank_free[i]

    def bank_release(i, consumers):
        bank_free[i] = list(consumers)

    c_wbct = P.dma("y", "cst", wbct.ap().rearrange("p (c r) -> p c r", r=96),
                   wbct_in.rearrange("(c p) r -> p c r", p=128), tag="c_wbct")
    c_wdt = P.dma("y", "cst", wdtoutT[:, :], wdtoutT_in, deps=[c_wbct],
                  tag="c_wdt")
    c_bdt = P.dma("y", "cst", bdts[:, :], bdts_in, deps=[c_wdt], tag="c_bdt")
    c_dsl = P.dma("y", "cst", dsl[:, :], dsl_in, deps=[c_bdt], tag="c_dsl")
    c_alog = P.dma("y", "cst", alogs[:, :], alogs_in, deps=[c_dsl], tag="c_alog")
    c_id = P.dma("y", "cst", ident[:, :], ident_in, deps=[c_alog], tag="c_id")
    c_idb = P.dma("y", "cst", identb[:, :], identb_in, deps=[c_id], tag="c_idb")
    c_sel = P.dma("y", "cst", seln[:, :], seln_in, deps=[c_idb], tag="c_sel")
    c_onb = P.dma("y", "cst", onesb[:, :], onesb_in, deps=[c_sel], tag="c_onb")
    c_on1 = P.dma("y", "cst", ones1[:, :], ones1_in, deps=[c_onb], tag="c_on1")
    c_fc = P.dma("y", "cst", fconst[:, :], fconst_in, deps=[c_on1], tag="c_fc")

    o_zc = P.op("v", lambda v: v.memset(zcol[:, :], 0.0), tag="zcol")
    o_onec = P.op("v", lambda v: v.memset(onescol[:, :], 1.0), tag="onec")
    o_aexp = P.op("s", lambda s: s.activation(aneg[:, :], alogs[:, :], AF.Exp),
                  deps=[c_alog], tag="aexp")
    o_aneg = P.op("v", lambda v: v.tensor_scalar_mul(aneg[:, :], aneg[:, :], -1.0),
                  deps=[o_aexp], tag="aneg")

    tern_units = [(1, i, 0, D, w1s_in) for i in range(8)] + \
                 [(2, i, hh, 4096, w2s_in) for i in range(2) for hh in range(2)]

    red_ops = []
    for ui, (mat, i, hh, wid, wsrc) in enumerate(tern_units):
        dst = twa if ui % 2 == 0 else twb
        prev = red_ops[ui - 2] if ui >= 2 else None
        d_ = P.dma("y", "wld", dst[:, 0:wid],
                   wsrc[i * 128:(i + 1) * 128, hh * 4096:hh * 4096 + wid],
                   deps=[prev], tag=f"tld{ui}", slot=ui % 2)
        r = P.op("v", lambda v, dst=dst, wid=wid, ui=ui: v.reduce_sum(
            tpart[:, ui:ui + 1], dst[:, 0:wid], axis=AX.X,
            apply_absolute_value=True), deps=[d_], tag=f"tred{ui}")
        red_ops.append(r)

    o_gmm = P.op("t", lambda t: t.matmul(pb[7][0:1, 0:12], onescol[:, :],
                                         tpart[:, 0:12], start=True, stop=True),
                 deps=bank_use(7, [red_ops[-1], o_onec]), tag="gmm")
    o_gr1 = P.op("v", lambda v: v.reduce_sum(grow[0:1, 0:1], pb[7][0:1, 0:8],
                                             axis=AX.X), deps=[o_gmm], tag="gr1")
    o_gr2 = P.op("v", lambda v: v.reduce_sum(grow[0:1, 1:2], pb[7][0:1, 8:12],
                                             axis=AX.X), deps=[o_gmm], tag="gr2")
    bank_release(7, [o_gr1, o_gr2])
    d_gar = P.dma("y", "ccs", ar_in[0:1, :], grow[0:1, 0:2],
                  deps=[o_gr1, o_gr2], tag="gar_st")
    cc_ar = P.collective(lambda g: g.collective_compute(
        "AllReduce", ALU.add, replica_groups=[list(range(NCORES))],
        ins=[ar_in.ap().opt()], outs=[ar_out.ap().opt()]),
        deps=[d_gar], tag="cc_ar")
    d_gld = P.dma("y", "ccs", grow[0:1, 2:4], ar_out[0:1, :], deps=[cc_ar],
                  tag="gar_ld")
    o_gth = P.op("v", lambda v: v.tensor_scalar(
        grow[0:1, 4:6], grow[0:1, 2:4], 1.0 / (HID * D), QEPS, ALU.mult, ALU.add),
        deps=[d_gld], tag="gth")
    o_gth2 = P.op("v", lambda v: v.tensor_scalar_mul(
        grow[0:1, 4:6], grow[0:1, 4:6], 0.5), deps=[o_gth], tag="gth2")
    o_gthn = P.op("v", lambda v: v.tensor_scalar_mul(
        grow[0:1, 6:8], grow[0:1, 4:6], -1.0), deps=[o_gth2], tag="gthn")
    o_gm1 = P.op("v", lambda v: v.tensor_scalar_mul(
        grow[0:1, 8:10], grow[0:1, 2:4], 1.0 / (HID * D)), deps=[d_gld],
        tag="gm1")
    o_gbc = P.op("t", lambda t: t.matmul(pb[7][:, 0:6], ones1[0:1, :],
                                         grow[0:1, 4:10], start=True, stop=True),
                 deps=bank_use(7, [o_gthn, o_gm1]), tag="gbc")
    o_gbe = P.op("s", lambda s: s.activation(gam[:, 0:6], pb[7][:, 0:6], AF.Copy),
                 deps=[o_gbc], tag="gbe")
    bank_release(7, [o_gbe])

    tern_done = []
    w1_last = w2_last = None
    for ui, (mat, i, hh, wid, wsrc) in enumerate(tern_units):
        thp = gam[:, mat - 1:mat]
        thn = gam[:, mat + 1:mat + 2]
        dst = twa if ui % 2 == 0 else twb
        prev = tern_done[ui - 2] if ui >= 2 else red_ops[-1]
        d_ = P.dma("y", "wld", dst[:, 0:wid],
                   wsrc[i * 128:(i + 1) * 128, hh * 4096:hh * 4096 + wid],
                   deps=[prev, red_ops[-1]], tag=f"t2ld{ui}", slot=ui % 2)
        subs = wid // 2048
        mops = []
        for ss in range(subs):
            sl = slice(ss * 2048, (ss + 1) * 2048)
            o1 = P.op("v", lambda v, dst=dst, sl=sl, thp=thp: v.tensor_single_scalar(
                twm1[:, 0:2048], dst[:, sl], thp, ALU.is_gt),
                deps=[d_, o_gbe], tag=f"tm1_{ui}_{ss}")
            o2 = P.op("v", lambda v, dst=dst, sl=sl, thn=thn: v.tensor_single_scalar(
                twm2[:, 0:2048], dst[:, sl], thn, ALU.is_lt),
                deps=[d_, o_gbe], tag=f"tm2_{ui}_{ss}")
            o3 = P.op("v", lambda v, sl=sl: v.tensor_sub(
                twh[:, sl], twm1[:, 0:2048], twm2[:, 0:2048]),
                deps=[o1, o2], tag=f"tmh_{ui}_{ss}")
            mops.append(o3)
        nblk = wid // 128
        evs = []
        prev_dst = tern_done[-1] if tern_done else None
        for j in range(nblk):
            bk = j % 2
            tr = P.op("t", lambda t, j=j, bk=bk: t.transpose(
                pbb[bk][:, 0:128], twh[:, j * 128:(j + 1) * 128], identb[:, :]),
                deps=bank_use(bk, [mops[j // 16], c_idb]), tag=f"ttr{ui}_{j}")
            ev = P.op("s", lambda s, j=j, bk=bk: s.activation(
                tstage[:, j * 128:(j + 1) * 128], pbb[bk][:, 0:128], AF.Copy),
                deps=[tr, prev_dst], tag=f"tte{ui}_{j}")
            bank_release(bk, [ev])
            evs.append(ev)
        agdst = w1ag_in if mat == 1 else w2ag_in
        dd = agdst.rearrange("(j p) o -> p j o", p=128)
        d_st = P.dma("y", "wst",
                     dd[:, hh * 32:hh * 32 + nblk, i * 128:(i + 1) * 128],
                     tstage.rearrange("p (j o) -> p j o", o=128)[:, 0:nblk, :],
                     deps=evs, tag=f"tst{ui}")
        if mat == 1:
            w1_last = d_st
        else:
            w2_last = d_st
        tern_done.append(d_st)

    cc_w1 = P.collective(lambda g: g.collective_compute(
        "AllGather", ALU.bypass, replica_groups=[list(range(NCORES))],
        ins=[w1ag_in.ap().opt()], outs=[w1ag_out.ap().opt()]),
        deps=[w1_last], tag="cc_w1")
    cc_w2 = P.collective(lambda g: g.collective_compute(
        "AllGather", ALU.bypass, replica_groups=[list(range(NCORES))],
        ins=[w2ag_in.ap().opt()], outs=[w2ag_out.ap().opt()]),
        deps=[w2_last], tag="cc_w2")

    dbc_dmas, ua_dmas, xh_dmas = [], [], []
    prev_tile = []
    for pt in range(NPT):
        t0 = pt * PT
        d_x = P.dma("y", "xld", xtm.rearrange("p (tt d) -> p tt d", d=D),
                    x_in[t0:t0 + PT, :].rearrange("(tt p) d -> p tt d", p=128),
                    deps=prev_tile, tag=f"pxl{pt}")
        pk_all = []
        for tt in range(2):
            bns = []
            for cch in range(4):
                o = P.op("v", lambda v, tt=tt, cch=cch: v.bn_stats(
                    stat6[:, cch * 6:(cch + 1) * 6],
                    xtm[:, tt * D + cch * 512:tt * D + (cch + 1) * 512]),
                    deps=[d_x], tag=f"pbn{pt}_{tt}_{cch}")
                bns.append(o)
            o_ag = P.op("v", lambda v, tt=tt: v.bn_aggr(
                statv[:, tt * 6:tt * 6 + 2], stat6[:, 0:24]),
                deps=[bns[-1]], tag=f"pag{pt}_{tt}")
            l1 = P.op("s", lambda s, tt=tt: s.activation(
                statv[:, tt * 6 + 4:tt * 6 + 5], statv[:, tt * 6 + 1:tt * 6 + 2],
                AF.Ln, bias=fconst[:, 0:1]), deps=[o_ag, c_fc], tag=f"pl1{pt}_{tt}")
            e1 = P.op("s", lambda s, tt=tt: s.activation(
                statv[:, tt * 6 + 2:tt * 6 + 3], statv[:, tt * 6 + 4:tt * 6 + 5],
                AF.Exp, scale=fconst[:, 2:3]), deps=[l1], tag=f"pe1{pt}_{tt}")
            vv = P.op("v", lambda v, tt=tt: v.tensor_scalar(
                statv[:, tt * 6 + 5:tt * 6 + 6], statv[:, tt * 6 + 1:tt * 6 + 2],
                1.0 + EPS2, EPS1 * EPS2, ALU.mult, ALU.add),
                deps=[o_ag], tag=f"pvv{pt}_{tt}")
            lC = P.op("s", lambda s, tt=tt: s.activation(
                statv[:, tt * 6 + 5:tt * 6 + 6], statv[:, tt * 6 + 5:tt * 6 + 6],
                AF.Ln), deps=[vv], tag=f"plC{pt}_{tt}")
            eC = P.op("s", lambda s, tt=tt: s.activation(
                statv[:, tt * 6 + 3:tt * 6 + 4], statv[:, tt * 6 + 5:tt * 6 + 6],
                AF.Exp, scale=fconst[:, 2:3]), deps=[lC], tag=f"peC{pt}_{tt}")
            for j, col in ((0, 0), (1, 2), (2, 3)):
                o = P.op("v", lambda v, tt=tt, j=j, col=col: v.tensor_copy(
                    spack[:, tt * 3 + j:tt * 3 + j + 1],
                    statv[:, tt * 6 + col:tt * 6 + col + 1]),
                    deps=[e1, eC], tag=f"ppk{pt}_{tt}_{j}")
                pk_all.append(o)
        tr_s = P.op("t", lambda t: t.transpose(pb[7][0:6, 0:128], spack[:, 0:6],
                                               ident[:, :]),
                    deps=bank_use(7, pk_all + [c_id]), tag=f"ptr{pt}")
        ev_s = P.op("s", lambda s: s.activation(srow[0:6, :], pb[7][0:6, 0:128],
                                                AF.Copy), deps=[tr_s],
                    tag=f"pev{pt}")
        bank_release(7, [ev_s])
        d_rm = None
        for jj in range(3):
            for tt in range(2):
                d_rm = P.dma("y", "srw",
                             srow2[0:1, jj * PT + tt * 128:jj * PT + tt * 128 + 128],
                             srow[tt * 3 + jj:tt * 3 + jj + 1, :],
                             deps=[ev_s, d_rm], tag=f"prm{pt}_{jj}_{tt}")
        reps = []
        for j, dstrep in ((0, mrep), (1, r1rep), (2, rCrep)):
            bc = P.op("t", lambda t, j=j: t.matmul(
                pb[7][:, 0:PT], ones1[0:1, :], srow2[0:1, j * PT:(j + 1) * PT],
                start=True, stop=True), deps=bank_use(7, [d_rm]),
                tag=f"pbc{pt}_{j}")
            evr = P.op("s", lambda s, dstrep=dstrep: s.activation(
                dstrep[:, :], pb[7][:, 0:PT], AF.Copy), deps=[bc],
                tag=f"pbe{pt}_{j}")
            bank_release(7, [evr])
            reps.append(evr)
        trev = []
        for tt in range(2):
            for dc in range(16):
                j = tt * 16 + dc
                bk = 3 + j % 2
                tr = P.op("t", lambda t, tt=tt, dc=dc, bk=bk: t.transpose(
                    pb[bk][:, 0:128],
                    xtm[:, tt * D + dc * 128:tt * D + (dc + 1) * 128],
                    ident[:, :]), deps=bank_use(bk, [d_x, c_id]),
                    tag=f"pxt{pt}_{j}")
                ev = P.op("s", lambda s, tt=tt, dc=dc, bk=bk: s.activation(
                    xfm[:, dc * PT + tt * 128:dc * PT + tt * 128 + 128],
                    pb[bk][:, 0:128], AF.Copy), deps=[tr], tag=f"pxe{pt}_{j}")
                bank_release(bk, [ev])
                trev.append(ev)
        xv = xfm.rearrange("p (dc t) -> p dc t", t=PT)
        mv = mrep.rearrange("p (o t) -> p o t", o=1)
        xvb, mvb = bass.broadcast_tensor_aps(xv, mv)
        o_sub = P.op("v", lambda v: v.tensor_sub(xv, xvb, mvb),
                     deps=trev + [reps[0]], tag=f"psub{pt}")
        uv = ubuf.rearrange("p (dc t) -> p dc t", t=PT)
        r1v = r1rep.rearrange("p (o t) -> p o t", o=1)
        _, r1vb = bass.broadcast_tensor_aps(xv, r1v)
        o_u = P.op("v", lambda v: v.tensor_mul(uv, xvb, r1vb),
                   deps=[o_sub, reps[1]], tag=f"pu{pt}")
        xh1v = xh1b.rearrange("p (dc t) -> p dc t", t=PT)
        rCv = rCrep.rearrange("p (o t) -> p o t", o=1)
        _, rCvb = bass.broadcast_tensor_aps(xv, rCv)
        o_xh = P.op("v", lambda v: v.tensor_mul(xh1v, xvb, rCvb),
                    deps=[o_sub, reps[2]], tag=f"pxh{pt}")
        mms = []
        for dc in range(16):
            mm = P.op("t", lambda t, dc=dc: t.matmul(
                pb[5][0:96, 0:PT], wbct[:, dc * 96:(dc + 1) * 96],
                xfm[:, dc * PT:(dc + 1) * PT], start=(dc == 0), stop=(dc == 15)),
                deps=(bank_use(5, [o_sub, c_wbct]) if dc == 0 else [mms[-1]]),
                tag=f"pdb{pt}_{dc}")
            mms.append(mm)
        o_dbe = P.op("v", lambda v: v.tensor_mul(
            sdtbc[0:96, :], pb[5][0:96, 0:PT], r1rep[0:96, :]),
            deps=[mms[-1], reps[1]], tag=f"pdbe{pt}")
        bank_release(5, [o_dbe])
        d_db = P.dma("y", "dbc", dbag_in[0:96, t0:t0 + PT], sdtbc[0:96, :],
                     deps=[o_dbe], tag=f"pdbs{pt}")
        d_ua = P.dma("y", "uas",
                     ua_in.rearrange("(dc p) t -> p dc t", p=128)[:, :, t0:t0 + PT],
                     uv, deps=[o_u], tag=f"pua{pt}")
        d_xh = P.dma("y", "xsp",
                     xh1sp.rearrange("p (dc t) -> p dc t", t=TOKC)[:, :, t0:t0 + PT],
                     xh1v, deps=[o_xh], tag=f"pxs{pt}")
        dbc_dmas.append(d_db)
        ua_dmas.append(d_ua)
        xh_dmas.append(d_xh)
        prev_tile = [o_u, o_xh, o_dbe, d_ua, d_xh, d_db, mms[-1]]

    cc_db = P.collective(lambda g: g.collective_compute(
        "AllGather", ALU.bypass, replica_groups=[list(range(NCORES))],
        ins=[dbag_in.ap().opt()], outs=[dbag_out.ap().opt()]),
        deps=dbc_dmas, tag="cc_db")
    cc_ua = P.collective(lambda g: g.collective_compute(
        "AllToAll", ALU.bypass, replica_groups=[list(range(NCORES))],
        ins=[ua_in.ap().opt()], outs=[ua_out.ap().opt()]),
        deps=ua_dmas, tag="cc_ua")

    war = {
        "sdl": [cc_db], "sbc": [cc_db], "sbcb": [], "suc": [cc_ua],
        "sout": {0: None, 1: None},
        "scan_pp": {0: None, 1: None},
        "carry_pp": {0: None, 1: None},
        "b_pp": {0: None, 1: None},
        "pool": {0: None, 1: None},
        "aexp_last": None,
        "b_last": None,
    }
    prev_carry = {}
    sdma_s = []
    ffn_prev = []

    scan_tail = []

    def emit_scan_chunk(c):
        import os as _os
        SPART = int(_os.environ.get("KSCANPART", "5"))
        r, off = c // 4, (c % 4) * SC
        d_dl = P.dma("y", "sdl", sdl[:, :],
                     dbag_out[r * 96:r * 96 + RNK, off:off + SC],
                     deps=war["sdl"], tag=f"sdl{c}")
        d_b = P.dma("y", "sbc", sbc[:, 0:SC],
                    dbag_out[r * 96 + RNK:r * 96 + 80, off:off + SC],
                    deps=war["sbc"], tag=f"sBl{c}", slot=0)
        d_c = P.dma("y", "sbc", sbc[:, SC:2 * SC],
                    dbag_out[r * 96 + 80:r * 96 + 96, off:off + SC],
                    deps=war["sbc"], tag=f"sCl{c}", slot=1)
        d_u0 = P.dma("y", "sc2", suc[:, 0:SC],
                     ua_out[r * 256:r * 256 + 128, off:off + SC],
                     deps=war["suc"], tag=f"su0{c}", slot=0)
        d_u1 = P.dma("y", "sc2", suc[:, SC:2 * SC],
                     ua_out[r * 256 + 128:r * 256 + 256, off:off + SC],
                     deps=war["suc"], tag=f"su1{c}", slot=1)
        o_bc16 = P.op("v", lambda v: v.tensor_copy(sbcb[:, :], sbc[:, :]),
                      deps=[d_b, d_c] + war["sbcb"], tag=f"sbc16_{c}")
        dt_ops = []
        for m in range(2):
            sl = slice(m * SC, (m + 1) * SC)
            mm = P.op("t", lambda t, m=m: t.matmul(
                pb[7][:, 0:SC], wdtoutT[:, m * 128:(m + 1) * 128], sdl[:, :],
                start=True, stop=True),
                deps=bank_use(7, [d_dl, c_wdt]), tag=f"sdtm{c}_{m}")
            a1 = P.op("s", lambda s, sl=sl, m=m: s.activation(
                sdta[:, sl], pb[7][:, 0:SC], AF.Abs, bias=bdts[:, m:m + 1]),
                deps=[mm, c_bdt], tag=f"sab{c}_{m}")
            r1 = P.op("v", lambda v, sl=sl, m=m: v.tensor_scalar(
                sdtb2[:, sl], pb[7][:, 0:SC], bdts[:, m:m + 1], 0.0,
                ALU.add, ALU.max), deps=[mm, c_bdt, a1], tag=f"srl{c}_{m}")
            bank_release(7, [a1, r1])
            e1 = P.op("s", lambda s, sl=sl: s.activation(
                sdta[:, sl], sdta[:, sl], AF.Exp, scale=-1.0),
                deps=[a1], tag=f"sex{c}_{m}")
            l1 = P.op("s", lambda s, sl=sl: s.activation(
                sdta[:, sl], sdta[:, sl], AF.Ln, bias=1.0),
                deps=[e1], tag=f"sln{c}_{m}")
            o_dt = P.op("v", lambda v, sl=sl: v.tensor_add(
                sdt[:, sl], sdta[:, sl], sdtb2[:, sl]),
                deps=[l1, r1, war["aexp_last"]], tag=f"sdt{c}_{m}")
            o_du = P.op("v", lambda v, sl=sl: v.tensor_mul(
                sdtu[:, sl], sdt[:, sl], suc[:, sl]),
                deps=[o_dt, d_u0, d_u1], tag=f"sdu{c}_{m}")
            dt_ops.append((o_dt, o_du))
        yield
        if SPART <= 1:
            scan_tail.append(dt_ops[1][1])
            war["sdl"] = [dt_ops[0][0], dt_ops[1][0]]
            war["sbc"] = [o_bc16]
            war["suc"] = [dt_ops[0][1], dt_ops[1][1]]
            war["sbcb"] = [o_bc16]
            return
        war["sdl"] = [dt_ops[0][0], dt_ops[1][0]]
        war["sbc"] = [o_bc16]
        last_a = None
        last_b = None
        last_bc = None
        for m in range(2):
            sl = slice(m * SC, (m + 1) * SC)
            cmuls = []
            for n in range(NST):
                g = m * NST + n
                pp = g % 2
                bcB = P.op("t", lambda t, n=n: t.matmul(
                    pb[5][:, 0:SC], seln[:, n * 128:(n + 1) * 128],
                    sbcb[:, 0:SC], start=True, stop=True),
                    deps=bank_use(5, [o_bc16, c_sel]), tag=f"bB{c}_{g}")
                evB = P.op("s", lambda s, pp=pp: s.activation(
                    sbrep[pp][:, :], pb[5][:, 0:SC], AF.Copy),
                    deps=[bcB, war["b_pp"][pp]], tag=f"eB{c}_{g}")
                bank_release(5, [evB])
                bcC = P.op("t", lambda t, n=n: t.matmul(
                    pb[6][:, 0:SC], seln[:, n * 128:(n + 1) * 128],
                    sbcb[:, SC:2 * SC], start=True, stop=True),
                    deps=bank_use(6, [o_bc16, c_sel]), tag=f"bC{c}_{g}")
                last_bc = bcC
                if SPART <= 2:
                    evC = P.op("s", lambda s, pp=pp: s.activation(
                        sbrep[pp][:, :], pb[6][:, 0:SC], AF.Copy),
                        deps=[bcC, war["b_pp"][pp]], tag=f"eC{c}_{g}")
                    bank_release(6, [evC])
                    war["b_pp"][pp] = evC
                    scan_tail.append(evC)
                    continue
                o_a = P.op("s", lambda s, sl=sl, m=m, n=n, pp=pp: s.activation(
                    sab[pp][:, :], sdt[:, sl], AF.Exp,
                    scale=aneg[:, m * 16 + n:m * 16 + n + 1]),
                    deps=[dt_ops[m][0], o_aneg, war["scan_pp"][pp]],
                    tag=f"a{c}_{g}")
                last_a = o_a
                o_b = P.op("v", lambda v, sl=sl, pp=pp: v.tensor_mul(
                    sbb[:, :], sdtu[:, sl], sbrep[pp][:, :]),
                    deps=[dt_ops[m][1], evB], tag=f"b{c}_{g}")
                last_b = o_b
                war["b_pp"][pp] = o_b
                if c == 0 or c == 16:
                    init = zcol[:, 0:1]
                    ideps = [o_a, o_b, o_zc, war["carry_pp"][pp]]
                else:
                    init = scarry[:, g:g + 1]
                    ideps = [o_a, o_b, prev_carry.get((c - 1, g)),
                             war["carry_pp"][pp]]
                o_x = P.op("v", lambda v, pp=pp, init=init: v.tensor_tensor_scan(
                    sxb[pp][:, :], sab[pp][:, :], sbb[:, :], init,
                    ALU.mult, ALU.add), deps=ideps, tag=f"x{c}_{g}")
                war["scan_pp"][pp] = o_x
                o_cr = P.op("g", lambda gp, pp=pp, g=g: gp.tensor_copy(
                    scarry[:, g:g + 1], sxb[pp][:, SC - 1:SC]),
                    deps=[o_x], tag=f"cr{c}_{g}")
                prev_carry[(c, g)] = o_cr
                war["carry_pp"][pp] = o_cr
                if SPART <= 3:
                    war["scan_pp"][pp] = o_x
                    bank_release(6, [o_x])
                    scan_tail.append(o_cr)
                    continue
                o_cm = P.op("v", lambda v, pp=pp, n=n, m=m: v.tensor_mul(
                    cx[m][:, n * SC:(n + 1) * SC],
                    sxb[pp][:, :], pb[6][:, 0:SC]),
                    deps=[o_x, bcC, war["pool"][m]], tag=f"cm{c}_{g}")
                bank_release(6, [o_cm])
                cmuls.append(o_cm)
                if n % 2 == 1:
                    yield
            if SPART <= 4:
                war["pool"][m] = cmuls[-1]
                scan_tail.append(cmuls[-1])
                continue
            yv = yred[:, m * SC:(m + 1) * SC]
            a0 = P.op("g", lambda gp, yv=yv, m=m: gp.tensor_add(
                yv, cx[m][:, 0:SC], cx[m][:, SC:2 * SC]),
                deps=[cmuls[0], cmuls[1]], tag=f"yr{c}_{m}_0")
            last = a0
            for n in range(2, NST):
                last = P.op("g", lambda gp, yv=yv, m=m, n=n: gp.tensor_add(
                    yv, yv, cx[m][:, n * SC:(n + 1) * SC]),
                    deps=[last, cmuls[n]], tag=f"yr{c}_{m}_{n}")
            sl = slice(m * SC, (m + 1) * SC)
            o_st = P.op("v", lambda gp, sl=sl, m=m: gp.scalar_tensor_tensor(
                sout[:, sl], suc[:, sl], dsl[:, m:m + 1], yred[:, sl],
                ALU.mult, ALU.add),
                deps=[last, c_dsl, d_u0, d_u1, war["sout"][m]],
                tag=f"ys{c}_{m}")
            war["pool"][m] = o_st
            d_s = P.dma("y", "sst",
                        sa_in[r * 256 + m * 128:r * 256 + (m + 1) * 128,
                              off:off + SC],
                        sout[:, sl], deps=[o_st], tag=f"ss{c}_{m}", slot=m)
            war["sout"][m] = d_s
            sdma_s.append(d_s)
        war["aexp_last"] = last_a
        war["b_last"] = last_b
        war["suc"] = [dt_ops[0][1], dt_ops[1][1], war["pool"][0], war["pool"][1]]
        war["sbcb"] = [last_bc]

    def emit_ffn_tile(f):
        nonlocal ffn_prev, g_last
        t0 = f * FT
        d_xh = P.dma("s", "xht", xh1t.rearrange("p (dc t) -> p dc t", t=FT),
                     xh1sp.rearrange("p (dc t) -> p dc t", t=TOKC)[
                         :, :, t0:t0 + FT],
                     deps=xh_dmas + [cc_w1, cc_w2] + ffn_prev, tag=f"fxh{f}")
        w1v = w1ag_out.rearrange("(r dc p) o -> p r dc o", p=128, dc=16)
        evacs = []
        wprev = [None, None]
        for oc in range(64):
            r_, oc_ = oc // 8, oc % 8
            half = oc % 2
            d_w = P.dma("s", "w1s",
                        w1buf[half].rearrange("p (dc o) -> p dc o", o=128),
                        w1v[:, r_, :, oc_ * 128:(oc_ + 1) * 128],
                        deps=[wprev[half], cc_w1], tag=f"fw1{f}_{oc}", slot=half)
            mms = []
            for kb in range(16):
                dep = ([d_w, d_xh] + bank_use(half, [])) if kb == 0 else [mms[-1]]
                mm = P.op("t", lambda t, half=half, kb=kb: t.matmul(
                    pb[half][:, 0:FT], w1buf[half][:, kb * 128:(kb + 1) * 128],
                    xh1t[:, kb * FT:(kb + 1) * FT],
                    start=(kb == 0), stop=(kb == 15)), deps=dep,
                    tag=f"fm1{f}_{oc}_{kb}")
                mms.append(mm)
            wprev[half] = mms[-1]
            yield
            ev = P.op("v", lambda v, oc=oc, half=half: v.tensor_scalar_mul(
                g1[:, oc * FT:(oc + 1) * FT], pb[half][:, 0:FT], gam[:, 4:5]),
                deps=[mms[-1], o_gbe], tag=f"fe1{f}_{oc}")
            bank_release(half, [ev])
            evacs.append(ev)
        o_gel = P.op("s", lambda s: s.activation(g1[:, :], g1[:, :], AF.Gelu),
                     deps=[evacs[-1]], tag=f"fgel{f}")
        summs = []
        for oc in range(64):
            dep = (bank_use(2, [o_gel, c_onb])) if oc == 0 else [summs[-1]]
            mm = P.op("t", lambda t, oc=oc: t.matmul(
                pb[2][0:1, 0:FT], onesb[:, 0:1], g1[:, oc * FT:(oc + 1) * FT],
                start=(oc == 0), stop=(oc == 63)), deps=dep, tag=f"fsm{f}_{oc}")
            summs.append(mm)
        yield
        ev_sum = P.op("v", lambda v: v.tensor_copy(strow[0:1, 0:FT],
                                                   pb[2][0:1, 0:FT]),
                      deps=[summs[-1], evacs[-1]], tag=f"fsev{f}")
        sqmms, sq_ops = [], []
        for oc in range(64):
            half = oc % 2
            o_sq = P.op("v", lambda v, oc=oc, half=half: v.tensor_mul(
                sqc[half][:, :], g1[:, oc * FT:(oc + 1) * FT],
                g1[:, oc * FT:(oc + 1) * FT]),
                deps=[ev_sum] if oc == 0 else ([sqmms[-2]] if oc >= 2 else []),
                tag=f"fsq{f}_{oc}")
            dep = [o_sq] + ([ev_sum] if oc == 0 else [sqmms[-1]])
            mm = P.op("t", lambda t, half=half, oc=oc: t.matmul(
                pb[2][0:1, 0:FT], onesb[:, 0:1], sqc[half][:, :],
                start=(oc == 0), stop=(oc == 63)), deps=dep, tag=f"fqm{f}_{oc}")
            sqmms.append(mm)
            sq_ops.append(o_sq)
        ev_sq = P.op("v", lambda v: v.tensor_copy(strow[0:1, FT:2 * FT],
                                                  pb[2][0:1, 0:FT]),
                     deps=[sqmms[-1]], tag=f"fqev{f}")
        bank_release(2, [ev_sq])
        o_m3 = P.op("v", lambda v: v.tensor_scalar_mul(
            strow[0:1, 2 * FT:3 * FT], strow[0:1, 0:FT], 1.0 / HID),
            deps=[ev_sum], tag=f"fm3{f}")
        o_s2 = P.op("s", lambda s: s.activation(
            strow[0:1, 3 * FT:4 * FT], strow[0:1, 2 * FT:3 * FT], AF.Square),
            deps=[o_m3], tag=f"fs2{f}")
        o_e2 = P.op("v", lambda v: v.tensor_scalar_mul(
            strow[0:1, 4 * FT:5 * FT], strow[0:1, FT:2 * FT], 1.0 / HID),
            deps=[ev_sq], tag=f"fe2{f}")
        o_var = P.op("v", lambda v: v.tensor_sub(
            strow[0:1, 3 * FT:4 * FT], strow[0:1, 4 * FT:5 * FT],
            strow[0:1, 3 * FT:4 * FT]), deps=[o_s2, o_e2], tag=f"fvar{f}")
        o_ln3 = P.op("s", lambda s: s.activation(
            strow[0:1, 4 * FT:5 * FT], strow[0:1, 3 * FT:4 * FT], AF.Ln,
            bias=fconst[0:1, 1:2]), deps=[o_var, c_fc], tag=f"fln{f}")
        o_r3 = P.op("s", lambda s: s.activation(
            strow[0:1, 5 * FT:6 * FT], strow[0:1, 4 * FT:5 * FT], AF.Exp,
            scale=fconst[0:1, 2:3]), deps=[o_ln3], tag=f"fr3{f}")
        bc_m = P.op("t", lambda t: t.matmul(
            pb[7][:, 0:FT], ones1[0:1, :], strow[0:1, 2 * FT:3 * FT],
            start=True, stop=True), deps=bank_use(7, [o_m3]), tag=f"fbm{f}")
        ev_m = P.op("s", lambda s: s.activation(m3rep[:, :], pb[7][:, 0:FT],
                                                AF.Copy),
                    deps=[bc_m] + ffn_prev, tag=f"fbme{f}")
        bank_release(7, [ev_m])
        bc_r = P.op("t", lambda t: t.matmul(
            pb[7][:, 0:FT], ones1[0:1, :], strow[0:1, 5 * FT:6 * FT],
            start=True, stop=True), deps=bank_use(7, [o_r3]), tag=f"fbr{f}")
        ev_r = P.op("s", lambda s: s.activation(r3rep[:, :], pb[7][:, 0:FT],
                                                AF.Copy),
                    deps=[bc_r] + ffn_prev, tag=f"fbre{f}")
        bank_release(7, [ev_r])
        g1v = g1.rearrange("p (oc t) -> p oc t", t=FT)
        m3v = m3rep.rearrange("p (o t) -> p o t", o=1)
        g1vb, m3vb = bass.broadcast_tensor_aps(g1v, m3v)
        o_x2a = P.op("v", lambda v: v.tensor_sub(g1v, g1vb, m3vb),
                     deps=[ev_m, ev_sq], tag=f"fx2a{f}")
        r3v = r3rep.rearrange("p (o t) -> p o t", o=1)
        _, r3vb = bass.broadcast_tensor_aps(g1v, r3v)
        o_x2b = P.op("v", lambda v: v.tensor_mul(g1v, g1vb, r3vb),
                     deps=[o_x2a, ev_r], tag=f"fx2b{f}")
        yield
        w2v = w2ag_out.rearrange("(r kk p) o -> p r kk o", p=128, kk=64)
        ev2s = []
        w2prev = [None, None]
        for oc2 in range(16):
            r_, oc_ = oc2 // 2, oc2 % 2
            bk = 3 + oc2 % 2
            dws = []
            mms = []
            for q in range(4):
                d_w = P.dma("s", "w2s",
                            w2buf[q % 2].rearrange("p (kk o) -> p kk o", o=128),
                            w2v[:, r_, q * 16:(q + 1) * 16,
                                oc_ * 128:(oc_ + 1) * 128],
                            deps=[w2prev[q % 2]] +
                                 ([cc_w2] if oc2 == 0 else []) +
                                 ([mms[(q - 2) * 16 + 15]] if q >= 2 else []),
                            tag=f"fw2{f}_{oc2}_{q}", slot=q % 2)
                dws.append(d_w)
                for kk in range(16):
                    kb = q * 16 + kk
                    if kb == 0:
                        dep = [d_w, o_x2b] + bank_use(bk, [])
                    elif kk == 0:
                        dep = [d_w, mms[-1]]
                    else:
                        dep = [mms[-1]]
                    mm = P.op("t", lambda t, bk=bk, kb=kb, q=q, kk=kk: t.matmul(
                        pb[bk][:, 0:FT],
                        w2buf[q % 2][:, kk * 128:(kk + 1) * 128],
                        g1[:, kb * FT:(kb + 1) * FT],
                        start=(kb == 0), stop=(kb == 63)), deps=dep,
                        tag=f"fm2{f}_{oc2}_{kb}")
                    mms.append(mm)
            w2prev = [mms[47], mms[63]]
            ev = P.op("v", lambda v, oc2=oc2, bk=bk: v.tensor_scalar_mul(
                o2n[:, oc2 * FT:(oc2 + 1) * FT], pb[bk][:, 0:FT], gam[:, 5:6]),
                deps=[mms[-1], o_gbe], tag=f"fe2{f}_{oc2}")
            bank_release(bk, [ev])
            ev2s.append(ev)
            yield
        o_sil = P.op("s", lambda s: s.activation(o2n[:, :], o2n[:, :], AF.Silu),
                     deps=[ev2s[-1]], tag=f"fsil{f}")
        d_g = P.dma("y", "gsp",
                    gsp.rearrange("p (dc t) -> p dc t", t=TOKC)[:, :, t0:t0 + FT],
                    o2n.rearrange("p (dc t) -> p dc t", t=FT),
                    deps=[o_sil], tag=f"fgs{f}")
        ffn_prev = [o_sil, d_g, o_x2b, ev2s[-1]]
        g_last = d_g
        yield

    import os
    PH = os.environ.get("KPHASES", "all")
    if PH == "prep":
        P.op("y", lambda y: y.nop(), deps=[cc_ua, cc_db, cc_w1, cc_w2],
             tag="end_prep")
        with allow_ici_all_to_all():
            P.emit(nc, stack)
        return nc, stack, len(P.all)

    g_last = None
    NCH = int(os.environ.get("KNCHUNK", "32"))

    def ffn_source():
        for f in range(NFT):
            yield from emit_ffn_tile(f)

    gen = ffn_source() if PH != "scan" else iter(())
    n_ffn_yields = 4 * (64 + 3 + 16 + 1)
    n_scan_yields = NCH * 17
    ratio = float(os.environ.get("KILV", "1")) * n_ffn_yields / max(n_scan_yields, 1)
    acc = 0.0
    for c in range(NCH):
        for _ in emit_scan_chunk(c):
            acc += ratio
            while acc >= 1.0:
                next(gen, None)
                acc -= 1.0
    for _ in gen:
        pass

    cc_sa = None
    if sdma_s:
        cc_sa = P.collective(lambda g: g.collective_compute(
            "AllToAll", ALU.bypass, replica_groups=[list(range(NCORES))],
            ins=[sa_in.ap().opt()], outs=[sa_out.ap().opt()]),
            deps=sdma_s, tag="cc_sa")

    if PH in ("scan", "ffn"):
        enddeps = ([cc_sa] if sdma_s else scan_tail[-4:]) + \
                  ([g_last] if g_last else [])
        P.op("y", lambda y: y.nop(), deps=enddeps, tag="end_part")
        with allow_ici_all_to_all():
            P.emit(nc, stack)
        return nc, stack, len(P.all)

    out_dmas = []
    prev_fin = [cc_sa, g_last] + ffn_prev
    for f in range(NFT):
        t0 = f * FT
        d_s = P.dma("y", "fin", fs.rearrange("p (dc t) -> p dc t", t=FT),
                    sa_out.rearrange("(dc p) t -> p dc t", p=128)[
                        :, :, t0:t0 + FT], deps=prev_fin, tag=f"zs{f}")
        d_g2 = P.dma("y", "fin", fg.rearrange("p (dc t) -> p dc t", t=FT),
                    gsp.rearrange("p (dc t) -> p dc t", t=TOKC)[:, :, t0:t0 + FT],
                    deps=prev_fin, tag=f"zg{f}", slot=1)
        d_x = P.dma("y", "fin", fxt.rearrange("p (tt d) -> p tt d", d=D),
                    x_in[t0:t0 + FT, :].rearrange("(tt p) d -> p tt d", p=128),
                    deps=prev_fin, tag=f"zx{f}", slot=2)
        o_sg = P.op("v", lambda v: v.tensor_mul(fsg[:, :], fs[:, :], fg[:, :]),
                    deps=[d_s, d_g2], tag=f"zsg{f}")
        trevs = []
        for dc in range(16):
            for tt in range(4):
                j = dc * 4 + tt
                bk = j % 2
                tr = P.op("t", lambda t, dc=dc, tt=tt, bk=bk: t.transpose(
                    pbb[bk][:, 0:128],
                    fsg[:, dc * FT + tt * 128:dc * FT + tt * 128 + 128],
                    identb[:, :]), deps=bank_use(bk, [o_sg, c_idb]),
                    tag=f"ztr{f}_{j}")
                ev = P.op("s", lambda s, dc=dc, tt=tt, bk=bk: s.activation(
                    fsgtm[:, tt * D + dc * 128:tt * D + (dc + 1) * 128],
                    pbb[bk][:, 0:128], AF.Copy), deps=[tr], tag=f"zte{f}_{j}")
                bank_release(bk, [ev])
                trevs.append(ev)
        o_out = P.op("v", lambda v: v.tensor_add(fxt[:, :], fxt[:, :],
                                                 fsgtm[:, :]),
                     deps=[trevs[-1], d_x], tag=f"zout{f}")
        d_o = P.dma("y", "out",
                    out_ext[t0:t0 + FT, :].rearrange("(tt p) d -> p tt d", p=128),
                    fxt.rearrange("p (tt d) -> p tt d", d=D),
                    deps=[o_out], tag=f"zo{f}")
        out_dmas.append(d_o)
        prev_fin = [o_out, d_o, o_sg]

    P.op("y", lambda y: y.nop(), deps=out_dmas, tag="end")

    with allow_ici_all_to_all():
        P.emit(nc, stack)
    n_ops = len(P.all)
    return nc, stack, n_ops


_CACHE = {}


def kernel(L0_attn_input, L0_attn_W_dt_in, L0_attn_W_dt_out, L0_attn_b_dt_out,
           L0_attn_W_B, L0_attn_W_C, L0_attn_D, L0_attn_A_log,
           L0_ffn_w_fc, L0_ffn_w_proj):
    import ml_dtypes
    x = np.ascontiguousarray(np.asarray(L0_attn_input, np.float32)
                             .reshape(TTOT, D))
    w_fc = np.ascontiguousarray(np.asarray(L0_ffn_w_fc, np.float32))
    w_proj = np.ascontiguousarray(np.asarray(L0_ffn_w_proj, np.float32))
    wdtin = np.asarray(L0_attn_W_dt_in, np.float32)
    wdtout = np.asarray(L0_attn_W_dt_out, np.float32)
    bdt = np.asarray(L0_attn_b_dt_out, np.float32)
    wB = np.asarray(L0_attn_W_B, np.float32)
    wC = np.asarray(L0_attn_W_C, np.float32)
    Dv = np.asarray(L0_attn_D, np.float32)
    alog = np.asarray(L0_attn_A_log, np.float32)

    wbct = np.ascontiguousarray(np.concatenate([wdtin, wB, wC], axis=0).T)
    ident = np.eye(128, dtype=np.float32)
    identb = ident.astype(ml_dtypes.bfloat16)
    seln = np.zeros((NST, NST * 128), dtype=ml_dtypes.bfloat16)
    for n in range(NST):
        seln[n, n * 128:(n + 1) * 128] = 1.0
    onesb = np.ones((128, 1), dtype=ml_dtypes.bfloat16)
    ones1 = np.ones((1, 128), dtype=np.float32)
    fconst = np.tile(np.array([[EPS1, EPS3, -0.5, 1.0]], np.float32), (128, 1))

    if "nc" not in _CACHE:
        _CACHE["nc"] = build_program()
    nc, _stack, _n = _CACHE["nc"]
    in_maps = make_in_maps(x, w_fc, w_proj, wbct, wdtout, bdt, Dv, alog,
                           ident, identb, seln, onesb, ones1, fconst)
    with allow_ici_all_to_all():
        res = run_bass_kernel_spmd(nc, in_maps, core_ids=list(range(NCORES)))
    out = np.concatenate([r["out"] for r in res.results], axis=0)
    return out.reshape(2, TTOT // 2, D)


def make_in_maps(x, w_fc, w_proj, wbct, wdtout, bdt, Dv, alog,
                 ident, identb, seln, onesb, ones1, fconst):
    in_maps = []
    for k in range(NCORES):
        hs = slice(HSL * k, HSL * (k + 1))
        in_maps.append(dict(
            x=np.ascontiguousarray(x[TOKC * k:TOKC * (k + 1)]),
            w1s=np.ascontiguousarray(w_fc[(HID // NCORES) * k:
                                          (HID // NCORES) * (k + 1)]),
            w2s=np.ascontiguousarray(w_proj[(D // NCORES) * k:
                                            (D // NCORES) * (k + 1)]),
            wbct=wbct,
            wdtoutT=np.ascontiguousarray(wdtout[hs, :].T),
            bdts=np.ascontiguousarray(bdt[hs].reshape(2, 128).T),
            dsl=np.ascontiguousarray(Dv[hs].reshape(2, 128).T),
            alogs=np.ascontiguousarray(
                alog[hs, :].reshape(2, 128, NST).transpose(1, 0, 2)
                .reshape(128, 2 * NST)),
            ident=ident, identb=identb, seln=seln, onesb=onesb, ones1=ones1,
            fconst=fconst,
        ))
    return in_maps


# version 04
# speedup vs baseline: 1.0333x; 1.0333x over previous
import numpy as np
from contextlib import ExitStack

import concourse.bass as bass
import concourse.mybir as mybir
from concourse.bass_utils import run_bass_kernel_spmd
from concourse.aot_env import allow_ici_all_to_all

F32 = mybir.dt.float32
BF16 = mybir.dt.bfloat16
AF = mybir.ActivationFunctionType
ALU = mybir.AluOpType
AX = mybir.AxisListType

NCORES = 8
D = 2048
HID = 8192
TOKC = 2048
TTOT = 16384
HSL = 256
NST = 16
RNK = 64
SC = 512
NSC = TTOT // SC
FT = 512
NFT = TOKC // FT
PT = 256
NPT = TOKC // PT

EPS1 = 1e-5
EPS2 = 1e-6
EPS3 = 1e-6
QEPS = 1e-6

GAMMA_SCALE = None


class Op:
    __slots__ = ("eng", "fn", "deps", "sem_key", "inc", "need_inc", "tag")

    def __init__(self, eng, fn, deps, sem_key, inc, tag):
        self.eng, self.fn, self.deps = eng, fn, deps
        self.sem_key, self.inc, self.tag = sem_key, inc, tag
        self.need_inc = False


class Prog:
    ENGINES = ("v", "s", "t", "g", "y")

    def __init__(self):
        self.streams = {e: [] for e in self.ENGINES}
        self.all = []
        self._ncc = 0

    def op(self, eng, fn, deps=(), tag=""):
        o = Op(eng, fn, [d for d in deps if d is not None], eng, 1, tag)
        self.streams[eng].append(o)
        self.all.append(o)
        return o

    def dma(self, eng, stream, out, in_, deps=(), tag="", slot=0):
        def fn(E, out=out, in_=in_):
            return E.dma_start(out=out, in_=in_)
        o = Op(eng, fn, [d for d in deps if d is not None],
               "q" + eng + stream + str(slot), 16, tag)
        self.streams[eng].append(o)
        self.all.append(o)
        return o

    def collective(self, fn, deps=(), tag="cc"):
        key = "cc%d" % self._ncc
        self._ncc += 1
        o = Op("g", fn, [d for d in deps if d is not None], key, 1, tag)
        self.streams["g"].append(o)
        self.all.append(o)
        return o

    def finalize(self):
        import os
        strict = os.environ.get("KSTRICT", "1") == "1"
        pos = {}
        for e in self.ENGINES:
            for i, o in enumerate(self.streams[e]):
                pos[id(o)] = (e, i)
        if not strict:
            for o in self.all:
                oe, oi = pos[id(o)]
                o.deps = [d for d in o.deps
                          if not (d.sem_key == d.eng == oe and pos[id(d)][1] < oi)]
        else:
            for e in self.ENGINES:
                st = self.streams[e]
                for i in range(1, len(st)):
                    st[i].deps.append(st[i - 1])
        for o in self.all:
            for d in o.deps:
                d.need_inc = True
        for o in self.all:
            if o.inc == 16 or o.sem_key.startswith("cc"):
                o.need_inc = True
        counters = {}
        self.val = {}
        for e in self.ENGINES:
            for o in self.streams[e]:
                if o.need_inc:
                    counters[o.sem_key] = counters.get(o.sem_key, 0) + o.inc
                    self.val[o] = counters[o.sem_key]
        self.sem_keys = sorted(counters)

    def check(self):
        ptr = {e: 0 for e in self.ENGINES}
        sem = {k: 0 for k in self.sem_keys}
        done, total, progress = 0, len(self.all), True
        while progress:
            progress = False
            for e in self.ENGINES:
                while ptr[e] < len(self.streams[e]):
                    o = self.streams[e][ptr[e]]
                    if all(sem[d.sem_key] >= self.val[d] for d in o.deps):
                        if o.need_inc:
                            sem[o.sem_key] += o.inc
                        ptr[e] += 1
                        done += 1
                        progress = True
                    else:
                        break
        if done != total:
            stuck = []
            for e in self.ENGINES:
                if ptr[e] < len(self.streams[e]):
                    o = self.streams[e][ptr[e]]
                    why = [(d.tag, d.eng, self.val[d], sem[d.sem_key])
                           for d in o.deps if sem[d.sem_key] < self.val[d]]
                    stuck.append((e, o.tag, why[:2]))
            raise RuntimeError(f"deadlock {done}/{total}: {stuck}")

    def emit(self, nc, stack):
        self.finalize()
        self.check()
        sems = {k: stack.enter_context(nc.semaphore("sm_" + k))
                for k in self.sem_keys}
        block = stack.enter_context(nc.Block())
        prog = self

        def section(ekey):
            def run(E):
                waited = {}
                for o in prog.streams[ekey]:
                    for d in o.deps:
                        v = prog.val[d]
                        if waited.get(d.sem_key, -1) < v:
                            E.wait_ge(sems[d.sem_key], v)
                            waited[d.sem_key] = v
                    inst = o.fn(E)
                    if o.need_inc:
                        inst.then_inc(sems[o.sem_key], o.inc)
            return run

        block.vector(section("v"))
        block.scalar(section("s"))
        block.tensor(section("t"))
        block.gpsimd(section("g"))
        block.sync(section("y"))


class Carver:
    def __init__(self, tensor, nbytes, start=0):
        self.t, self.nbytes, self.cur = tensor, nbytes, start

    def take(self, rows, cols, dt):
        esz = 4 if dt == F32 else 2
        start = (self.cur + 3) // 4 * 4
        self.cur = start + cols * esz
        assert self.cur <= self.nbytes, f"arena overflow {self.cur}>{self.nbytes}"
        if dt == F32:
            return self.t.ap()[0:rows, start // 4:start // 4 + cols]
        return self.t.ap().bitcast(BF16)[0:rows, start // 2:start // 2 + cols]


def build_program():
    nc = bass.Bass("TRN2", target_bir_lowering=False, debug=False)
    P = Prog()

    x_in = nc.dram_tensor("x", [TOKC, D], F32, kind="ExternalInput").ap()
    w1s_in = nc.dram_tensor("w1s", [HID // NCORES, D], F32, kind="ExternalInput").ap()
    w2s_in = nc.dram_tensor("w2s", [D // NCORES, HID], F32, kind="ExternalInput").ap()
    wbct_in = nc.dram_tensor("wbct", [D, 96], F32, kind="ExternalInput").ap()
    wdtoutT_in = nc.dram_tensor("wdtoutT", [RNK, HSL], F32, kind="ExternalInput").ap()
    bdts_in = nc.dram_tensor("bdts", [128, 2], F32, kind="ExternalInput").ap()
    dsl_in = nc.dram_tensor("dsl", [128, 2], F32, kind="ExternalInput").ap()
    alogs_in = nc.dram_tensor("alogs", [128, 32], F32, kind="ExternalInput").ap()
    ident_in = nc.dram_tensor("ident", [128, 128], F32, kind="ExternalInput").ap()
    identb_in = nc.dram_tensor("identb", [128, 128], BF16, kind="ExternalInput").ap()
    seln_in = nc.dram_tensor("seln", [NST, NST * 128], BF16, kind="ExternalInput").ap()
    onesb_in = nc.dram_tensor("onesb", [128, 1], BF16, kind="ExternalInput").ap()
    ones1_in = nc.dram_tensor("ones1", [1, 128], F32, kind="ExternalInput").ap()
    fconst_in = nc.dram_tensor("fconst", [128, 4], F32, kind="ExternalInput").ap()
    out_ext = nc.dram_tensor("out", [TOKC, D], F32, kind="ExternalOutput").ap()

    ar_in = nc.dram_tensor("ar_in", [1, 2], F32)
    ar_out = nc.dram_tensor("ar_out", [1, 2], F32)
    w1ag_in = nc.dram_tensor("w1ag_in", [D, HID // NCORES], BF16)
    w1ag_out = nc.dram_tensor("w1ag_out", [NCORES * D, HID // NCORES], BF16)
    w2ag_in = nc.dram_tensor("w2ag_in", [HID, D // NCORES], BF16)
    w2ag_out = nc.dram_tensor("w2ag_out", [NCORES * HID, D // NCORES], BF16)
    dbag_in = nc.dram_tensor("dbag_in", [96, TOKC], F32)
    dbag_out = nc.dram_tensor("dbag_out", [NCORES * 96, TOKC], F32)
    ua_in = nc.dram_tensor("ua_in", [D, TOKC], BF16)
    ua_out = nc.dram_tensor("ua_out", [D, TOKC], BF16)
    sa_in = nc.dram_tensor("sa_in", [D, TOKC], BF16)
    sa_out = nc.dram_tensor("sa_out", [D, TOKC], BF16)
    xh1sp = nc.dram_tensor("xh1sp", [128, 16 * TOKC], BF16)
    gsp = nc.dram_tensor("gsp", [128, 16 * TOKC], BF16)

    stack = ExitStack()

    def sbuf(name, shape, dt=F32):
        return stack.enter_context(nc.sbuf_tensor(name, shape, dt))

    wbct = sbuf("s_wbct", [128, 16 * 96])
    wdtoutT = sbuf("s_wdtoutT", [RNK, HSL])
    bdts = sbuf("s_bdts", [128, 2])
    dsl = sbuf("s_dsl", [128, 2])
    alogs = sbuf("s_alogs", [128, 32])
    aneg = sbuf("aneg", [128, 32])
    ident = sbuf("s_ident", [128, 128])
    identb = sbuf("s_identb", [128, 128], BF16)
    seln = sbuf("s_seln", [NST, NST * 128], BF16)
    onesb = sbuf("s_onesb", [128, 1], BF16)
    ones1 = sbuf("s_ones1", [1, 128])
    fconst = sbuf("s_fconst", [128, 4])
    gam = sbuf("gam", [128, 6])
    onescol = sbuf("s_onescol", [128, 1])
    grow = sbuf("grow", [1, 16])
    tpart = sbuf("tpart", [128, 16])
    spack = sbuf("spack", [128, 8])
    srow = sbuf("srow", [8, 128])
    sdl = sbuf("sdl", [RNK, SC])
    sbc = sbuf("sbc", [NST, 2 * SC])
    sbcb = sbuf("sbcb", [NST, 2 * SC], BF16)
    sdt = sbuf("sdt", [128, 2 * SC])
    sdta = sbuf("sdta", [128, 2 * SC])
    sdtb2 = sbuf("sdtb2", [128, SC])
    sdtu = sbuf("sdtu", [128, 2 * SC], BF16)
    suc = sbuf("suc", [128, 2 * SC], BF16)
    sab = [sbuf(f"sab{i}", [128, SC]) for i in range(2)]
    sbb = sbuf("sbb", [128, SC], BF16)
    sbrep = [sbuf(f"sbrep{i}", [128, SC], BF16) for i in range(4)]
    sxb = [sbuf(f"sxb{i}", [128, SC]) for i in range(2)]
    cx = [sbuf(f"cx{i}", [128, SC * NST], BF16) for i in range(2)]
    yred = sbuf("yred", [128, 2 * SC])
    sout = sbuf("sout", [128, 2 * SC], BF16)
    scarry = sbuf("scarry", [128, 32])
    zcol = sbuf("zcol", [128, 1])

    ARENA_BYTES = 122368
    arena_t = sbuf("arena", [128, ARENA_BYTES // 4])
    At = Carver(arena_t, ARENA_BYTES)
    twa = At.take(128, 4096, F32)
    twb = At.take(128, 4096, F32)
    twm1 = At.take(128, 2048, F32)
    twm2 = At.take(128, 2048, F32)
    twh = At.take(128, 4096, BF16)
    tstage = At.take(128, 32 * 128, BF16)
    Ap = Carver(arena_t, ARENA_BYTES, start=At.cur)
    xtm = Ap.take(128, 2 * D, F32)
    xfm = Ap.take(128, 16 * PT, F32)
    ubuf = Ap.take(128, 16 * PT, BF16)
    xh1b = Ap.take(128, 16 * PT, BF16)
    mrep = Ap.take(128, PT, F32)
    r1rep = Ap.take(128, PT, F32)
    rCrep = Ap.take(128, PT, F32)
    srow2 = Ap.take(1, 3 * PT, F32)
    stat6 = Ap.take(128, 24, F32)
    statv = Ap.take(128, 2 * 6, F32)
    sdtbc = Ap.take(96, PT, F32)
    Af = Carver(arena_t, ARENA_BYTES)
    g1 = Af.take(128, 64 * FT, BF16)
    xh1t = Af.take(128, 16 * FT, BF16)
    o2n = xh1t
    w1buf = [Af.take(128, 16 * 128, BF16) for _ in range(2)]
    w2buf = [Af.take(128, 16 * 128, BF16) for _ in range(2)]
    sqc = [Af.take(128, FT, BF16) for _ in range(2)]
    strow = Af.take(1, 6 * FT, F32)
    m3rep = Af.take(128, FT, BF16)
    r3rep = Af.take(128, FT, BF16)
    Az = Carver(arena_t, ARENA_BYTES)
    fs = Az.take(128, 16 * FT, BF16)
    fg = Az.take(128, 16 * FT, BF16)
    fsg = Az.take(128, 16 * FT, BF16)
    fsgtm = Az.take(128, 4 * D, BF16)
    fxt = Az.take(128, 4 * D, F32)

    pb = [stack.enter_context(nc.psum_tensor(f"pb{i}", [128, 512], F32))
          for i in range(8)]
    pbb = [p.ap().bitcast(BF16) for p in pb]
    bank_free = {i: [] for i in range(8)}

    def bank_use(i, deps):
        return list(deps) + bank_free[i]

    def bank_release(i, consumers):
        bank_free[i] = list(consumers)

    c_wbct = P.dma("y", "cst", wbct.ap().rearrange("p (c r) -> p c r", r=96),
                   wbct_in.rearrange("(c p) r -> p c r", p=128), tag="c_wbct")
    c_wdt = P.dma("y", "cst", wdtoutT[:, :], wdtoutT_in, deps=[c_wbct],
                  tag="c_wdt")
    c_bdt = P.dma("y", "cst", bdts[:, :], bdts_in, deps=[c_wdt], tag="c_bdt")
    c_dsl = P.dma("y", "cst", dsl[:, :], dsl_in, deps=[c_bdt], tag="c_dsl")
    c_alog = P.dma("y", "cst", alogs[:, :], alogs_in, deps=[c_dsl], tag="c_alog")
    c_id = P.dma("y", "cst", ident[:, :], ident_in, deps=[c_alog], tag="c_id")
    c_idb = P.dma("y", "cst", identb[:, :], identb_in, deps=[c_id], tag="c_idb")
    c_sel = P.dma("y", "cst", seln[:, :], seln_in, deps=[c_idb], tag="c_sel")
    c_onb = P.dma("y", "cst", onesb[:, :], onesb_in, deps=[c_sel], tag="c_onb")
    c_on1 = P.dma("y", "cst", ones1[:, :], ones1_in, deps=[c_onb], tag="c_on1")
    c_fc = P.dma("y", "cst", fconst[:, :], fconst_in, deps=[c_on1], tag="c_fc")

    o_zc = P.op("v", lambda v: v.memset(zcol[:, :], 0.0), tag="zcol")
    o_onec = P.op("v", lambda v: v.memset(onescol[:, :], 1.0), tag="onec")
    o_aexp = P.op("s", lambda s: s.activation(aneg[:, :], alogs[:, :], AF.Exp),
                  deps=[c_alog], tag="aexp")
    o_aneg = P.op("v", lambda v: v.tensor_scalar_mul(aneg[:, :], aneg[:, :], -1.0),
                  deps=[o_aexp], tag="aneg")

    tern_units = [(1, i, 0, D, w1s_in) for i in range(8)] + \
                 [(2, i, hh, 4096, w2s_in) for i in range(2) for hh in range(2)]

    red_ops = []
    for ui, (mat, i, hh, wid, wsrc) in enumerate(tern_units):
        dst = twa if ui % 2 == 0 else twb
        prev = red_ops[ui - 2] if ui >= 2 else None
        d_ = P.dma("y", "wld", dst[:, 0:wid],
                   wsrc[i * 128:(i + 1) * 128, hh * 4096:hh * 4096 + wid],
                   deps=[prev], tag=f"tld{ui}", slot=ui % 2)
        r = P.op("v", lambda v, dst=dst, wid=wid, ui=ui: v.reduce_sum(
            tpart[:, ui:ui + 1], dst[:, 0:wid], axis=AX.X,
            apply_absolute_value=True), deps=[d_], tag=f"tred{ui}")
        red_ops.append(r)

    o_gmm = P.op("t", lambda t: t.matmul(pb[7][0:1, 0:12], onescol[:, :],
                                         tpart[:, 0:12], start=True, stop=True),
                 deps=bank_use(7, [red_ops[-1], o_onec]), tag="gmm")
    o_gr1 = P.op("v", lambda v: v.reduce_sum(grow[0:1, 0:1], pb[7][0:1, 0:8],
                                             axis=AX.X), deps=[o_gmm], tag="gr1")
    o_gr2 = P.op("v", lambda v: v.reduce_sum(grow[0:1, 1:2], pb[7][0:1, 8:12],
                                             axis=AX.X), deps=[o_gmm], tag="gr2")
    bank_release(7, [o_gr1, o_gr2])
    d_gar = P.dma("y", "ccs", ar_in[0:1, :], grow[0:1, 0:2],
                  deps=[o_gr1, o_gr2], tag="gar_st")
    cc_ar = P.collective(lambda g: g.collective_compute(
        "AllReduce", ALU.add, replica_groups=[list(range(NCORES))],
        ins=[ar_in.ap().opt()], outs=[ar_out.ap().opt()]),
        deps=[d_gar], tag="cc_ar")
    d_gld = P.dma("y", "ccs", grow[0:1, 2:4], ar_out[0:1, :], deps=[cc_ar],
                  tag="gar_ld")
    o_gth = P.op("v", lambda v: v.tensor_scalar(
        grow[0:1, 4:6], grow[0:1, 2:4], 1.0 / (HID * D), QEPS, ALU.mult, ALU.add),
        deps=[d_gld], tag="gth")
    o_gth2 = P.op("v", lambda v: v.tensor_scalar_mul(
        grow[0:1, 4:6], grow[0:1, 4:6], 0.5), deps=[o_gth], tag="gth2")
    o_gthn = P.op("v", lambda v: v.tensor_scalar_mul(
        grow[0:1, 6:8], grow[0:1, 4:6], -1.0), deps=[o_gth2], tag="gthn")
    o_gm1 = P.op("v", lambda v: v.tensor_scalar_mul(
        grow[0:1, 8:10], grow[0:1, 2:4], 1.0 / (HID * D)), deps=[d_gld],
        tag="gm1")
    o_gbc = P.op("t", lambda t: t.matmul(pb[7][:, 0:6], ones1[0:1, :],
                                         grow[0:1, 4:10], start=True, stop=True),
                 deps=bank_use(7, [o_gthn, o_gm1]), tag="gbc")
    o_gbe = P.op("s", lambda s: s.activation(gam[:, 0:6], pb[7][:, 0:6], AF.Copy),
                 deps=[o_gbc], tag="gbe")
    bank_release(7, [o_gbe])

    tern_done = []
    w1_last = w2_last = None
    for ui, (mat, i, hh, wid, wsrc) in enumerate(tern_units):
        thp = gam[:, mat - 1:mat]
        thn = gam[:, mat + 1:mat + 2]
        dst = twa if ui % 2 == 0 else twb
        prev = tern_done[ui - 2] if ui >= 2 else red_ops[-1]
        d_ = P.dma("y", "wld", dst[:, 0:wid],
                   wsrc[i * 128:(i + 1) * 128, hh * 4096:hh * 4096 + wid],
                   deps=[prev, red_ops[-1]], tag=f"t2ld{ui}", slot=ui % 2)
        subs = wid // 2048
        mops = []
        for ss in range(subs):
            sl = slice(ss * 2048, (ss + 1) * 2048)
            o1 = P.op("v", lambda v, dst=dst, sl=sl, thp=thp: v.tensor_single_scalar(
                twm1[:, 0:2048], dst[:, sl], thp, ALU.is_gt),
                deps=[d_, o_gbe], tag=f"tm1_{ui}_{ss}")
            o2 = P.op("v", lambda v, dst=dst, sl=sl, thn=thn: v.tensor_single_scalar(
                twm2[:, 0:2048], dst[:, sl], thn, ALU.is_lt),
                deps=[d_, o_gbe], tag=f"tm2_{ui}_{ss}")
            o3 = P.op("v", lambda v, sl=sl: v.tensor_sub(
                twh[:, sl], twm1[:, 0:2048], twm2[:, 0:2048]),
                deps=[o1, o2], tag=f"tmh_{ui}_{ss}")
            mops.append(o3)
        nblk = wid // 128
        evs = []
        prev_dst = tern_done[-1] if tern_done else None
        for j in range(nblk):
            bk = j % 2
            tr = P.op("t", lambda t, j=j, bk=bk: t.transpose(
                pbb[bk][:, 0:128], twh[:, j * 128:(j + 1) * 128], identb[:, :]),
                deps=bank_use(bk, [mops[j // 16], c_idb]), tag=f"ttr{ui}_{j}")
            ev = P.op("s", lambda s, j=j, bk=bk: s.activation(
                tstage[:, j * 128:(j + 1) * 128], pbb[bk][:, 0:128], AF.Copy),
                deps=[tr, prev_dst], tag=f"tte{ui}_{j}")
            bank_release(bk, [ev])
            evs.append(ev)
        agdst = w1ag_in if mat == 1 else w2ag_in
        dd = agdst.rearrange("(j p) o -> p j o", p=128)
        d_st = P.dma("y", "wst",
                     dd[:, hh * 32:hh * 32 + nblk, i * 128:(i + 1) * 128],
                     tstage.rearrange("p (j o) -> p j o", o=128)[:, 0:nblk, :],
                     deps=evs, tag=f"tst{ui}")
        if mat == 1:
            w1_last = d_st
        else:
            w2_last = d_st
        tern_done.append(d_st)

    cc_w1 = P.collective(lambda g: g.collective_compute(
        "AllGather", ALU.bypass, replica_groups=[list(range(NCORES))],
        ins=[w1ag_in.ap().opt()], outs=[w1ag_out.ap().opt()]),
        deps=[w1_last], tag="cc_w1")
    cc_w2 = P.collective(lambda g: g.collective_compute(
        "AllGather", ALU.bypass, replica_groups=[list(range(NCORES))],
        ins=[w2ag_in.ap().opt()], outs=[w2ag_out.ap().opt()]),
        deps=[w2_last], tag="cc_w2")

    dbc_dmas, ua_dmas, xh_dmas = [], [], []
    prev_tile = []
    for pt in range(NPT):
        t0 = pt * PT
        d_x = P.dma("y", "xld", xtm.rearrange("p (tt d) -> p tt d", d=D),
                    x_in[t0:t0 + PT, :].rearrange("(tt p) d -> p tt d", p=128),
                    deps=prev_tile, tag=f"pxl{pt}")
        pk_all = []
        for tt in range(2):
            bns = []
            for cch in range(4):
                o = P.op("v", lambda v, tt=tt, cch=cch: v.bn_stats(
                    stat6[:, cch * 6:(cch + 1) * 6],
                    xtm[:, tt * D + cch * 512:tt * D + (cch + 1) * 512]),
                    deps=[d_x], tag=f"pbn{pt}_{tt}_{cch}")
                bns.append(o)
            o_ag = P.op("v", lambda v, tt=tt: v.bn_aggr(
                statv[:, tt * 6:tt * 6 + 2], stat6[:, 0:24]),
                deps=[bns[-1]], tag=f"pag{pt}_{tt}")
            l1 = P.op("s", lambda s, tt=tt: s.activation(
                statv[:, tt * 6 + 4:tt * 6 + 5], statv[:, tt * 6 + 1:tt * 6 + 2],
                AF.Ln, bias=fconst[:, 0:1]), deps=[o_ag, c_fc], tag=f"pl1{pt}_{tt}")
            e1 = P.op("s", lambda s, tt=tt: s.activation(
                statv[:, tt * 6 + 2:tt * 6 + 3], statv[:, tt * 6 + 4:tt * 6 + 5],
                AF.Exp, scale=fconst[:, 2:3]), deps=[l1], tag=f"pe1{pt}_{tt}")
            vv = P.op("v", lambda v, tt=tt: v.tensor_scalar(
                statv[:, tt * 6 + 5:tt * 6 + 6], statv[:, tt * 6 + 1:tt * 6 + 2],
                1.0 + EPS2, EPS1 * EPS2, ALU.mult, ALU.add),
                deps=[o_ag], tag=f"pvv{pt}_{tt}")
            lC = P.op("s", lambda s, tt=tt: s.activation(
                statv[:, tt * 6 + 5:tt * 6 + 6], statv[:, tt * 6 + 5:tt * 6 + 6],
                AF.Ln), deps=[vv], tag=f"plC{pt}_{tt}")
            eC = P.op("s", lambda s, tt=tt: s.activation(
                statv[:, tt * 6 + 3:tt * 6 + 4], statv[:, tt * 6 + 5:tt * 6 + 6],
                AF.Exp, scale=fconst[:, 2:3]), deps=[lC], tag=f"peC{pt}_{tt}")
            for j, col in ((0, 0), (1, 2), (2, 3)):
                o = P.op("v", lambda v, tt=tt, j=j, col=col: v.tensor_copy(
                    spack[:, tt * 3 + j:tt * 3 + j + 1],
                    statv[:, tt * 6 + col:tt * 6 + col + 1]),
                    deps=[e1, eC], tag=f"ppk{pt}_{tt}_{j}")
                pk_all.append(o)
        tr_s = P.op("t", lambda t: t.transpose(pb[7][0:6, 0:128], spack[:, 0:6],
                                               ident[:, :]),
                    deps=bank_use(7, pk_all + [c_id]), tag=f"ptr{pt}")
        ev_s = P.op("s", lambda s: s.activation(srow[0:6, :], pb[7][0:6, 0:128],
                                                AF.Copy), deps=[tr_s],
                    tag=f"pev{pt}")
        bank_release(7, [ev_s])
        d_rm = None
        for jj in range(3):
            for tt in range(2):
                d_rm = P.dma("y", "srw",
                             srow2[0:1, jj * PT + tt * 128:jj * PT + tt * 128 + 128],
                             srow[tt * 3 + jj:tt * 3 + jj + 1, :],
                             deps=[ev_s, d_rm], tag=f"prm{pt}_{jj}_{tt}")
        reps = []
        for j, dstrep in ((0, mrep), (1, r1rep), (2, rCrep)):
            bc = P.op("t", lambda t, j=j: t.matmul(
                pb[7][:, 0:PT], ones1[0:1, :], srow2[0:1, j * PT:(j + 1) * PT],
                start=True, stop=True), deps=bank_use(7, [d_rm]),
                tag=f"pbc{pt}_{j}")
            evr = P.op("s", lambda s, dstrep=dstrep: s.activation(
                dstrep[:, :], pb[7][:, 0:PT], AF.Copy), deps=[bc],
                tag=f"pbe{pt}_{j}")
            bank_release(7, [evr])
            reps.append(evr)
        trev = []
        for tt in range(2):
            for dc in range(16):
                j = tt * 16 + dc
                bk = 3 + j % 2
                tr = P.op("t", lambda t, tt=tt, dc=dc, bk=bk: t.transpose(
                    pb[bk][:, 0:128],
                    xtm[:, tt * D + dc * 128:tt * D + (dc + 1) * 128],
                    ident[:, :]), deps=bank_use(bk, [d_x, c_id]),
                    tag=f"pxt{pt}_{j}")
                ev = P.op("s", lambda s, tt=tt, dc=dc, bk=bk: s.activation(
                    xfm[:, dc * PT + tt * 128:dc * PT + tt * 128 + 128],
                    pb[bk][:, 0:128], AF.Copy), deps=[tr], tag=f"pxe{pt}_{j}")
                bank_release(bk, [ev])
                trev.append(ev)
        xv = xfm.rearrange("p (dc t) -> p dc t", t=PT)
        mv = mrep.rearrange("p (o t) -> p o t", o=1)
        xvb, mvb = bass.broadcast_tensor_aps(xv, mv)
        o_sub = P.op("v", lambda v: v.tensor_sub(xv, xvb, mvb),
                     deps=trev + [reps[0]], tag=f"psub{pt}")
        uv = ubuf.rearrange("p (dc t) -> p dc t", t=PT)
        r1v = r1rep.rearrange("p (o t) -> p o t", o=1)
        _, r1vb = bass.broadcast_tensor_aps(xv, r1v)
        o_u = P.op("v", lambda v: v.tensor_mul(uv, xvb, r1vb),
                   deps=[o_sub, reps[1]], tag=f"pu{pt}")
        xh1v = xh1b.rearrange("p (dc t) -> p dc t", t=PT)
        rCv = rCrep.rearrange("p (o t) -> p o t", o=1)
        _, rCvb = bass.broadcast_tensor_aps(xv, rCv)
        o_xh = P.op("v", lambda v: v.tensor_mul(xh1v, xvb, rCvb),
                    deps=[o_sub, reps[2]], tag=f"pxh{pt}")
        mms = []
        for dc in range(16):
            mm = P.op("t", lambda t, dc=dc: t.matmul(
                pb[5][0:96, 0:PT], wbct[:, dc * 96:(dc + 1) * 96],
                xfm[:, dc * PT:(dc + 1) * PT], start=(dc == 0), stop=(dc == 15)),
                deps=(bank_use(5, [o_sub, c_wbct]) if dc == 0 else [mms[-1]]),
                tag=f"pdb{pt}_{dc}")
            mms.append(mm)
        o_dbe = P.op("v", lambda v: v.tensor_mul(
            sdtbc[0:96, :], pb[5][0:96, 0:PT], r1rep[0:96, :]),
            deps=[mms[-1], reps[1]], tag=f"pdbe{pt}")
        bank_release(5, [o_dbe])
        d_db = P.dma("y", "dbc", dbag_in[0:96, t0:t0 + PT], sdtbc[0:96, :],
                     deps=[o_dbe], tag=f"pdbs{pt}")
        d_ua = P.dma("y", "uas",
                     ua_in.rearrange("(dc p) t -> p dc t", p=128)[:, :, t0:t0 + PT],
                     uv, deps=[o_u], tag=f"pua{pt}")
        d_xh = P.dma("y", "xsp",
                     xh1sp.rearrange("p (dc t) -> p dc t", t=TOKC)[:, :, t0:t0 + PT],
                     xh1v, deps=[o_xh], tag=f"pxs{pt}")
        dbc_dmas.append(d_db)
        ua_dmas.append(d_ua)
        xh_dmas.append(d_xh)
        prev_tile = [o_u, o_xh, o_dbe, d_ua, d_xh, d_db, mms[-1]]

    cc_db = P.collective(lambda g: g.collective_compute(
        "AllGather", ALU.bypass, replica_groups=[list(range(NCORES))],
        ins=[dbag_in.ap().opt()], outs=[dbag_out.ap().opt()]),
        deps=dbc_dmas, tag="cc_db")
    cc_ua = P.collective(lambda g: g.collective_compute(
        "AllToAll", ALU.bypass, replica_groups=[list(range(NCORES))],
        ins=[ua_in.ap().opt()], outs=[ua_out.ap().opt()]),
        deps=ua_dmas, tag="cc_ua")

    war = {
        "sdl": [cc_db], "sbc": [cc_db], "sbcb": [], "suc": [cc_ua],
        "sout": {0: None, 1: None},
        "scan_pp": {0: None, 1: None},
        "carry_pp": {0: None, 1: None},
        "b_pp": {0: None, 1: None},
        "c_pp": {0: None, 1: None},
        "pool": {0: None, 1: None},
        "aexp_last": None,
        "b_last": None,
    }
    prev_carry = {}
    sdma_s = []
    ffn_prev = []

    scan_tail = []

    def emit_scan_chunk(c):
        import os as _os
        SPART = int(_os.environ.get("KSCANPART", "5"))
        r, off = c // 4, (c % 4) * SC
        d_dl = P.dma("y", "sdl", sdl[:, :],
                     dbag_out[r * 96:r * 96 + RNK, off:off + SC],
                     deps=war["sdl"], tag=f"sdl{c}")
        d_b = P.dma("y", "sbc", sbc[:, 0:SC],
                    dbag_out[r * 96 + RNK:r * 96 + 80, off:off + SC],
                    deps=war["sbc"], tag=f"sBl{c}", slot=0)
        d_c = P.dma("y", "sbc", sbc[:, SC:2 * SC],
                    dbag_out[r * 96 + 80:r * 96 + 96, off:off + SC],
                    deps=war["sbc"], tag=f"sCl{c}", slot=1)
        d_u0 = P.dma("y", "sc2", suc[:, 0:SC],
                     ua_out[r * 256:r * 256 + 128, off:off + SC],
                     deps=war["suc"], tag=f"su0{c}", slot=0)
        d_u1 = P.dma("y", "sc2", suc[:, SC:2 * SC],
                     ua_out[r * 256 + 128:r * 256 + 256, off:off + SC],
                     deps=war["suc"], tag=f"su1{c}", slot=1)
        o_bc16 = P.op("v", lambda v: v.tensor_copy(sbcb[:, :], sbc[:, :]),
                      deps=[d_b, d_c] + war["sbcb"], tag=f"sbc16_{c}")
        dt_ops = []
        for m in range(2):
            sl = slice(m * SC, (m + 1) * SC)
            mm = P.op("t", lambda t, m=m: t.matmul(
                pb[7][:, 0:SC], wdtoutT[:, m * 128:(m + 1) * 128], sdl[:, :],
                start=True, stop=True),
                deps=bank_use(7, [d_dl, c_wdt]), tag=f"sdtm{c}_{m}")
            a1 = P.op("s", lambda s, sl=sl, m=m: s.activation(
                sdta[:, sl], pb[7][:, 0:SC], AF.Abs, bias=bdts[:, m:m + 1]),
                deps=[mm, c_bdt], tag=f"sab{c}_{m}")
            r1 = P.op("v", lambda v, m=m: v.tensor_scalar(
                sdtb2[:, :], pb[7][:, 0:SC], bdts[:, m:m + 1], 0.0,
                ALU.add, ALU.max), deps=[mm, c_bdt, a1], tag=f"srl{c}_{m}")
            bank_release(7, [a1, r1])
            e1 = P.op("s", lambda s, sl=sl: s.activation(
                sdta[:, sl], sdta[:, sl], AF.Exp, scale=-1.0),
                deps=[a1], tag=f"sex{c}_{m}")
            l1 = P.op("s", lambda s, sl=sl: s.activation(
                sdta[:, sl], sdta[:, sl], AF.Ln, bias=1.0),
                deps=[e1], tag=f"sln{c}_{m}")
            o_dt = P.op("v", lambda v, sl=sl: v.tensor_add(
                sdt[:, sl], sdta[:, sl], sdtb2[:, :]),
                deps=[l1, r1, war["aexp_last"]], tag=f"sdt{c}_{m}")
            o_du = P.op("v", lambda v, sl=sl: v.tensor_mul(
                sdtu[:, sl], sdt[:, sl], suc[:, sl]),
                deps=[o_dt, d_u0, d_u1], tag=f"sdu{c}_{m}")
            dt_ops.append((o_dt, o_du))
        yield
        if SPART <= 1:
            scan_tail.append(dt_ops[1][1])
            war["sdl"] = [dt_ops[0][0], dt_ops[1][0]]
            war["sbc"] = [o_bc16]
            war["suc"] = [dt_ops[0][1], dt_ops[1][1]]
            war["sbcb"] = [o_bc16]
            return
        war["sdl"] = [dt_ops[0][0], dt_ops[1][0]]
        war["sbc"] = [o_bc16]
        last_a = None
        last_b = None
        last_bc = None
        for m in range(2):
            sl = slice(m * SC, (m + 1) * SC)
            cmuls = []
            for n in range(NST):
                g = m * NST + n
                pp = g % 2
                bcB = P.op("t", lambda t, n=n: t.matmul(
                    pb[5][:, 0:SC], seln[:, n * 128:(n + 1) * 128],
                    sbcb[:, 0:SC], start=True, stop=True),
                    deps=bank_use(5, [o_bc16, c_sel]), tag=f"bB{c}_{g}")
                evB = P.op("s", lambda s, pp=pp: s.activation(
                    sbrep[pp][:, :], pb[5][:, 0:SC], AF.Copy),
                    deps=[bcB, war["b_pp"][pp]], tag=f"eB{c}_{g}")
                bank_release(5, [evB])
                bcC = P.op("t", lambda t, n=n: t.matmul(
                    pb[6][:, 0:SC], seln[:, n * 128:(n + 1) * 128],
                    sbcb[:, SC:2 * SC], start=True, stop=True),
                    deps=bank_use(6, [o_bc16, c_sel]), tag=f"bC{c}_{g}")
                evC = P.op("s", lambda s, pp=pp: s.activation(
                    sbrep[2 + pp][:, :], pb[6][:, 0:SC], AF.Copy),
                    deps=[bcC, war["c_pp"][pp]], tag=f"eC{c}_{g}")
                bank_release(6, [evC])
                last_bc = bcC
                if SPART <= 2:
                    evC = P.op("s", lambda s, pp=pp: s.activation(
                        sbrep[pp][:, :], pb[6][:, 0:SC], AF.Copy),
                        deps=[bcC, war["b_pp"][pp]], tag=f"eC{c}_{g}")
                    bank_release(6, [evC])
                    war["b_pp"][pp] = evC
                    scan_tail.append(evC)
                    continue
                o_a = P.op("s", lambda s, sl=sl, m=m, n=n, pp=pp: s.activation(
                    sab[pp][:, :], sdt[:, sl], AF.Exp,
                    scale=aneg[:, m * 16 + n:m * 16 + n + 1]),
                    deps=[dt_ops[m][0], o_aneg, war["scan_pp"][pp]],
                    tag=f"a{c}_{g}")
                last_a = o_a
                o_b = P.op("v", lambda v, sl=sl, pp=pp: v.tensor_mul(
                    sbb[:, :], sdtu[:, sl], sbrep[pp][:, :]),
                    deps=[dt_ops[m][1], evB], tag=f"b{c}_{g}")
                last_b = o_b
                war["b_pp"][pp] = o_b
                if c == 0 or c == 16:
                    init = zcol[:, 0:1]
                    ideps = [o_a, o_b, o_zc, war["carry_pp"][pp]]
                else:
                    init = scarry[:, g:g + 1]
                    ideps = [o_a, o_b, prev_carry.get((c - 1, g)),
                             war["carry_pp"][pp]]
                o_x = P.op("v", lambda v, pp=pp, init=init: v.tensor_tensor_scan(
                    sxb[pp][:, :], sab[pp][:, :], sbb[:, :], init,
                    ALU.mult, ALU.add), deps=ideps, tag=f"x{c}_{g}")
                war["scan_pp"][pp] = o_x
                o_cr = P.op("g", lambda gp, pp=pp, g=g: gp.tensor_copy(
                    scarry[:, g:g + 1], sxb[pp][:, SC - 1:SC]),
                    deps=[o_x], tag=f"cr{c}_{g}")
                prev_carry[(c, g)] = o_cr
                war["carry_pp"][pp] = o_cr
                if SPART <= 3:
                    war["scan_pp"][pp] = o_x
                    bank_release(6, [o_x])
                    scan_tail.append(o_cr)
                    continue
                o_cm = P.op("v", lambda v, pp=pp, n=n, m=m: v.tensor_mul(
                    cx[m][:, n * SC:(n + 1) * SC],
                    sxb[pp][:, :], sbrep[2 + pp][:, :]),
                    deps=[o_x, evC, war["pool"][m]], tag=f"cm{c}_{g}")
                war["c_pp"][pp] = o_cm
                cmuls.append(o_cm)
                if n % 4 == 3:
                    yield
            if SPART <= 4:
                war["pool"][m] = cmuls[-1]
                scan_tail.append(cmuls[-1])
                continue
            yv = yred[:, m * SC:(m + 1) * SC]
            a0 = P.op("g", lambda gp, yv=yv, m=m: gp.tensor_add(
                yv, cx[m][:, 0:SC], cx[m][:, SC:2 * SC]),
                deps=[cmuls[0], cmuls[1]], tag=f"yr{c}_{m}_0")
            last = a0
            for n in range(2, NST):
                last = P.op("g", lambda gp, yv=yv, m=m, n=n: gp.tensor_add(
                    yv, yv, cx[m][:, n * SC:(n + 1) * SC]),
                    deps=[last, cmuls[n]], tag=f"yr{c}_{m}_{n}")
            sl = slice(m * SC, (m + 1) * SC)
            o_st = P.op("v", lambda gp, sl=sl, m=m: gp.scalar_tensor_tensor(
                sout[:, sl], suc[:, sl], dsl[:, m:m + 1], yred[:, sl],
                ALU.mult, ALU.add),
                deps=[last, c_dsl, d_u0, d_u1, war["sout"][m]],
                tag=f"ys{c}_{m}")
            war["pool"][m] = o_st
            d_s = P.dma("y", "sst",
                        sa_in[r * 256 + m * 128:r * 256 + (m + 1) * 128,
                              off:off + SC],
                        sout[:, sl], deps=[o_st], tag=f"ss{c}_{m}", slot=m)
            war["sout"][m] = d_s
            sdma_s.append(d_s)
        war["aexp_last"] = last_a
        war["b_last"] = last_b
        war["suc"] = [dt_ops[0][1], dt_ops[1][1], war["pool"][0], war["pool"][1]]
        war["sbcb"] = [last_bc]

    def emit_ffn_tile(f):
        nonlocal ffn_prev, g_last
        t0 = f * FT
        d_xh = P.dma("s", "xht", xh1t.rearrange("p (dc t) -> p dc t", t=FT),
                     xh1sp.rearrange("p (dc t) -> p dc t", t=TOKC)[
                         :, :, t0:t0 + FT],
                     deps=xh_dmas + [cc_w1, cc_w2] + ffn_prev, tag=f"fxh{f}")
        w1v = w1ag_out.rearrange("(r dc p) o -> p r dc o", p=128, dc=16)
        evacs = []
        wprev = [None, None]
        for oc in range(64):
            r_, oc_ = oc // 8, oc % 8
            half = oc % 2
            d_w = P.dma("s", "w1s",
                        w1buf[half].rearrange("p (dc o) -> p dc o", o=128),
                        w1v[:, r_, :, oc_ * 128:(oc_ + 1) * 128],
                        deps=[wprev[half], cc_w1], tag=f"fw1{f}_{oc}", slot=half)
            mms = []
            for kb in range(16):
                dep = ([d_w, d_xh] + bank_use(half, [])) if kb == 0 else [mms[-1]]
                mm = P.op("t", lambda t, half=half, kb=kb: t.matmul(
                    pb[half][:, 0:FT], w1buf[half][:, kb * 128:(kb + 1) * 128],
                    xh1t[:, kb * FT:(kb + 1) * FT],
                    start=(kb == 0), stop=(kb == 15)), deps=dep,
                    tag=f"fm1{f}_{oc}_{kb}")
                mms.append(mm)
            wprev[half] = mms[-1]
            if oc % 8 == 7:
                yield
            ev = P.op("v", lambda v, oc=oc, half=half: v.tensor_scalar_mul(
                g1[:, oc * FT:(oc + 1) * FT], pb[half][:, 0:FT], gam[:, 4:5]),
                deps=[mms[-1], o_gbe], tag=f"fe1{f}_{oc}")
            bank_release(half, [ev])
            evacs.append(ev)
        o_gel = P.op("s", lambda s: s.activation(g1[:, :], g1[:, :], AF.Gelu),
                     deps=[evacs[-1]], tag=f"fgel{f}")
        summs = []
        for oc in range(64):
            dep = (bank_use(2, [o_gel, c_onb])) if oc == 0 else [summs[-1]]
            mm = P.op("t", lambda t, oc=oc: t.matmul(
                pb[2][0:1, 0:FT], onesb[:, 0:1], g1[:, oc * FT:(oc + 1) * FT],
                start=(oc == 0), stop=(oc == 63)), deps=dep, tag=f"fsm{f}_{oc}")
            summs.append(mm)
        yield
        ev_sum = P.op("v", lambda v: v.tensor_copy(strow[0:1, 0:FT],
                                                   pb[2][0:1, 0:FT]),
                      deps=[summs[-1], evacs[-1]], tag=f"fsev{f}")
        sqmms, sq_ops = [], []
        for oc in range(64):
            half = oc % 2
            o_sq = P.op("v", lambda v, oc=oc, half=half: v.tensor_mul(
                sqc[half][:, :], g1[:, oc * FT:(oc + 1) * FT],
                g1[:, oc * FT:(oc + 1) * FT]),
                deps=[ev_sum] if oc == 0 else ([sqmms[-2]] if oc >= 2 else []),
                tag=f"fsq{f}_{oc}")
            dep = [o_sq] + ([ev_sum] if oc == 0 else [sqmms[-1]])
            mm = P.op("t", lambda t, half=half, oc=oc: t.matmul(
                pb[2][0:1, 0:FT], onesb[:, 0:1], sqc[half][:, :],
                start=(oc == 0), stop=(oc == 63)), deps=dep, tag=f"fqm{f}_{oc}")
            sqmms.append(mm)
            sq_ops.append(o_sq)
        ev_sq = P.op("v", lambda v: v.tensor_copy(strow[0:1, FT:2 * FT],
                                                  pb[2][0:1, 0:FT]),
                     deps=[sqmms[-1]], tag=f"fqev{f}")
        bank_release(2, [ev_sq])
        o_m3 = P.op("v", lambda v: v.tensor_scalar_mul(
            strow[0:1, 2 * FT:3 * FT], strow[0:1, 0:FT], 1.0 / HID),
            deps=[ev_sum], tag=f"fm3{f}")
        o_s2 = P.op("s", lambda s: s.activation(
            strow[0:1, 3 * FT:4 * FT], strow[0:1, 2 * FT:3 * FT], AF.Square),
            deps=[o_m3], tag=f"fs2{f}")
        o_e2 = P.op("v", lambda v: v.tensor_scalar_mul(
            strow[0:1, 4 * FT:5 * FT], strow[0:1, FT:2 * FT], 1.0 / HID),
            deps=[ev_sq], tag=f"fe2{f}")
        o_var = P.op("v", lambda v: v.tensor_sub(
            strow[0:1, 3 * FT:4 * FT], strow[0:1, 4 * FT:5 * FT],
            strow[0:1, 3 * FT:4 * FT]), deps=[o_s2, o_e2], tag=f"fvar{f}")
        o_ln3 = P.op("s", lambda s: s.activation(
            strow[0:1, 4 * FT:5 * FT], strow[0:1, 3 * FT:4 * FT], AF.Ln,
            bias=fconst[0:1, 1:2]), deps=[o_var, c_fc], tag=f"fln{f}")
        o_r3 = P.op("s", lambda s: s.activation(
            strow[0:1, 5 * FT:6 * FT], strow[0:1, 4 * FT:5 * FT], AF.Exp,
            scale=fconst[0:1, 2:3]), deps=[o_ln3], tag=f"fr3{f}")
        bc_m = P.op("t", lambda t: t.matmul(
            pb[7][:, 0:FT], ones1[0:1, :], strow[0:1, 2 * FT:3 * FT],
            start=True, stop=True), deps=bank_use(7, [o_m3]), tag=f"fbm{f}")
        ev_m = P.op("s", lambda s: s.activation(m3rep[:, :], pb[7][:, 0:FT],
                                                AF.Copy),
                    deps=[bc_m] + ffn_prev, tag=f"fbme{f}")
        bank_release(7, [ev_m])
        bc_r = P.op("t", lambda t: t.matmul(
            pb[7][:, 0:FT], ones1[0:1, :], strow[0:1, 5 * FT:6 * FT],
            start=True, stop=True), deps=bank_use(7, [o_r3]), tag=f"fbr{f}")
        ev_r = P.op("s", lambda s: s.activation(r3rep[:, :], pb[7][:, 0:FT],
                                                AF.Copy),
                    deps=[bc_r] + ffn_prev, tag=f"fbre{f}")
        bank_release(7, [ev_r])
        g1v = g1.rearrange("p (oc t) -> p oc t", t=FT)
        m3v = m3rep.rearrange("p (o t) -> p o t", o=1)
        g1vb, m3vb = bass.broadcast_tensor_aps(g1v, m3v)
        o_x2a = P.op("v", lambda v: v.tensor_sub(g1v, g1vb, m3vb),
                     deps=[ev_m, ev_sq], tag=f"fx2a{f}")
        r3v = r3rep.rearrange("p (o t) -> p o t", o=1)
        _, r3vb = bass.broadcast_tensor_aps(g1v, r3v)
        o_x2b = P.op("v", lambda v: v.tensor_mul(g1v, g1vb, r3vb),
                     deps=[o_x2a, ev_r], tag=f"fx2b{f}")
        yield
        w2v = w2ag_out.rearrange("(r kk p) o -> p r kk o", p=128, kk=64)
        ev2s = []
        w2prev = [None, None]
        for oc2 in range(16):
            r_, oc_ = oc2 // 2, oc2 % 2
            bk = 3 + oc2 % 2
            dws = []
            mms = []
            for q in range(4):
                d_w = P.dma("s", "w2s",
                            w2buf[q % 2].rearrange("p (kk o) -> p kk o", o=128),
                            w2v[:, r_, q * 16:(q + 1) * 16,
                                oc_ * 128:(oc_ + 1) * 128],
                            deps=[w2prev[q % 2]] +
                                 ([cc_w2] if oc2 == 0 else []) +
                                 ([mms[(q - 2) * 16 + 15]] if q >= 2 else []),
                            tag=f"fw2{f}_{oc2}_{q}", slot=q % 2)
                dws.append(d_w)
                for kk in range(16):
                    kb = q * 16 + kk
                    if kb == 0:
                        dep = [d_w, o_x2b] + bank_use(bk, [])
                    elif kk == 0:
                        dep = [d_w, mms[-1]]
                    else:
                        dep = [mms[-1]]
                    mm = P.op("t", lambda t, bk=bk, kb=kb, q=q, kk=kk: t.matmul(
                        pb[bk][:, 0:FT],
                        w2buf[q % 2][:, kk * 128:(kk + 1) * 128],
                        g1[:, kb * FT:(kb + 1) * FT],
                        start=(kb == 0), stop=(kb == 63)), deps=dep,
                        tag=f"fm2{f}_{oc2}_{kb}")
                    mms.append(mm)
            w2prev = [mms[47], mms[63]]
            ev = P.op("v", lambda v, oc2=oc2, bk=bk: v.tensor_scalar_mul(
                o2n[:, oc2 * FT:(oc2 + 1) * FT], pb[bk][:, 0:FT], gam[:, 5:6]),
                deps=[mms[-1], o_gbe], tag=f"fe2{f}_{oc2}")
            bank_release(bk, [ev])
            ev2s.append(ev)
            if oc2 % 2 == 1:
                yield
        o_sil = P.op("s", lambda s: s.activation(o2n[:, :], o2n[:, :], AF.Silu),
                     deps=[ev2s[-1]], tag=f"fsil{f}")
        d_g = P.dma("y", "gsp",
                    gsp.rearrange("p (dc t) -> p dc t", t=TOKC)[:, :, t0:t0 + FT],
                    o2n.rearrange("p (dc t) -> p dc t", t=FT),
                    deps=[o_sil], tag=f"fgs{f}")
        ffn_prev = [o_sil, d_g, o_x2b, ev2s[-1]]
        g_last = d_g
        yield

    import os
    PH = os.environ.get("KPHASES", "all")
    if PH == "prep":
        P.op("y", lambda y: y.nop(), deps=[cc_ua, cc_db, cc_w1, cc_w2],
             tag="end_prep")
        with allow_ici_all_to_all():
            P.emit(nc, stack)
        return nc, stack, len(P.all)

    g_last = None
    NCH = int(os.environ.get("KNCHUNK", "32"))

    def ffn_source():
        for f in range(NFT):
            yield from emit_ffn_tile(f)

    gen = ffn_source() if PH != "scan" else iter(())
    n_ffn_yields = 4 * (8 + 3 + 8 + 1)
    n_scan_yields = NCH * 9
    ratio = float(os.environ.get("KILV", "1")) * n_ffn_yields / max(n_scan_yields, 1)
    acc = 0.0
    for c in range(NCH):
        for _ in emit_scan_chunk(c):
            acc += ratio
            while acc >= 1.0:
                next(gen, None)
                acc -= 1.0
    for _ in gen:
        pass

    cc_sa = None
    if sdma_s:
        cc_sa = P.collective(lambda g: g.collective_compute(
            "AllToAll", ALU.bypass, replica_groups=[list(range(NCORES))],
            ins=[sa_in.ap().opt()], outs=[sa_out.ap().opt()]),
            deps=sdma_s, tag="cc_sa")

    if PH in ("scan", "ffn"):
        enddeps = ([cc_sa] if sdma_s else scan_tail[-4:]) + \
                  ([g_last] if g_last else [])
        P.op("y", lambda y: y.nop(), deps=enddeps, tag="end_part")
        with allow_ici_all_to_all():
            P.emit(nc, stack)
        return nc, stack, len(P.all)

    out_dmas = []
    prev_fin = [cc_sa, g_last] + ffn_prev
    for f in range(NFT):
        t0 = f * FT
        d_s = P.dma("y", "fin", fs.rearrange("p (dc t) -> p dc t", t=FT),
                    sa_out.rearrange("(dc p) t -> p dc t", p=128)[
                        :, :, t0:t0 + FT], deps=prev_fin, tag=f"zs{f}")
        d_g2 = P.dma("y", "fin", fg.rearrange("p (dc t) -> p dc t", t=FT),
                    gsp.rearrange("p (dc t) -> p dc t", t=TOKC)[:, :, t0:t0 + FT],
                    deps=prev_fin, tag=f"zg{f}", slot=1)
        d_x = P.dma("y", "fin", fxt.rearrange("p (tt d) -> p tt d", d=D),
                    x_in[t0:t0 + FT, :].rearrange("(tt p) d -> p tt d", p=128),
                    deps=prev_fin, tag=f"zx{f}", slot=2)
        o_sg = P.op("v", lambda v: v.tensor_mul(fsg[:, :], fs[:, :], fg[:, :]),
                    deps=[d_s, d_g2], tag=f"zsg{f}")
        trevs = []
        for dc in range(16):
            for tt in range(4):
                j = dc * 4 + tt
                bk = j % 2
                tr = P.op("t", lambda t, dc=dc, tt=tt, bk=bk: t.transpose(
                    pbb[bk][:, 0:128],
                    fsg[:, dc * FT + tt * 128:dc * FT + tt * 128 + 128],
                    identb[:, :]), deps=bank_use(bk, [o_sg, c_idb]),
                    tag=f"ztr{f}_{j}")
                ev = P.op("s", lambda s, dc=dc, tt=tt, bk=bk: s.activation(
                    fsgtm[:, tt * D + dc * 128:tt * D + (dc + 1) * 128],
                    pbb[bk][:, 0:128], AF.Copy), deps=[tr], tag=f"zte{f}_{j}")
                bank_release(bk, [ev])
                trevs.append(ev)
        o_out = P.op("v", lambda v: v.tensor_add(fxt[:, :], fxt[:, :],
                                                 fsgtm[:, :]),
                     deps=[trevs[-1], d_x], tag=f"zout{f}")
        d_o = P.dma("y", "out",
                    out_ext[t0:t0 + FT, :].rearrange("(tt p) d -> p tt d", p=128),
                    fxt.rearrange("p (tt d) -> p tt d", d=D),
                    deps=[o_out], tag=f"zo{f}")
        out_dmas.append(d_o)
        prev_fin = [o_out, d_o, o_sg]

    P.op("y", lambda y: y.nop(), deps=out_dmas, tag="end")

    with allow_ici_all_to_all():
        P.emit(nc, stack)
    n_ops = len(P.all)
    return nc, stack, n_ops


_CACHE = {}


def kernel(L0_attn_input, L0_attn_W_dt_in, L0_attn_W_dt_out, L0_attn_b_dt_out,
           L0_attn_W_B, L0_attn_W_C, L0_attn_D, L0_attn_A_log,
           L0_ffn_w_fc, L0_ffn_w_proj):
    import ml_dtypes
    x = np.ascontiguousarray(np.asarray(L0_attn_input, np.float32)
                             .reshape(TTOT, D))
    w_fc = np.ascontiguousarray(np.asarray(L0_ffn_w_fc, np.float32))
    w_proj = np.ascontiguousarray(np.asarray(L0_ffn_w_proj, np.float32))
    wdtin = np.asarray(L0_attn_W_dt_in, np.float32)
    wdtout = np.asarray(L0_attn_W_dt_out, np.float32)
    bdt = np.asarray(L0_attn_b_dt_out, np.float32)
    wB = np.asarray(L0_attn_W_B, np.float32)
    wC = np.asarray(L0_attn_W_C, np.float32)
    Dv = np.asarray(L0_attn_D, np.float32)
    alog = np.asarray(L0_attn_A_log, np.float32)

    wbct = np.ascontiguousarray(np.concatenate([wdtin, wB, wC], axis=0).T)
    ident = np.eye(128, dtype=np.float32)
    identb = ident.astype(ml_dtypes.bfloat16)
    seln = np.zeros((NST, NST * 128), dtype=ml_dtypes.bfloat16)
    for n in range(NST):
        seln[n, n * 128:(n + 1) * 128] = 1.0
    onesb = np.ones((128, 1), dtype=ml_dtypes.bfloat16)
    ones1 = np.ones((1, 128), dtype=np.float32)
    fconst = np.tile(np.array([[EPS1, EPS3, -0.5, 1.0]], np.float32), (128, 1))

    if "nc" not in _CACHE:
        _CACHE["nc"] = build_program()
    nc, _stack, _n = _CACHE["nc"]
    in_maps = make_in_maps(x, w_fc, w_proj, wbct, wdtout, bdt, Dv, alog,
                           ident, identb, seln, onesb, ones1, fconst)
    with allow_ici_all_to_all():
        res = run_bass_kernel_spmd(nc, in_maps, core_ids=list(range(NCORES)))
    out = np.concatenate([r["out"] for r in res.results], axis=0)
    return out.reshape(2, TTOT // 2, D)


def make_in_maps(x, w_fc, w_proj, wbct, wdtout, bdt, Dv, alog,
                 ident, identb, seln, onesb, ones1, fconst):
    in_maps = []
    for k in range(NCORES):
        hs = slice(HSL * k, HSL * (k + 1))
        in_maps.append(dict(
            x=np.ascontiguousarray(x[TOKC * k:TOKC * (k + 1)]),
            w1s=np.ascontiguousarray(w_fc[(HID // NCORES) * k:
                                          (HID // NCORES) * (k + 1)]),
            w2s=np.ascontiguousarray(w_proj[(D // NCORES) * k:
                                            (D // NCORES) * (k + 1)]),
            wbct=wbct,
            wdtoutT=np.ascontiguousarray(wdtout[hs, :].T),
            bdts=np.ascontiguousarray(bdt[hs].reshape(2, 128).T),
            dsl=np.ascontiguousarray(Dv[hs].reshape(2, 128).T),
            alogs=np.ascontiguousarray(
                alog[hs, :].reshape(2, 128, NST).transpose(1, 0, 2)
                .reshape(128, 2 * NST)),
            ident=ident, identb=identb, seln=seln, onesb=onesb, ones1=ones1,
            fconst=fconst,
        ))
    return in_maps


# version 05
# speedup vs baseline: 1.6052x; 1.5535x over previous
import numpy as np
from contextlib import ExitStack

import concourse.bass as bass
import concourse.mybir as mybir
from concourse.bass_utils import run_bass_kernel_spmd
from concourse.aot_env import allow_ici_all_to_all

F32 = mybir.dt.float32
BF16 = mybir.dt.bfloat16
AF = mybir.ActivationFunctionType
ALU = mybir.AluOpType
AX = mybir.AxisListType

NCORES = 8
D = 2048
HID = 8192
TOKC = 2048
TTOT = 16384
HSL = 256
NST = 16
RNK = 64
SC = 512
NSC = TTOT // SC
FT = 512
NFT = TOKC // FT
PT = 256
NPT = TOKC // PT

EPS1 = 1e-5
EPS2 = 1e-6
EPS3 = 1e-6
QEPS = 1e-6

GAMMA_SCALE = None


class Op:
    __slots__ = ("eng", "fn", "deps", "sem_key", "inc", "need_inc", "tag")

    def __init__(self, eng, fn, deps, sem_key, inc, tag):
        self.eng, self.fn, self.deps = eng, fn, deps
        self.sem_key, self.inc, self.tag = sem_key, inc, tag
        self.need_inc = False


class Prog:
    ENGINES = ("v", "s", "t", "g", "y")

    def __init__(self):
        self.streams = {e: [] for e in self.ENGINES}
        self.all = []
        self._ncc = 0

    def op(self, eng, fn, deps=(), tag=""):
        o = Op(eng, fn, [d for d in deps if d is not None], eng, 1, tag)
        self.streams[eng].append(o)
        self.all.append(o)
        return o

    def dma(self, eng, stream, out, in_, deps=(), tag="", slot=0):
        def fn(E, out=out, in_=in_):
            return E.dma_start(out=out, in_=in_)
        o = Op(eng, fn, [d for d in deps if d is not None],
               "q" + eng + stream + str(slot), 16, tag)
        self.streams[eng].append(o)
        self.all.append(o)
        return o

    def collective(self, fn, deps=(), tag="cc"):
        key = "cc%d" % self._ncc
        self._ncc += 1
        o = Op("g", fn, [d for d in deps if d is not None], key, 1, tag)
        self.streams["g"].append(o)
        self.all.append(o)
        return o

    def finalize(self):
        import os
        strict = os.environ.get("KSTRICT", "0") == "1"
        pos = {}
        for e in self.ENGINES:
            for i, o in enumerate(self.streams[e]):
                pos[id(o)] = (e, i)
        if strict:
            for e in self.ENGINES:
                st = self.streams[e]
                for i in range(1, len(st)):
                    st[i].deps.append(st[i - 1])
        else:
            for o in self.all:
                oe, oi = pos[id(o)]
                if oe in ("t", "y"):
                    o.deps = [d for d in o.deps
                              if not (d.sem_key == d.eng == oe
                                      and pos[id(d)][1] < oi)]
        for o in self.all:
            for d in o.deps:
                d.need_inc = True
        for o in self.all:
            if o.inc == 16 or o.sem_key.startswith("cc"):
                o.need_inc = True
        counters = {}
        self.val = {}
        for e in self.ENGINES:
            for o in self.streams[e]:
                if o.need_inc:
                    counters[o.sem_key] = counters.get(o.sem_key, 0) + o.inc
                    self.val[o] = counters[o.sem_key]
        self.sem_keys = sorted(counters)

    def check(self):
        ptr = {e: 0 for e in self.ENGINES}
        sem = {k: 0 for k in self.sem_keys}
        done, total, progress = 0, len(self.all), True
        while progress:
            progress = False
            for e in self.ENGINES:
                while ptr[e] < len(self.streams[e]):
                    o = self.streams[e][ptr[e]]
                    if all(sem[d.sem_key] >= self.val[d] for d in o.deps):
                        if o.need_inc:
                            sem[o.sem_key] += o.inc
                        ptr[e] += 1
                        done += 1
                        progress = True
                    else:
                        break
        if done != total:
            stuck = []
            for e in self.ENGINES:
                if ptr[e] < len(self.streams[e]):
                    o = self.streams[e][ptr[e]]
                    why = [(d.tag, d.eng, self.val[d], sem[d.sem_key])
                           for d in o.deps if sem[d.sem_key] < self.val[d]]
                    stuck.append((e, o.tag, why[:2]))
            raise RuntimeError(f"deadlock {done}/{total}: {stuck}")

    def emit(self, nc, stack):
        self.finalize()
        self.check()
        sems = {k: stack.enter_context(nc.semaphore("sm_" + k))
                for k in self.sem_keys}
        block = stack.enter_context(nc.Block())
        prog = self

        def section(ekey):
            def run(E):
                waited = {}
                for o in prog.streams[ekey]:
                    for d in o.deps:
                        v = prog.val[d]
                        if waited.get(d.sem_key, -1) < v:
                            E.wait_ge(sems[d.sem_key], v)
                            waited[d.sem_key] = v
                    inst = o.fn(E)
                    if o.need_inc:
                        inst.then_inc(sems[o.sem_key], o.inc)
            return run

        block.vector(section("v"))
        block.scalar(section("s"))
        block.tensor(section("t"))
        block.gpsimd(section("g"))
        block.sync(section("y"))


class Carver:
    def __init__(self, tensor, nbytes, start=0):
        self.t, self.nbytes, self.cur = tensor, nbytes, start

    def take(self, rows, cols, dt):
        esz = 4 if dt == F32 else 2
        start = (self.cur + 3) // 4 * 4
        self.cur = start + cols * esz
        assert self.cur <= self.nbytes, f"arena overflow {self.cur}>{self.nbytes}"
        if dt == F32:
            return self.t.ap()[0:rows, start // 4:start // 4 + cols]
        return self.t.ap().bitcast(BF16)[0:rows, start // 2:start // 2 + cols]


def build_program():
    nc = bass.Bass("TRN2", target_bir_lowering=False, debug=False)
    P = Prog()

    x_in = nc.dram_tensor("x", [TOKC, D], F32, kind="ExternalInput").ap()
    w1s_in = nc.dram_tensor("w1s", [HID // NCORES, D], F32, kind="ExternalInput").ap()
    w2s_in = nc.dram_tensor("w2s", [D // NCORES, HID], F32, kind="ExternalInput").ap()
    wbct_in = nc.dram_tensor("wbct", [D, 96], F32, kind="ExternalInput").ap()
    wdtoutT_in = nc.dram_tensor("wdtoutT", [RNK, HSL], F32, kind="ExternalInput").ap()
    bdts_in = nc.dram_tensor("bdts", [128, 2], F32, kind="ExternalInput").ap()
    dsl_in = nc.dram_tensor("dsl", [128, 2], F32, kind="ExternalInput").ap()
    alogs_in = nc.dram_tensor("alogs", [128, 32], F32, kind="ExternalInput").ap()
    ident_in = nc.dram_tensor("ident", [128, 128], F32, kind="ExternalInput").ap()
    identb_in = nc.dram_tensor("identb", [128, 128], BF16, kind="ExternalInput").ap()
    seln_in = nc.dram_tensor("seln", [NST, NST * 128], BF16, kind="ExternalInput").ap()
    onesb_in = nc.dram_tensor("onesb", [128, 1], BF16, kind="ExternalInput").ap()
    ones1_in = nc.dram_tensor("ones1", [1, 128], F32, kind="ExternalInput").ap()
    fconst_in = nc.dram_tensor("fconst", [128, 4], F32, kind="ExternalInput").ap()
    out_ext = nc.dram_tensor("out", [TOKC, D], F32, kind="ExternalOutput").ap()

    ar_in = nc.dram_tensor("ar_in", [1, 2], F32)
    ar_out = nc.dram_tensor("ar_out", [1, 2], F32)
    w1ag_in = nc.dram_tensor("w1ag_in", [D, HID // NCORES], BF16)
    w1ag_out = nc.dram_tensor("w1ag_out", [NCORES * D, HID // NCORES], BF16)
    w2ag_in = nc.dram_tensor("w2ag_in", [HID, D // NCORES], BF16)
    w2ag_out = nc.dram_tensor("w2ag_out", [NCORES * HID, D // NCORES], BF16)
    dbag_in = nc.dram_tensor("dbag_in", [96, TOKC], F32)
    dbag_out = nc.dram_tensor("dbag_out", [NCORES * 96, TOKC], F32)
    ua_in = nc.dram_tensor("ua_in", [D, TOKC], BF16)
    ua_out = nc.dram_tensor("ua_out", [D, TOKC], BF16)
    sa_in = nc.dram_tensor("sa_in", [D, TOKC], BF16)
    sa_out = nc.dram_tensor("sa_out", [D, TOKC], BF16)
    xh1sp = nc.dram_tensor("xh1sp", [128, 16 * TOKC], BF16)
    gsp = nc.dram_tensor("gsp", [128, 16 * TOKC], BF16)

    stack = ExitStack()

    def sbuf(name, shape, dt=F32):
        return stack.enter_context(nc.sbuf_tensor(name, shape, dt))

    wbct = sbuf("s_wbct", [128, 16 * 96])
    wdtoutT = sbuf("s_wdtoutT", [RNK, HSL])
    bdts = sbuf("s_bdts", [128, 2])
    dsl = sbuf("s_dsl", [128, 2])
    alogs = sbuf("s_alogs", [128, 32])
    aneg = sbuf("aneg", [128, 32])
    ident = sbuf("s_ident", [128, 128])
    identb = sbuf("s_identb", [128, 128], BF16)
    seln = sbuf("s_seln", [NST, NST * 128], BF16)
    onesb = sbuf("s_onesb", [128, 1], BF16)
    ones1 = sbuf("s_ones1", [1, 128])
    fconst = sbuf("s_fconst", [128, 4])
    gam = sbuf("gam", [128, 6])
    onescol = sbuf("s_onescol", [128, 1])
    grow = sbuf("grow", [1, 16])
    tpart = sbuf("tpart", [128, 16])
    spack = sbuf("spack", [128, 8])
    srow = sbuf("srow", [8, 128])
    sdl = sbuf("sdl", [RNK, SC])
    sbc = sbuf("sbc", [NST, 2 * SC])
    sbcb = sbuf("sbcb", [NST, 2 * SC], BF16)
    sdt = sbuf("sdt", [128, 2 * SC])
    sdta = sbuf("sdta", [128, 2 * SC])
    sdtb2 = sbuf("sdtb2", [128, SC])
    sdtu = sbuf("sdtu", [128, 2 * SC], BF16)
    suc = sbuf("suc", [128, 2 * SC], BF16)
    sab = [sbuf(f"sab{i}", [128, SC]) for i in range(2)]
    sbb = sbuf("sbb", [128, SC], BF16)
    sbrep = [sbuf(f"sbrep{i}", [128, SC], BF16) for i in range(4)]
    sxb = [sbuf(f"sxb{i}", [128, SC]) for i in range(2)]
    cx = [sbuf(f"cx{i}", [128, SC * NST], BF16) for i in range(2)]
    yred = sbuf("yred", [128, 2 * SC])
    sout = sbuf("sout", [128, 2 * SC], BF16)
    scarry = sbuf("scarry", [128, 32])
    zcol = sbuf("zcol", [128, 1])

    ARENA_BYTES = 122368
    arena_t = sbuf("arena", [128, ARENA_BYTES // 4])
    At = Carver(arena_t, ARENA_BYTES)
    twa = At.take(128, 4096, F32)
    twb = At.take(128, 4096, F32)
    twm1 = At.take(128, 2048, F32)
    twm2 = At.take(128, 2048, F32)
    twh = At.take(128, 4096, BF16)
    tstage = At.take(128, 32 * 128, BF16)
    Ap = Carver(arena_t, ARENA_BYTES, start=At.cur)
    xtm = Ap.take(128, 2 * D, F32)
    xfm = Ap.take(128, 16 * PT, F32)
    ubuf = Ap.take(128, 16 * PT, BF16)
    xh1b = Ap.take(128, 16 * PT, BF16)
    mrep = Ap.take(128, PT, F32)
    r1rep = Ap.take(128, PT, F32)
    rCrep = Ap.take(128, PT, F32)
    srow2 = Ap.take(1, 3 * PT, F32)
    stat6 = Ap.take(128, 24, F32)
    statv = Ap.take(128, 2 * 6, F32)
    sdtbc = Ap.take(96, PT, F32)
    Af = Carver(arena_t, ARENA_BYTES)
    g1 = Af.take(128, 64 * FT, BF16)
    xh1t = Af.take(128, 16 * FT, BF16)
    o2n = xh1t
    w1buf = [Af.take(128, 16 * 128, BF16) for _ in range(2)]
    w2buf = [Af.take(128, 16 * 128, BF16) for _ in range(2)]
    sqc = [Af.take(128, FT, BF16) for _ in range(2)]
    strow = Af.take(1, 6 * FT, F32)
    m3rep = Af.take(128, FT, BF16)
    r3rep = Af.take(128, FT, BF16)
    Az = Carver(arena_t, ARENA_BYTES)
    fs = Az.take(128, 16 * FT, BF16)
    fg = Az.take(128, 16 * FT, BF16)
    fsg = Az.take(128, 16 * FT, BF16)
    fsgtm = Az.take(128, 4 * D, BF16)
    fxt = Az.take(128, 4 * D, F32)

    pb = [stack.enter_context(nc.psum_tensor(f"pb{i}", [128, 512], F32))
          for i in range(8)]
    pbb = [p.ap().bitcast(BF16) for p in pb]
    bank_free = {i: [] for i in range(8)}

    def bank_use(i, deps):
        return list(deps) + bank_free[i]

    def bank_release(i, consumers):
        bank_free[i] = list(consumers)

    c_wbct = P.dma("y", "cst", wbct.ap().rearrange("p (c r) -> p c r", r=96),
                   wbct_in.rearrange("(c p) r -> p c r", p=128), tag="c_wbct")
    c_wdt = P.dma("y", "cst", wdtoutT[:, :], wdtoutT_in, deps=[c_wbct],
                  tag="c_wdt")
    c_bdt = P.dma("y", "cst", bdts[:, :], bdts_in, deps=[c_wdt], tag="c_bdt")
    c_dsl = P.dma("y", "cst", dsl[:, :], dsl_in, deps=[c_bdt], tag="c_dsl")
    c_alog = P.dma("y", "cst", alogs[:, :], alogs_in, deps=[c_dsl], tag="c_alog")
    c_id = P.dma("y", "cst", ident[:, :], ident_in, deps=[c_alog], tag="c_id")
    c_idb = P.dma("y", "cst", identb[:, :], identb_in, deps=[c_id], tag="c_idb")
    c_sel = P.dma("y", "cst", seln[:, :], seln_in, deps=[c_idb], tag="c_sel")
    c_onb = P.dma("y", "cst", onesb[:, :], onesb_in, deps=[c_sel], tag="c_onb")
    c_on1 = P.dma("y", "cst", ones1[:, :], ones1_in, deps=[c_onb], tag="c_on1")
    c_fc = P.dma("y", "cst", fconst[:, :], fconst_in, deps=[c_on1], tag="c_fc")

    o_zc = P.op("v", lambda v: v.memset(zcol[:, :], 0.0), tag="zcol")
    o_onec = P.op("v", lambda v: v.memset(onescol[:, :], 1.0), tag="onec")
    o_aexp = P.op("s", lambda s: s.activation(aneg[:, :], alogs[:, :], AF.Exp),
                  deps=[c_alog], tag="aexp")
    o_aneg = P.op("v", lambda v: v.tensor_scalar_mul(aneg[:, :], aneg[:, :], -1.0),
                  deps=[o_aexp], tag="aneg")

    tern_units = [(1, i, 0, D, w1s_in) for i in range(8)] + \
                 [(2, i, hh, 4096, w2s_in) for i in range(2) for hh in range(2)]

    red_ops = []
    for ui, (mat, i, hh, wid, wsrc) in enumerate(tern_units):
        dst = twa if ui % 2 == 0 else twb
        prev = red_ops[ui - 2] if ui >= 2 else None
        d_ = P.dma("y", "wld", dst[:, 0:wid],
                   wsrc[i * 128:(i + 1) * 128, hh * 4096:hh * 4096 + wid],
                   deps=[prev], tag=f"tld{ui}", slot=ui % 2)
        r = P.op("v", lambda v, dst=dst, wid=wid, ui=ui: v.reduce_sum(
            tpart[:, ui:ui + 1], dst[:, 0:wid], axis=AX.X,
            apply_absolute_value=True), deps=[d_], tag=f"tred{ui}")
        red_ops.append(r)

    o_gmm = P.op("t", lambda t: t.matmul(pb[7][0:1, 0:12], onescol[:, :],
                                         tpart[:, 0:12], start=True, stop=True),
                 deps=bank_use(7, [red_ops[-1], o_onec]), tag="gmm")
    o_gr1 = P.op("v", lambda v: v.reduce_sum(grow[0:1, 0:1], pb[7][0:1, 0:8],
                                             axis=AX.X), deps=[o_gmm], tag="gr1")
    o_gr2 = P.op("v", lambda v: v.reduce_sum(grow[0:1, 1:2], pb[7][0:1, 8:12],
                                             axis=AX.X), deps=[o_gmm], tag="gr2")
    bank_release(7, [o_gr1, o_gr2])
    d_gar = P.dma("y", "ccs", ar_in[0:1, :], grow[0:1, 0:2],
                  deps=[o_gr1, o_gr2], tag="gar_st")
    cc_ar = P.collective(lambda g: g.collective_compute(
        "AllReduce", ALU.add, replica_groups=[list(range(NCORES))],
        ins=[ar_in.ap().opt()], outs=[ar_out.ap().opt()]),
        deps=[d_gar], tag="cc_ar")
    d_gld = P.dma("y", "ccs", grow[0:1, 2:4], ar_out[0:1, :], deps=[cc_ar],
                  tag="gar_ld")
    o_gth = P.op("v", lambda v: v.tensor_scalar(
        grow[0:1, 4:6], grow[0:1, 2:4], 1.0 / (HID * D), QEPS, ALU.mult, ALU.add),
        deps=[d_gld], tag="gth")
    o_gth2 = P.op("v", lambda v: v.tensor_scalar_mul(
        grow[0:1, 4:6], grow[0:1, 4:6], 0.5), deps=[o_gth], tag="gth2")
    o_gthn = P.op("v", lambda v: v.tensor_scalar_mul(
        grow[0:1, 6:8], grow[0:1, 4:6], -1.0), deps=[o_gth2], tag="gthn")
    o_gm1 = P.op("v", lambda v: v.tensor_scalar_mul(
        grow[0:1, 8:10], grow[0:1, 2:4], 1.0 / (HID * D)), deps=[d_gld],
        tag="gm1")
    o_gbc = P.op("t", lambda t: t.matmul(pb[7][:, 0:6], ones1[0:1, :],
                                         grow[0:1, 4:10], start=True, stop=True),
                 deps=bank_use(7, [o_gthn, o_gm1]), tag="gbc")
    o_gbe = P.op("s", lambda s: s.activation(gam[:, 0:6], pb[7][:, 0:6], AF.Copy),
                 deps=[o_gbc], tag="gbe")
    bank_release(7, [o_gbe])

    tern_done = []
    w1_last = w2_last = None
    for ui, (mat, i, hh, wid, wsrc) in enumerate(tern_units):
        thp = gam[:, mat - 1:mat]
        thn = gam[:, mat + 1:mat + 2]
        dst = twa if ui % 2 == 0 else twb
        prev = tern_done[ui - 2] if ui >= 2 else red_ops[-1]
        d_ = P.dma("y", "wld", dst[:, 0:wid],
                   wsrc[i * 128:(i + 1) * 128, hh * 4096:hh * 4096 + wid],
                   deps=[prev, red_ops[-1]], tag=f"t2ld{ui}", slot=ui % 2)
        subs = wid // 2048
        mops = []
        for ss in range(subs):
            sl = slice(ss * 2048, (ss + 1) * 2048)
            o1 = P.op("v", lambda v, dst=dst, sl=sl, thp=thp: v.tensor_single_scalar(
                twm1[:, 0:2048], dst[:, sl], thp, ALU.is_gt),
                deps=[d_, o_gbe], tag=f"tm1_{ui}_{ss}")
            o2 = P.op("v", lambda v, dst=dst, sl=sl, thn=thn: v.tensor_single_scalar(
                twm2[:, 0:2048], dst[:, sl], thn, ALU.is_lt),
                deps=[d_, o_gbe], tag=f"tm2_{ui}_{ss}")
            o3 = P.op("v", lambda v, sl=sl: v.tensor_sub(
                twh[:, sl], twm1[:, 0:2048], twm2[:, 0:2048]),
                deps=[o1, o2], tag=f"tmh_{ui}_{ss}")
            mops.append(o3)
        nblk = wid // 128
        evs = []
        prev_dst = tern_done[-1] if tern_done else None
        for j in range(nblk):
            bk = j % 2
            tr = P.op("t", lambda t, j=j, bk=bk: t.transpose(
                pbb[bk][:, 0:128], twh[:, j * 128:(j + 1) * 128], identb[:, :]),
                deps=bank_use(bk, [mops[j // 16], c_idb]), tag=f"ttr{ui}_{j}")
            ev = P.op("s", lambda s, j=j, bk=bk: s.activation(
                tstage[:, j * 128:(j + 1) * 128], pbb[bk][:, 0:128], AF.Copy),
                deps=[tr, prev_dst], tag=f"tte{ui}_{j}")
            bank_release(bk, [ev])
            evs.append(ev)
        agdst = w1ag_in if mat == 1 else w2ag_in
        dd = agdst.rearrange("(j p) o -> p j o", p=128)
        d_st = P.dma("y", "wst",
                     dd[:, hh * 32:hh * 32 + nblk, i * 128:(i + 1) * 128],
                     tstage.rearrange("p (j o) -> p j o", o=128)[:, 0:nblk, :],
                     deps=evs, tag=f"tst{ui}")
        if mat == 1:
            w1_last = d_st
        else:
            w2_last = d_st
        tern_done.append(d_st)

    cc_w1 = P.collective(lambda g: g.collective_compute(
        "AllGather", ALU.bypass, replica_groups=[list(range(NCORES))],
        ins=[w1ag_in.ap().opt()], outs=[w1ag_out.ap().opt()]),
        deps=[w1_last], tag="cc_w1")
    cc_w2 = P.collective(lambda g: g.collective_compute(
        "AllGather", ALU.bypass, replica_groups=[list(range(NCORES))],
        ins=[w2ag_in.ap().opt()], outs=[w2ag_out.ap().opt()]),
        deps=[w2_last], tag="cc_w2")

    dbc_dmas, ua_dmas, xh_dmas = [], [], []
    prev_tile = []
    for pt in range(NPT):
        t0 = pt * PT
        d_x = P.dma("y", "xld", xtm.rearrange("p (tt d) -> p tt d", d=D),
                    x_in[t0:t0 + PT, :].rearrange("(tt p) d -> p tt d", p=128),
                    deps=prev_tile, tag=f"pxl{pt}")
        pk_all = []
        for tt in range(2):
            bns = []
            for cch in range(4):
                o = P.op("v", lambda v, tt=tt, cch=cch: v.bn_stats(
                    stat6[:, cch * 6:(cch + 1) * 6],
                    xtm[:, tt * D + cch * 512:tt * D + (cch + 1) * 512]),
                    deps=[d_x], tag=f"pbn{pt}_{tt}_{cch}")
                bns.append(o)
            o_ag = P.op("v", lambda v, tt=tt: v.bn_aggr(
                statv[:, tt * 6:tt * 6 + 2], stat6[:, 0:24]),
                deps=[bns[-1]], tag=f"pag{pt}_{tt}")
            l1 = P.op("s", lambda s, tt=tt: s.activation(
                statv[:, tt * 6 + 4:tt * 6 + 5], statv[:, tt * 6 + 1:tt * 6 + 2],
                AF.Ln, bias=fconst[:, 0:1]), deps=[o_ag, c_fc], tag=f"pl1{pt}_{tt}")
            e1 = P.op("s", lambda s, tt=tt: s.activation(
                statv[:, tt * 6 + 2:tt * 6 + 3], statv[:, tt * 6 + 4:tt * 6 + 5],
                AF.Exp, scale=fconst[:, 2:3]), deps=[l1], tag=f"pe1{pt}_{tt}")
            vv = P.op("v", lambda v, tt=tt: v.tensor_scalar(
                statv[:, tt * 6 + 5:tt * 6 + 6], statv[:, tt * 6 + 1:tt * 6 + 2],
                1.0 + EPS2, EPS1 * EPS2, ALU.mult, ALU.add),
                deps=[o_ag], tag=f"pvv{pt}_{tt}")
            lC = P.op("s", lambda s, tt=tt: s.activation(
                statv[:, tt * 6 + 5:tt * 6 + 6], statv[:, tt * 6 + 5:tt * 6 + 6],
                AF.Ln), deps=[vv], tag=f"plC{pt}_{tt}")
            eC = P.op("s", lambda s, tt=tt: s.activation(
                statv[:, tt * 6 + 3:tt * 6 + 4], statv[:, tt * 6 + 5:tt * 6 + 6],
                AF.Exp, scale=fconst[:, 2:3]), deps=[lC], tag=f"peC{pt}_{tt}")
            for j, col in ((0, 0), (1, 2), (2, 3)):
                o = P.op("v", lambda v, tt=tt, j=j, col=col: v.tensor_copy(
                    spack[:, tt * 3 + j:tt * 3 + j + 1],
                    statv[:, tt * 6 + col:tt * 6 + col + 1]),
                    deps=[e1, eC], tag=f"ppk{pt}_{tt}_{j}")
                pk_all.append(o)
        tr_s = P.op("t", lambda t: t.transpose(pb[7][0:6, 0:128], spack[:, 0:6],
                                               ident[:, :]),
                    deps=bank_use(7, pk_all + [c_id]), tag=f"ptr{pt}")
        ev_s = P.op("s", lambda s: s.activation(srow[0:6, :], pb[7][0:6, 0:128],
                                                AF.Copy), deps=[tr_s],
                    tag=f"pev{pt}")
        bank_release(7, [ev_s])
        d_rm = None
        for jj in range(3):
            for tt in range(2):
                d_rm = P.dma("y", "srw",
                             srow2[0:1, jj * PT + tt * 128:jj * PT + tt * 128 + 128],
                             srow[tt * 3 + jj:tt * 3 + jj + 1, :],
                             deps=[ev_s, d_rm], tag=f"prm{pt}_{jj}_{tt}")
        reps = []
        for j, dstrep in ((0, mrep), (1, r1rep), (2, rCrep)):
            bc = P.op("t", lambda t, j=j: t.matmul(
                pb[7][:, 0:PT], ones1[0:1, :], srow2[0:1, j * PT:(j + 1) * PT],
                start=True, stop=True), deps=bank_use(7, [d_rm]),
                tag=f"pbc{pt}_{j}")
            evr = P.op("s", lambda s, dstrep=dstrep: s.activation(
                dstrep[:, :], pb[7][:, 0:PT], AF.Copy), deps=[bc],
                tag=f"pbe{pt}_{j}")
            bank_release(7, [evr])
            reps.append(evr)
        trev = []
        for tt in range(2):
            for dc in range(16):
                j = tt * 16 + dc
                bk = 3 + j % 2
                tr = P.op("t", lambda t, tt=tt, dc=dc, bk=bk: t.transpose(
                    pb[bk][:, 0:128],
                    xtm[:, tt * D + dc * 128:tt * D + (dc + 1) * 128],
                    ident[:, :]), deps=bank_use(bk, [d_x, c_id]),
                    tag=f"pxt{pt}_{j}")
                ev = P.op("s", lambda s, tt=tt, dc=dc, bk=bk: s.activation(
                    xfm[:, dc * PT + tt * 128:dc * PT + tt * 128 + 128],
                    pb[bk][:, 0:128], AF.Copy), deps=[tr], tag=f"pxe{pt}_{j}")
                bank_release(bk, [ev])
                trev.append(ev)
        xv = xfm.rearrange("p (dc t) -> p dc t", t=PT)
        mv = mrep.rearrange("p (o t) -> p o t", o=1)
        xvb, mvb = bass.broadcast_tensor_aps(xv, mv)
        o_sub = P.op("v", lambda v: v.tensor_sub(xv, xvb, mvb),
                     deps=trev + [reps[0]], tag=f"psub{pt}")
        uv = ubuf.rearrange("p (dc t) -> p dc t", t=PT)
        r1v = r1rep.rearrange("p (o t) -> p o t", o=1)
        _, r1vb = bass.broadcast_tensor_aps(xv, r1v)
        o_u = P.op("v", lambda v: v.tensor_mul(uv, xvb, r1vb),
                   deps=[o_sub, reps[1]], tag=f"pu{pt}")
        xh1v = xh1b.rearrange("p (dc t) -> p dc t", t=PT)
        rCv = rCrep.rearrange("p (o t) -> p o t", o=1)
        _, rCvb = bass.broadcast_tensor_aps(xv, rCv)
        o_xh = P.op("v", lambda v: v.tensor_mul(xh1v, xvb, rCvb),
                    deps=[o_sub, reps[2]], tag=f"pxh{pt}")
        mms = []
        for dc in range(16):
            mm = P.op("t", lambda t, dc=dc: t.matmul(
                pb[5][0:96, 0:PT], wbct[:, dc * 96:(dc + 1) * 96],
                xfm[:, dc * PT:(dc + 1) * PT], start=(dc == 0), stop=(dc == 15)),
                deps=(bank_use(5, [o_sub, c_wbct]) if dc == 0 else [mms[-1]]),
                tag=f"pdb{pt}_{dc}")
            mms.append(mm)
        o_dbe = P.op("v", lambda v: v.tensor_mul(
            sdtbc[0:96, :], pb[5][0:96, 0:PT], r1rep[0:96, :]),
            deps=[mms[-1], reps[1]], tag=f"pdbe{pt}")
        bank_release(5, [o_dbe])
        d_db = P.dma("y", "dbc", dbag_in[0:96, t0:t0 + PT], sdtbc[0:96, :],
                     deps=[o_dbe], tag=f"pdbs{pt}")
        d_ua = P.dma("y", "uas",
                     ua_in.rearrange("(dc p) t -> p dc t", p=128)[:, :, t0:t0 + PT],
                     uv, deps=[o_u], tag=f"pua{pt}")
        d_xh = P.dma("y", "xsp",
                     xh1sp.rearrange("p (dc t) -> p dc t", t=TOKC)[:, :, t0:t0 + PT],
                     xh1v, deps=[o_xh], tag=f"pxs{pt}")
        dbc_dmas.append(d_db)
        ua_dmas.append(d_ua)
        xh_dmas.append(d_xh)
        prev_tile = [o_u, o_xh, o_dbe, d_ua, d_xh, d_db, mms[-1]]

    cc_db = P.collective(lambda g: g.collective_compute(
        "AllGather", ALU.bypass, replica_groups=[list(range(NCORES))],
        ins=[dbag_in.ap().opt()], outs=[dbag_out.ap().opt()]),
        deps=dbc_dmas, tag="cc_db")
    cc_ua = P.collective(lambda g: g.collective_compute(
        "AllToAll", ALU.bypass, replica_groups=[list(range(NCORES))],
        ins=[ua_in.ap().opt()], outs=[ua_out.ap().opt()]),
        deps=ua_dmas, tag="cc_ua")

    war = {
        "sdl": [cc_db], "sbc": [cc_db], "sbcb": [], "suc": [cc_ua],
        "sout": {0: None, 1: None},
        "scan_pp": {0: None, 1: None},
        "carry_pp": {0: None, 1: None},
        "b_pp": {0: None, 1: None},
        "c_pp": {0: None, 1: None},
        "pool": {0: None, 1: None},
        "aexp_last": None,
        "b_last": None,
    }
    prev_carry = {}
    sdma_s = []
    ffn_prev = []

    scan_tail = []

    def emit_scan_chunk(c):
        import os as _os
        SPART = int(_os.environ.get("KSCANPART", "5"))
        r, off = c // 4, (c % 4) * SC
        d_dl = P.dma("y", "sdl", sdl[:, :],
                     dbag_out[r * 96:r * 96 + RNK, off:off + SC],
                     deps=war["sdl"], tag=f"sdl{c}")
        d_b = P.dma("y", "sbc", sbc[:, 0:SC],
                    dbag_out[r * 96 + RNK:r * 96 + 80, off:off + SC],
                    deps=war["sbc"], tag=f"sBl{c}", slot=0)
        d_c = P.dma("y", "sbc", sbc[:, SC:2 * SC],
                    dbag_out[r * 96 + 80:r * 96 + 96, off:off + SC],
                    deps=war["sbc"], tag=f"sCl{c}", slot=1)
        d_u0 = P.dma("y", "sc2", suc[:, 0:SC],
                     ua_out[r * 256:r * 256 + 128, off:off + SC],
                     deps=war["suc"], tag=f"su0{c}", slot=0)
        d_u1 = P.dma("y", "sc2", suc[:, SC:2 * SC],
                     ua_out[r * 256 + 128:r * 256 + 256, off:off + SC],
                     deps=war["suc"], tag=f"su1{c}", slot=1)
        o_bc16 = P.op("v", lambda v: v.tensor_copy(sbcb[:, :], sbc[:, :]),
                      deps=[d_b, d_c] + war["sbcb"], tag=f"sbc16_{c}")
        dt_ops = []
        for m in range(2):
            sl = slice(m * SC, (m + 1) * SC)
            mm = P.op("t", lambda t, m=m: t.matmul(
                pb[7][:, 0:SC], wdtoutT[:, m * 128:(m + 1) * 128], sdl[:, :],
                start=True, stop=True),
                deps=bank_use(7, [d_dl, c_wdt]), tag=f"sdtm{c}_{m}")
            a1 = P.op("s", lambda s, sl=sl, m=m: s.activation(
                sdta[:, sl], pb[7][:, 0:SC], AF.Abs, bias=bdts[:, m:m + 1]),
                deps=[mm, c_bdt], tag=f"sab{c}_{m}")
            r1 = P.op("v", lambda v, m=m: v.tensor_scalar(
                sdtb2[:, :], pb[7][:, 0:SC], bdts[:, m:m + 1], 0.0,
                ALU.add, ALU.max), deps=[mm, c_bdt, a1], tag=f"srl{c}_{m}")
            bank_release(7, [a1, r1])
            e1 = P.op("s", lambda s, sl=sl: s.activation(
                sdta[:, sl], sdta[:, sl], AF.Exp, scale=-1.0),
                deps=[a1], tag=f"sex{c}_{m}")
            l1 = P.op("s", lambda s, sl=sl: s.activation(
                sdta[:, sl], sdta[:, sl], AF.Ln, bias=1.0),
                deps=[e1], tag=f"sln{c}_{m}")
            o_dt = P.op("v", lambda v, sl=sl: v.tensor_add(
                sdt[:, sl], sdta[:, sl], sdtb2[:, :]),
                deps=[l1, r1, war["aexp_last"]], tag=f"sdt{c}_{m}")
            o_du = P.op("v", lambda v, sl=sl: v.tensor_mul(
                sdtu[:, sl], sdt[:, sl], suc[:, sl]),
                deps=[o_dt, d_u0, d_u1], tag=f"sdu{c}_{m}")
            dt_ops.append((o_dt, o_du))
        yield
        if SPART <= 1:
            scan_tail.append(dt_ops[1][1])
            war["sdl"] = [dt_ops[0][0], dt_ops[1][0]]
            war["sbc"] = [o_bc16]
            war["suc"] = [dt_ops[0][1], dt_ops[1][1]]
            war["sbcb"] = [o_bc16]
            return
        war["sdl"] = [dt_ops[0][0], dt_ops[1][0]]
        war["sbc"] = [o_bc16]
        last_a = None
        last_b = None
        last_bc = None
        for m in range(2):
            sl = slice(m * SC, (m + 1) * SC)
            cmuls = []
            for n in range(NST):
                g = m * NST + n
                pp = g % 2
                bcB = P.op("t", lambda t, n=n: t.matmul(
                    pb[5][:, 0:SC], seln[:, n * 128:(n + 1) * 128],
                    sbcb[:, 0:SC], start=True, stop=True),
                    deps=bank_use(5, [o_bc16, c_sel]), tag=f"bB{c}_{g}")
                evB = P.op("s", lambda s, pp=pp: s.activation(
                    sbrep[pp][:, :], pb[5][:, 0:SC], AF.Copy),
                    deps=[bcB, war["b_pp"][pp]], tag=f"eB{c}_{g}")
                bank_release(5, [evB])
                bcC = P.op("t", lambda t, n=n: t.matmul(
                    pb[6][:, 0:SC], seln[:, n * 128:(n + 1) * 128],
                    sbcb[:, SC:2 * SC], start=True, stop=True),
                    deps=bank_use(6, [o_bc16, c_sel]), tag=f"bC{c}_{g}")
                evC = P.op("s", lambda s, pp=pp: s.activation(
                    sbrep[2 + pp][:, :], pb[6][:, 0:SC], AF.Copy),
                    deps=[bcC, war["c_pp"][pp]], tag=f"eC{c}_{g}")
                bank_release(6, [evC])
                last_bc = bcC
                if SPART <= 2:
                    evC = P.op("s", lambda s, pp=pp: s.activation(
                        sbrep[pp][:, :], pb[6][:, 0:SC], AF.Copy),
                        deps=[bcC, war["b_pp"][pp]], tag=f"eC{c}_{g}")
                    bank_release(6, [evC])
                    war["b_pp"][pp] = evC
                    scan_tail.append(evC)
                    continue
                o_a = P.op("s", lambda s, sl=sl, m=m, n=n, pp=pp: s.activation(
                    sab[pp][:, :], sdt[:, sl], AF.Exp,
                    scale=aneg[:, m * 16 + n:m * 16 + n + 1]),
                    deps=[dt_ops[m][0], o_aneg, war["scan_pp"][pp]],
                    tag=f"a{c}_{g}")
                last_a = o_a
                o_b = P.op("v", lambda v, sl=sl, pp=pp: v.tensor_mul(
                    sbb[:, :], sdtu[:, sl], sbrep[pp][:, :]),
                    deps=[dt_ops[m][1], evB], tag=f"b{c}_{g}")
                last_b = o_b
                war["b_pp"][pp] = o_b
                if c == 0 or c == 16:
                    init = zcol[:, 0:1]
                    ideps = [o_a, o_b, o_zc, war["carry_pp"][pp]]
                else:
                    init = scarry[:, g:g + 1]
                    ideps = [o_a, o_b, prev_carry.get((c - 1, g)),
                             war["carry_pp"][pp]]
                o_x = P.op("v", lambda v, pp=pp, init=init: v.tensor_tensor_scan(
                    sxb[pp][:, :], sab[pp][:, :], sbb[:, :], init,
                    ALU.mult, ALU.add), deps=ideps, tag=f"x{c}_{g}")
                war["scan_pp"][pp] = o_x
                o_cr = P.op("g", lambda gp, pp=pp, g=g: gp.tensor_copy(
                    scarry[:, g:g + 1], sxb[pp][:, SC - 1:SC]),
                    deps=[o_x], tag=f"cr{c}_{g}")
                prev_carry[(c, g)] = o_cr
                war["carry_pp"][pp] = o_cr
                if SPART <= 3:
                    war["scan_pp"][pp] = o_x
                    bank_release(6, [o_x])
                    scan_tail.append(o_cr)
                    continue
                o_cm = P.op("v", lambda v, pp=pp, n=n, m=m: v.tensor_mul(
                    cx[m][:, n * SC:(n + 1) * SC],
                    sxb[pp][:, :], sbrep[2 + pp][:, :]),
                    deps=[o_x, evC, war["pool"][m]], tag=f"cm{c}_{g}")
                war["c_pp"][pp] = o_cm
                cmuls.append(o_cm)
                if n % 4 == 3:
                    yield
            if SPART <= 4:
                war["pool"][m] = cmuls[-1]
                scan_tail.append(cmuls[-1])
                continue
            yv = yred[:, m * SC:(m + 1) * SC]
            a0 = P.op("g", lambda gp, yv=yv, m=m: gp.tensor_add(
                yv, cx[m][:, 0:SC], cx[m][:, SC:2 * SC]),
                deps=[cmuls[0], cmuls[1]], tag=f"yr{c}_{m}_0")
            last = a0
            for n in range(2, NST):
                last = P.op("g", lambda gp, yv=yv, m=m, n=n: gp.tensor_add(
                    yv, yv, cx[m][:, n * SC:(n + 1) * SC]),
                    deps=[last, cmuls[n]], tag=f"yr{c}_{m}_{n}")
            sl = slice(m * SC, (m + 1) * SC)
            o_st = P.op("v", lambda gp, sl=sl, m=m: gp.scalar_tensor_tensor(
                sout[:, sl], suc[:, sl], dsl[:, m:m + 1], yred[:, sl],
                ALU.mult, ALU.add),
                deps=[last, c_dsl, d_u0, d_u1, war["sout"][m]],
                tag=f"ys{c}_{m}")
            war["pool"][m] = o_st
            d_s = P.dma("y", "sst",
                        sa_in[r * 256 + m * 128:r * 256 + (m + 1) * 128,
                              off:off + SC],
                        sout[:, sl], deps=[o_st], tag=f"ss{c}_{m}", slot=m)
            war["sout"][m] = d_s
            sdma_s.append(d_s)
        war["aexp_last"] = last_a
        war["b_last"] = last_b
        war["suc"] = [dt_ops[0][1], dt_ops[1][1], war["pool"][0], war["pool"][1]]
        war["sbcb"] = [last_bc]

    def emit_ffn_tile(f):
        nonlocal ffn_prev, g_last
        t0 = f * FT
        d_xh = P.dma("y", "xht", xh1t.rearrange("p (dc t) -> p dc t", t=FT),
                     xh1sp.rearrange("p (dc t) -> p dc t", t=TOKC)[
                         :, :, t0:t0 + FT],
                     deps=xh_dmas + [cc_w1, cc_w2] + ffn_prev, tag=f"fxh{f}")
        w1v = w1ag_out.rearrange("(r dc p) o -> p r dc o", p=128, dc=16)
        evacs = []
        wprev = [None, None]
        for oc in range(64):
            r_, oc_ = oc // 8, oc % 8
            half = oc % 2
            d_w = P.dma("y", "w1s",
                        w1buf[half].rearrange("p (dc o) -> p dc o", o=128),
                        w1v[:, r_, :, oc_ * 128:(oc_ + 1) * 128],
                        deps=[wprev[half], cc_w1], tag=f"fw1{f}_{oc}", slot=half)
            mms = []
            for kb in range(16):
                dep = ([d_w, d_xh] + bank_use(half, [])) if kb == 0 else [mms[-1]]
                mm = P.op("t", lambda t, half=half, kb=kb: t.matmul(
                    pb[half][:, 0:FT], w1buf[half][:, kb * 128:(kb + 1) * 128],
                    xh1t[:, kb * FT:(kb + 1) * FT],
                    start=(kb == 0), stop=(kb == 15)), deps=dep,
                    tag=f"fm1{f}_{oc}_{kb}")
                mms.append(mm)
            wprev[half] = mms[-1]
            if oc % 8 == 7:
                yield
            ev = P.op("v", lambda v, oc=oc, half=half: v.tensor_scalar_mul(
                g1[:, oc * FT:(oc + 1) * FT], pb[half][:, 0:FT], gam[:, 4:5]),
                deps=[mms[-1], o_gbe], tag=f"fe1{f}_{oc}")
            bank_release(half, [ev])
            evacs.append(ev)
        o_gel = P.op("s", lambda s: s.activation(g1[:, :], g1[:, :], AF.Gelu),
                     deps=[evacs[-1]], tag=f"fgel{f}")
        summs = []
        for oc in range(64):
            dep = (bank_use(2, [o_gel, c_onb])) if oc == 0 else [summs[-1]]
            mm = P.op("t", lambda t, oc=oc: t.matmul(
                pb[2][0:1, 0:FT], onesb[:, 0:1], g1[:, oc * FT:(oc + 1) * FT],
                start=(oc == 0), stop=(oc == 63)), deps=dep, tag=f"fsm{f}_{oc}")
            summs.append(mm)
        yield
        ev_sum = P.op("v", lambda v: v.tensor_copy(strow[0:1, 0:FT],
                                                   pb[2][0:1, 0:FT]),
                      deps=[summs[-1], evacs[-1]], tag=f"fsev{f}")
        sqmms, sq_ops = [], []
        for oc in range(64):
            half = oc % 2
            o_sq = P.op("v", lambda v, oc=oc, half=half: v.tensor_mul(
                sqc[half][:, :], g1[:, oc * FT:(oc + 1) * FT],
                g1[:, oc * FT:(oc + 1) * FT]),
                deps=[ev_sum] if oc == 0 else ([sqmms[-2]] if oc >= 2 else []),
                tag=f"fsq{f}_{oc}")
            dep = [o_sq] + ([ev_sum] if oc == 0 else [sqmms[-1]])
            mm = P.op("t", lambda t, half=half, oc=oc: t.matmul(
                pb[2][0:1, 0:FT], onesb[:, 0:1], sqc[half][:, :],
                start=(oc == 0), stop=(oc == 63)), deps=dep, tag=f"fqm{f}_{oc}")
            sqmms.append(mm)
            sq_ops.append(o_sq)
        ev_sq = P.op("v", lambda v: v.tensor_copy(strow[0:1, FT:2 * FT],
                                                  pb[2][0:1, 0:FT]),
                     deps=[sqmms[-1]], tag=f"fqev{f}")
        bank_release(2, [ev_sq])
        o_m3 = P.op("v", lambda v: v.tensor_scalar_mul(
            strow[0:1, 2 * FT:3 * FT], strow[0:1, 0:FT], 1.0 / HID),
            deps=[ev_sum], tag=f"fm3{f}")
        o_s2 = P.op("s", lambda s: s.activation(
            strow[0:1, 3 * FT:4 * FT], strow[0:1, 2 * FT:3 * FT], AF.Square),
            deps=[o_m3], tag=f"fs2{f}")
        o_e2 = P.op("v", lambda v: v.tensor_scalar_mul(
            strow[0:1, 4 * FT:5 * FT], strow[0:1, FT:2 * FT], 1.0 / HID),
            deps=[ev_sq], tag=f"fe2{f}")
        o_var = P.op("v", lambda v: v.tensor_sub(
            strow[0:1, 3 * FT:4 * FT], strow[0:1, 4 * FT:5 * FT],
            strow[0:1, 3 * FT:4 * FT]), deps=[o_s2, o_e2], tag=f"fvar{f}")
        o_ln3 = P.op("s", lambda s: s.activation(
            strow[0:1, 4 * FT:5 * FT], strow[0:1, 3 * FT:4 * FT], AF.Ln,
            bias=fconst[0:1, 1:2]), deps=[o_var, c_fc], tag=f"fln{f}")
        o_r3 = P.op("s", lambda s: s.activation(
            strow[0:1, 5 * FT:6 * FT], strow[0:1, 4 * FT:5 * FT], AF.Exp,
            scale=fconst[0:1, 2:3]), deps=[o_ln3], tag=f"fr3{f}")
        bc_m = P.op("t", lambda t: t.matmul(
            pb[7][:, 0:FT], ones1[0:1, :], strow[0:1, 2 * FT:3 * FT],
            start=True, stop=True), deps=bank_use(7, [o_m3]), tag=f"fbm{f}")
        ev_m = P.op("s", lambda s: s.activation(m3rep[:, :], pb[7][:, 0:FT],
                                                AF.Copy),
                    deps=[bc_m] + ffn_prev, tag=f"fbme{f}")
        bank_release(7, [ev_m])
        bc_r = P.op("t", lambda t: t.matmul(
            pb[7][:, 0:FT], ones1[0:1, :], strow[0:1, 5 * FT:6 * FT],
            start=True, stop=True), deps=bank_use(7, [o_r3]), tag=f"fbr{f}")
        ev_r = P.op("s", lambda s: s.activation(r3rep[:, :], pb[7][:, 0:FT],
                                                AF.Copy),
                    deps=[bc_r] + ffn_prev, tag=f"fbre{f}")
        bank_release(7, [ev_r])
        g1v = g1.rearrange("p (oc t) -> p oc t", t=FT)
        m3v = m3rep.rearrange("p (o t) -> p o t", o=1)
        g1vb, m3vb = bass.broadcast_tensor_aps(g1v, m3v)
        o_x2a = P.op("v", lambda v: v.tensor_sub(g1v, g1vb, m3vb),
                     deps=[ev_m, ev_sq], tag=f"fx2a{f}")
        r3v = r3rep.rearrange("p (o t) -> p o t", o=1)
        _, r3vb = bass.broadcast_tensor_aps(g1v, r3v)
        o_x2b = P.op("v", lambda v: v.tensor_mul(g1v, g1vb, r3vb),
                     deps=[o_x2a, ev_r], tag=f"fx2b{f}")
        yield
        w2v = w2ag_out.rearrange("(r kk p) o -> p r kk o", p=128, kk=64)
        ev2s = []
        w2prev = [None, None]
        for oc2 in range(16):
            r_, oc_ = oc2 // 2, oc2 % 2
            bk = 3 + oc2 % 2
            dws = []
            mms = []
            for q in range(4):
                d_w = P.dma("y", "w2s",
                            w2buf[q % 2].rearrange("p (kk o) -> p kk o", o=128),
                            w2v[:, r_, q * 16:(q + 1) * 16,
                                oc_ * 128:(oc_ + 1) * 128],
                            deps=[w2prev[q % 2]] +
                                 ([cc_w2] if oc2 == 0 else []) +
                                 ([mms[(q - 2) * 16 + 15]] if q >= 2 else []),
                            tag=f"fw2{f}_{oc2}_{q}", slot=q % 2)
                dws.append(d_w)
                for kk in range(16):
                    kb = q * 16 + kk
                    if kb == 0:
                        dep = [d_w, o_x2b] + bank_use(bk, [])
                    elif kk == 0:
                        dep = [d_w, mms[-1]]
                    else:
                        dep = [mms[-1]]
                    mm = P.op("t", lambda t, bk=bk, kb=kb, q=q, kk=kk: t.matmul(
                        pb[bk][:, 0:FT],
                        w2buf[q % 2][:, kk * 128:(kk + 1) * 128],
                        g1[:, kb * FT:(kb + 1) * FT],
                        start=(kb == 0), stop=(kb == 63)), deps=dep,
                        tag=f"fm2{f}_{oc2}_{kb}")
                    mms.append(mm)
            w2prev = [mms[47], mms[63]]
            ev = P.op("v", lambda v, oc2=oc2, bk=bk: v.tensor_scalar_mul(
                o2n[:, oc2 * FT:(oc2 + 1) * FT], pb[bk][:, 0:FT], gam[:, 5:6]),
                deps=[mms[-1], o_gbe], tag=f"fe2{f}_{oc2}")
            bank_release(bk, [ev])
            ev2s.append(ev)
            if oc2 % 2 == 1:
                yield
        o_sil = P.op("s", lambda s: s.activation(o2n[:, :], o2n[:, :], AF.Silu),
                     deps=[ev2s[-1]], tag=f"fsil{f}")
        d_g = P.dma("y", "gsp",
                    gsp.rearrange("p (dc t) -> p dc t", t=TOKC)[:, :, t0:t0 + FT],
                    o2n.rearrange("p (dc t) -> p dc t", t=FT),
                    deps=[o_sil], tag=f"fgs{f}")
        ffn_prev = [o_sil, d_g, o_x2b, ev2s[-1]]
        g_last = d_g
        yield

    import os
    PH = os.environ.get("KPHASES", "all")
    if PH == "prep":
        P.op("y", lambda y: y.nop(), deps=[cc_ua, cc_db, cc_w1, cc_w2],
             tag="end_prep")
        with allow_ici_all_to_all():
            P.emit(nc, stack)
        return nc, stack, len(P.all)

    g_last = None
    NCH = int(os.environ.get("KNCHUNK", "32"))

    def ffn_source():
        for f in range(NFT):
            yield from emit_ffn_tile(f)

    gen = ffn_source() if PH != "scan" else iter(())
    n_ffn_yields = 4 * (8 + 3 + 8 + 1)
    n_scan_yields = NCH * 9
    ratio = float(os.environ.get("KILV", "1")) * n_ffn_yields / max(n_scan_yields, 1)
    acc = 0.0
    for c in range(NCH):
        for _ in emit_scan_chunk(c):
            acc += ratio
            while acc >= 1.0:
                next(gen, None)
                acc -= 1.0
    for _ in gen:
        pass

    cc_sa = None
    if sdma_s:
        cc_sa = P.collective(lambda g: g.collective_compute(
            "AllToAll", ALU.bypass, replica_groups=[list(range(NCORES))],
            ins=[sa_in.ap().opt()], outs=[sa_out.ap().opt()]),
            deps=sdma_s, tag="cc_sa")

    if PH in ("scan", "ffn"):
        enddeps = ([cc_sa] if sdma_s else scan_tail[-4:]) + \
                  ([g_last] if g_last else [])
        P.op("y", lambda y: y.nop(), deps=enddeps, tag="end_part")
        with allow_ici_all_to_all():
            P.emit(nc, stack)
        return nc, stack, len(P.all)

    out_dmas = []
    prev_fin = [cc_sa, g_last] + ffn_prev
    for f in range(NFT):
        t0 = f * FT
        d_s = P.dma("y", "fin", fs.rearrange("p (dc t) -> p dc t", t=FT),
                    sa_out.rearrange("(dc p) t -> p dc t", p=128)[
                        :, :, t0:t0 + FT], deps=prev_fin, tag=f"zs{f}")
        d_g2 = P.dma("y", "fin", fg.rearrange("p (dc t) -> p dc t", t=FT),
                    gsp.rearrange("p (dc t) -> p dc t", t=TOKC)[:, :, t0:t0 + FT],
                    deps=prev_fin, tag=f"zg{f}", slot=1)
        d_x = P.dma("y", "fin", fxt.rearrange("p (tt d) -> p tt d", d=D),
                    x_in[t0:t0 + FT, :].rearrange("(tt p) d -> p tt d", p=128),
                    deps=prev_fin, tag=f"zx{f}", slot=2)
        o_sg = P.op("v", lambda v: v.tensor_mul(fsg[:, :], fs[:, :], fg[:, :]),
                    deps=[d_s, d_g2], tag=f"zsg{f}")
        trevs = []
        for dc in range(16):
            for tt in range(4):
                j = dc * 4 + tt
                bk = j % 2
                tr = P.op("t", lambda t, dc=dc, tt=tt, bk=bk: t.transpose(
                    pbb[bk][:, 0:128],
                    fsg[:, dc * FT + tt * 128:dc * FT + tt * 128 + 128],
                    identb[:, :]), deps=bank_use(bk, [o_sg, c_idb]),
                    tag=f"ztr{f}_{j}")
                ev = P.op("s", lambda s, dc=dc, tt=tt, bk=bk: s.activation(
                    fsgtm[:, tt * D + dc * 128:tt * D + (dc + 1) * 128],
                    pbb[bk][:, 0:128], AF.Copy), deps=[tr], tag=f"zte{f}_{j}")
                bank_release(bk, [ev])
                trevs.append(ev)
        o_out = P.op("v", lambda v: v.tensor_add(fxt[:, :], fxt[:, :],
                                                 fsgtm[:, :]),
                     deps=[trevs[-1], d_x], tag=f"zout{f}")
        d_o = P.dma("y", "out",
                    out_ext[t0:t0 + FT, :].rearrange("(tt p) d -> p tt d", p=128),
                    fxt.rearrange("p (tt d) -> p tt d", d=D),
                    deps=[o_out], tag=f"zo{f}")
        out_dmas.append(d_o)
        prev_fin = [o_out, d_o, o_sg]

    P.op("y", lambda y: y.nop(), deps=out_dmas, tag="end")

    with allow_ici_all_to_all():
        P.emit(nc, stack)
    n_ops = len(P.all)
    return nc, stack, n_ops


_CACHE = {}


def kernel(L0_attn_input, L0_attn_W_dt_in, L0_attn_W_dt_out, L0_attn_b_dt_out,
           L0_attn_W_B, L0_attn_W_C, L0_attn_D, L0_attn_A_log,
           L0_ffn_w_fc, L0_ffn_w_proj):
    import ml_dtypes
    x = np.ascontiguousarray(np.asarray(L0_attn_input, np.float32)
                             .reshape(TTOT, D))
    w_fc = np.ascontiguousarray(np.asarray(L0_ffn_w_fc, np.float32))
    w_proj = np.ascontiguousarray(np.asarray(L0_ffn_w_proj, np.float32))
    wdtin = np.asarray(L0_attn_W_dt_in, np.float32)
    wdtout = np.asarray(L0_attn_W_dt_out, np.float32)
    bdt = np.asarray(L0_attn_b_dt_out, np.float32)
    wB = np.asarray(L0_attn_W_B, np.float32)
    wC = np.asarray(L0_attn_W_C, np.float32)
    Dv = np.asarray(L0_attn_D, np.float32)
    alog = np.asarray(L0_attn_A_log, np.float32)

    wbct = np.ascontiguousarray(np.concatenate([wdtin, wB, wC], axis=0).T)
    ident = np.eye(128, dtype=np.float32)
    identb = ident.astype(ml_dtypes.bfloat16)
    seln = np.zeros((NST, NST * 128), dtype=ml_dtypes.bfloat16)
    for n in range(NST):
        seln[n, n * 128:(n + 1) * 128] = 1.0
    onesb = np.ones((128, 1), dtype=ml_dtypes.bfloat16)
    ones1 = np.ones((1, 128), dtype=np.float32)
    fconst = np.tile(np.array([[EPS1, EPS3, -0.5, 1.0]], np.float32), (128, 1))

    if "nc" not in _CACHE:
        _CACHE["nc"] = build_program()
    nc, _stack, _n = _CACHE["nc"]
    in_maps = make_in_maps(x, w_fc, w_proj, wbct, wdtout, bdt, Dv, alog,
                           ident, identb, seln, onesb, ones1, fconst)
    with allow_ici_all_to_all():
        res = run_bass_kernel_spmd(nc, in_maps, core_ids=list(range(NCORES)))
    out = np.concatenate([r["out"] for r in res.results], axis=0)
    return out.reshape(2, TTOT // 2, D)


def make_in_maps(x, w_fc, w_proj, wbct, wdtout, bdt, Dv, alog,
                 ident, identb, seln, onesb, ones1, fconst):
    in_maps = []
    for k in range(NCORES):
        hs = slice(HSL * k, HSL * (k + 1))
        in_maps.append(dict(
            x=np.ascontiguousarray(x[TOKC * k:TOKC * (k + 1)]),
            w1s=np.ascontiguousarray(w_fc[(HID // NCORES) * k:
                                          (HID // NCORES) * (k + 1)]),
            w2s=np.ascontiguousarray(w_proj[(D // NCORES) * k:
                                            (D // NCORES) * (k + 1)]),
            wbct=wbct,
            wdtoutT=np.ascontiguousarray(wdtout[hs, :].T),
            bdts=np.ascontiguousarray(bdt[hs].reshape(2, 128).T),
            dsl=np.ascontiguousarray(Dv[hs].reshape(2, 128).T),
            alogs=np.ascontiguousarray(
                alog[hs, :].reshape(2, 128, NST).transpose(1, 0, 2)
                .reshape(128, 2 * NST)),
            ident=ident, identb=identb, seln=seln, onesb=onesb, ones1=ones1,
            fconst=fconst,
        ))
    return in_maps


# version 07
# speedup vs baseline: 1.6585x; 1.0332x over previous
import numpy as np
from contextlib import ExitStack

import concourse.bass as bass
import concourse.mybir as mybir
from concourse.bass_utils import run_bass_kernel_spmd
from concourse.aot_env import allow_ici_all_to_all

F32 = mybir.dt.float32
BF16 = mybir.dt.bfloat16
AF = mybir.ActivationFunctionType
ALU = mybir.AluOpType
AX = mybir.AxisListType

NCORES = 8
D = 2048
HID = 8192
TOKC = 2048
TTOT = 16384
HSL = 256
NST = 16
RNK = 64
SC = 512
NSC = TTOT // SC
FT = 512
NFT = TOKC // FT
PT = 256
NPT = TOKC // PT

EPS1 = 1e-5
EPS2 = 1e-6
EPS3 = 1e-6
QEPS = 1e-6

GAMMA_SCALE = None


class Op:
    __slots__ = ("eng", "fn", "deps", "sem_key", "inc", "need_inc", "tag")

    def __init__(self, eng, fn, deps, sem_key, inc, tag):
        self.eng, self.fn, self.deps = eng, fn, deps
        self.sem_key, self.inc, self.tag = sem_key, inc, tag
        self.need_inc = False


class Prog:
    ENGINES = ("v", "s", "t", "g", "y")

    def __init__(self):
        self.streams = {e: [] for e in self.ENGINES}
        self.all = []
        self._ncc = 0

    def op(self, eng, fn, deps=(), tag=""):
        o = Op(eng, fn, [d for d in deps if d is not None], eng, 1, tag)
        self.streams[eng].append(o)
        self.all.append(o)
        return o

    def dma(self, eng, stream, out, in_, deps=(), tag="", slot=0):
        def fn(E, out=out, in_=in_):
            return E.dma_start(out=out, in_=in_)
        o = Op(eng, fn, [d for d in deps if d is not None],
               "q" + eng + stream + str(slot), 16, tag)
        self.streams[eng].append(o)
        self.all.append(o)
        return o

    def collective(self, fn, deps=(), tag="cc"):
        key = "cc%d" % self._ncc
        self._ncc += 1
        o = Op("g", fn, [d for d in deps if d is not None], key, 1, tag)
        self.streams["g"].append(o)
        self.all.append(o)
        return o

    def finalize(self):
        import os
        strict = os.environ.get("KSTRICT", "0") == "1"
        pos = {}
        for e in self.ENGINES:
            for i, o in enumerate(self.streams[e]):
                pos[id(o)] = (e, i)
        if strict:
            for e in self.ENGINES:
                st = self.streams[e]
                for i in range(1, len(st)):
                    st[i].deps.append(st[i - 1])
        else:
            for o in self.all:
                oe, oi = pos[id(o)]
                if oe in ("t", "y"):
                    o.deps = [d for d in o.deps
                              if not (d.sem_key == d.eng == oe
                                      and pos[id(d)][1] < oi)]
        for o in self.all:
            for d in o.deps:
                d.need_inc = True
        for o in self.all:
            if o.inc == 16 or o.sem_key.startswith("cc"):
                o.need_inc = True
        counters = {}
        self.val = {}
        for e in self.ENGINES:
            for o in self.streams[e]:
                if o.need_inc:
                    counters[o.sem_key] = counters.get(o.sem_key, 0) + o.inc
                    self.val[o] = counters[o.sem_key]
        self.sem_keys = sorted(counters)

    def check(self):
        ptr = {e: 0 for e in self.ENGINES}
        sem = {k: 0 for k in self.sem_keys}
        done, total, progress = 0, len(self.all), True
        while progress:
            progress = False
            for e in self.ENGINES:
                while ptr[e] < len(self.streams[e]):
                    o = self.streams[e][ptr[e]]
                    if all(sem[d.sem_key] >= self.val[d] for d in o.deps):
                        if o.need_inc:
                            sem[o.sem_key] += o.inc
                        ptr[e] += 1
                        done += 1
                        progress = True
                    else:
                        break
        if done != total:
            stuck = []
            for e in self.ENGINES:
                if ptr[e] < len(self.streams[e]):
                    o = self.streams[e][ptr[e]]
                    why = [(d.tag, d.eng, self.val[d], sem[d.sem_key])
                           for d in o.deps if sem[d.sem_key] < self.val[d]]
                    stuck.append((e, o.tag, why[:2]))
            raise RuntimeError(f"deadlock {done}/{total}: {stuck}")

    def emit(self, nc, stack):
        self.finalize()
        self.check()
        sems = {k: stack.enter_context(nc.semaphore("sm_" + k))
                for k in self.sem_keys}
        block = stack.enter_context(nc.Block())
        prog = self

        def section(ekey):
            def run(E):
                waited = {}
                for o in prog.streams[ekey]:
                    for d in o.deps:
                        v = prog.val[d]
                        if waited.get(d.sem_key, -1) < v:
                            E.wait_ge(sems[d.sem_key], v)
                            waited[d.sem_key] = v
                    inst = o.fn(E)
                    if o.need_inc:
                        inst.then_inc(sems[o.sem_key], o.inc)
            return run

        block.vector(section("v"))
        block.scalar(section("s"))
        block.tensor(section("t"))
        block.gpsimd(section("g"))
        block.sync(section("y"))


class Carver:
    def __init__(self, tensor, nbytes, start=0):
        self.t, self.nbytes, self.cur = tensor, nbytes, start

    def take(self, rows, cols, dt):
        esz = 4 if dt == F32 else 2
        start = (self.cur + 3) // 4 * 4
        self.cur = start + cols * esz
        assert self.cur <= self.nbytes, f"arena overflow {self.cur}>{self.nbytes}"
        if dt == F32:
            return self.t.ap()[0:rows, start // 4:start // 4 + cols]
        return self.t.ap().bitcast(BF16)[0:rows, start // 2:start // 2 + cols]


def build_program():
    nc = bass.Bass("TRN2", target_bir_lowering=False, debug=False)
    P = Prog()

    x_in = nc.dram_tensor("x", [TOKC, D], F32, kind="ExternalInput").ap()
    w1s_in = nc.dram_tensor("w1s", [HID // NCORES, D], F32, kind="ExternalInput").ap()
    w2s_in = nc.dram_tensor("w2s", [D // NCORES, HID], F32, kind="ExternalInput").ap()
    wbct_in = nc.dram_tensor("wbct", [D, 96], F32, kind="ExternalInput").ap()
    wdtoutT_in = nc.dram_tensor("wdtoutT", [RNK, HSL], F32, kind="ExternalInput").ap()
    bdts_in = nc.dram_tensor("bdts", [128, 2], F32, kind="ExternalInput").ap()
    dsl_in = nc.dram_tensor("dsl", [128, 2], F32, kind="ExternalInput").ap()
    alogs_in = nc.dram_tensor("alogs", [128, 32], F32, kind="ExternalInput").ap()
    ident_in = nc.dram_tensor("ident", [128, 128], F32, kind="ExternalInput").ap()
    identb_in = nc.dram_tensor("identb", [128, 128], BF16, kind="ExternalInput").ap()
    seln_in = nc.dram_tensor("seln", [NST, NST * 128], BF16, kind="ExternalInput").ap()
    onesb_in = nc.dram_tensor("onesb", [128, 1], BF16, kind="ExternalInput").ap()
    ones1_in = nc.dram_tensor("ones1", [1, 128], F32, kind="ExternalInput").ap()
    fconst_in = nc.dram_tensor("fconst", [128, 4], F32, kind="ExternalInput").ap()
    out_ext = nc.dram_tensor("out", [TOKC, D], F32, kind="ExternalOutput").ap()

    ar_in = nc.dram_tensor("ar_in", [1, 2], F32)
    ar_out = nc.dram_tensor("ar_out", [1, 2], F32)
    w1ag_in = nc.dram_tensor("w1ag_in", [D, HID // NCORES], BF16)
    w1ag_out = nc.dram_tensor("w1ag_out", [NCORES * D, HID // NCORES], BF16)
    w2ag_in = nc.dram_tensor("w2ag_in", [HID, D // NCORES], BF16)
    w2ag_out = nc.dram_tensor("w2ag_out", [NCORES * HID, D // NCORES], BF16)
    dbag_in = nc.dram_tensor("dbag_in", [96, TOKC], F32)
    dbag_out = nc.dram_tensor("dbag_out", [NCORES * 96, TOKC], F32)
    ua_in = nc.dram_tensor("ua_in", [D, TOKC], BF16)
    ua_out = nc.dram_tensor("ua_out", [D, TOKC], BF16)
    sa_in = nc.dram_tensor("sa_in", [D, TOKC], BF16)
    sa_out = nc.dram_tensor("sa_out", [D, TOKC], BF16)
    xh1sp = nc.dram_tensor("xh1sp", [128, 16 * TOKC], BF16)
    gsp = nc.dram_tensor("gsp", [128, 16 * TOKC], BF16)

    stack = ExitStack()

    def sbuf(name, shape, dt=F32):
        return stack.enter_context(nc.sbuf_tensor(name, shape, dt))

    wbct = sbuf("s_wbct", [128, 16 * 96])
    wdtoutT = sbuf("s_wdtoutT", [RNK, HSL])
    bdts = sbuf("s_bdts", [128, 2])
    dsl = sbuf("s_dsl", [128, 2])
    alogs = sbuf("s_alogs", [128, 32])
    aneg = sbuf("aneg", [128, 32])
    ident = sbuf("s_ident", [128, 128])
    identb = sbuf("s_identb", [128, 128], BF16)
    seln = sbuf("s_seln", [NST, NST * 128], BF16)
    onesb = sbuf("s_onesb", [128, 1], BF16)
    ones1 = sbuf("s_ones1", [1, 128])
    fconst = sbuf("s_fconst", [128, 4])
    gam = sbuf("gam", [128, 6])
    onescol = sbuf("s_onescol", [128, 1])
    grow = sbuf("grow", [1, 16])
    tpart = sbuf("tpart", [128, 16])
    spack = sbuf("spack", [128, 8])
    srow = sbuf("srow", [8, 128])
    sdl = sbuf("sdl", [RNK, SC])
    sbc = sbuf("sbc", [NST, 2 * SC])
    sbcb = sbuf("sbcb", [NST, 2 * SC], BF16)
    sdt = sbuf("sdt", [128, 2 * SC])
    sdta = sbuf("sdta", [128, 2 * SC])
    sdtb2 = sbuf("sdtb2", [128, SC])
    sdtu = sbuf("sdtu", [128, 2 * SC], BF16)
    suc = sbuf("suc", [128, 2 * SC], BF16)
    sab = [sbuf(f"sab{i}", [128, SC]) for i in range(2)]
    sbb = sbuf("sbb", [128, SC], BF16)
    sbrep = [sbuf(f"sbrep{i}", [128, SC], BF16) for i in range(4)]
    sxb = [sbuf(f"sxb{i}", [128, SC]) for i in range(2)]
    cx = [sbuf(f"cx{i}", [128, SC * NST], BF16) for i in range(2)]
    yred = sbuf("yred", [128, 2 * SC])
    sout = sbuf("sout", [128, 2 * SC], BF16)
    scarry = sbuf("scarry", [128, 32])
    zcol = sbuf("zcol", [128, 1])

    ARENA_BYTES = 122368
    arena_t = sbuf("arena", [128, ARENA_BYTES // 4])
    At = Carver(arena_t, ARENA_BYTES)
    twa = At.take(128, 4096, F32)
    twb = At.take(128, 4096, F32)
    twm1 = At.take(128, 2048, F32)
    twm2 = At.take(128, 2048, F32)
    twh = At.take(128, 4096, BF16)
    tstage = At.take(128, 32 * 128, BF16)
    Ap = Carver(arena_t, ARENA_BYTES, start=At.cur)
    xtm = Ap.take(128, 2 * D, F32)
    xfm = Ap.take(128, 16 * PT, F32)
    ubuf = Ap.take(128, 16 * PT, BF16)
    xh1b = Ap.take(128, 16 * PT, BF16)
    mrep = Ap.take(128, PT, F32)
    r1rep = Ap.take(128, PT, F32)
    rCrep = Ap.take(128, PT, F32)
    srow2 = Ap.take(1, 3 * PT, F32)
    stat6 = Ap.take(128, 24, F32)
    statv = Ap.take(128, 2 * 6, F32)
    sdtbc = Ap.take(96, PT, F32)
    Af = Carver(arena_t, ARENA_BYTES)
    g1 = Af.take(128, 64 * FT, BF16)
    xh1t = Af.take(128, 16 * FT, BF16)
    o2n = xh1t
    w1buf = [Af.take(128, 16 * 128, BF16) for _ in range(2)]
    w2buf = [Af.take(128, 16 * 128, BF16) for _ in range(2)]
    sqc = [Af.take(128, FT, BF16) for _ in range(2)]
    strow = Af.take(1, 6 * FT, F32)
    m3rep = Af.take(128, FT, BF16)
    r3rep = Af.take(128, FT, BF16)
    Az = Carver(arena_t, ARENA_BYTES)
    fs = Az.take(128, 16 * FT, BF16)
    fg = Az.take(128, 16 * FT, BF16)
    fsg = Az.take(128, 16 * FT, BF16)
    fsgtm = Az.take(128, 4 * D, BF16)
    fxt = Az.take(128, 4 * D, F32)

    pb = [stack.enter_context(nc.psum_tensor(f"pb{i}", [128, 512], F32))
          for i in range(8)]
    pbb = [p.ap().bitcast(BF16) for p in pb]
    bank_free = {i: [] for i in range(8)}

    def bank_use(i, deps):
        return list(deps) + bank_free[i]

    def bank_release(i, consumers):
        bank_free[i] = list(consumers)

    c_wbct = P.dma("y", "cst", wbct.ap().rearrange("p (c r) -> p c r", r=96),
                   wbct_in.rearrange("(c p) r -> p c r", p=128), tag="c_wbct")
    c_wdt = P.dma("y", "cst", wdtoutT[:, :], wdtoutT_in, deps=[c_wbct],
                  tag="c_wdt")
    c_bdt = P.dma("y", "cst", bdts[:, :], bdts_in, deps=[c_wdt], tag="c_bdt")
    c_dsl = P.dma("y", "cst", dsl[:, :], dsl_in, deps=[c_bdt], tag="c_dsl")
    c_alog = P.dma("y", "cst", alogs[:, :], alogs_in, deps=[c_dsl], tag="c_alog")
    c_id = P.dma("y", "cst", ident[:, :], ident_in, deps=[c_alog], tag="c_id")
    c_idb = P.dma("y", "cst", identb[:, :], identb_in, deps=[c_id], tag="c_idb")
    c_sel = P.dma("y", "cst", seln[:, :], seln_in, deps=[c_idb], tag="c_sel")
    c_onb = P.dma("y", "cst", onesb[:, :], onesb_in, deps=[c_sel], tag="c_onb")
    c_on1 = P.dma("y", "cst", ones1[:, :], ones1_in, deps=[c_onb], tag="c_on1")
    c_fc = P.dma("y", "cst", fconst[:, :], fconst_in, deps=[c_on1], tag="c_fc")

    o_zc = P.op("v", lambda v: v.memset(zcol[:, :], 0.0), tag="zcol")
    o_onec = P.op("v", lambda v: v.memset(onescol[:, :], 1.0), tag="onec")
    o_aexp = P.op("s", lambda s: s.activation(aneg[:, :], alogs[:, :], AF.Exp),
                  deps=[c_alog], tag="aexp")
    o_aneg = P.op("v", lambda v: v.tensor_scalar_mul(aneg[:, :], aneg[:, :], -1.0),
                  deps=[o_aexp], tag="aneg")

    tern_units = [(1, i, 0, D, w1s_in) for i in range(8)] + \
                 [(2, i, hh, 4096, w2s_in) for i in range(2) for hh in range(2)]

    red_ops = []
    for ui, (mat, i, hh, wid, wsrc) in enumerate(tern_units):
        dst = twa if ui % 2 == 0 else twb
        prev = red_ops[ui - 2] if ui >= 2 else None
        d_ = P.dma("y", "wld", dst[:, 0:wid],
                   wsrc[i * 128:(i + 1) * 128, hh * 4096:hh * 4096 + wid],
                   deps=[prev], tag=f"tld{ui}", slot=ui % 2)
        r = P.op("v", lambda v, dst=dst, wid=wid, ui=ui: v.reduce_sum(
            tpart[:, ui:ui + 1], dst[:, 0:wid], axis=AX.X,
            apply_absolute_value=True), deps=[d_], tag=f"tred{ui}")
        red_ops.append(r)

    o_gmm = P.op("t", lambda t: t.matmul(pb[7][0:1, 0:12], onescol[:, :],
                                         tpart[:, 0:12], start=True, stop=True),
                 deps=bank_use(7, [red_ops[-1], o_onec]), tag="gmm")
    o_gr1 = P.op("v", lambda v: v.reduce_sum(grow[0:1, 0:1], pb[7][0:1, 0:8],
                                             axis=AX.X), deps=[o_gmm], tag="gr1")
    o_gr2 = P.op("v", lambda v: v.reduce_sum(grow[0:1, 1:2], pb[7][0:1, 8:12],
                                             axis=AX.X), deps=[o_gmm], tag="gr2")
    bank_release(7, [o_gr1, o_gr2])
    d_gar = P.dma("y", "ccs", ar_in[0:1, :], grow[0:1, 0:2],
                  deps=[o_gr1, o_gr2], tag="gar_st")
    cc_ar = P.collective(lambda g: g.collective_compute(
        "AllReduce", ALU.add, replica_groups=[list(range(NCORES))],
        ins=[ar_in.ap().opt()], outs=[ar_out.ap().opt()]),
        deps=[d_gar], tag="cc_ar")
    d_gld = P.dma("y", "ccs", grow[0:1, 2:4], ar_out[0:1, :], deps=[cc_ar],
                  tag="gar_ld")
    o_gth = P.op("v", lambda v: v.tensor_scalar(
        grow[0:1, 4:6], grow[0:1, 2:4], 1.0 / (HID * D), QEPS, ALU.mult, ALU.add),
        deps=[d_gld], tag="gth")
    o_gth2 = P.op("v", lambda v: v.tensor_scalar_mul(
        grow[0:1, 4:6], grow[0:1, 4:6], 0.5), deps=[o_gth], tag="gth2")
    o_gthn = P.op("v", lambda v: v.tensor_scalar_mul(
        grow[0:1, 6:8], grow[0:1, 4:6], -1.0), deps=[o_gth2], tag="gthn")
    o_gm1 = P.op("v", lambda v: v.tensor_scalar_mul(
        grow[0:1, 8:10], grow[0:1, 2:4], 1.0 / (HID * D)), deps=[d_gld],
        tag="gm1")
    o_gbc = P.op("t", lambda t: t.matmul(pb[7][:, 0:6], ones1[0:1, :],
                                         grow[0:1, 4:10], start=True, stop=True),
                 deps=bank_use(7, [o_gthn, o_gm1]), tag="gbc")
    o_gbe = P.op("s", lambda s: s.activation(gam[:, 0:6], pb[7][:, 0:6], AF.Copy),
                 deps=[o_gbc], tag="gbe")
    bank_release(7, [o_gbe])

    tern_done = []
    w1_last = w2_last = None
    for ui, (mat, i, hh, wid, wsrc) in enumerate(tern_units):
        thp = gam[:, mat - 1:mat]
        thn = gam[:, mat + 1:mat + 2]
        dst = twa if ui % 2 == 0 else twb
        prev = tern_done[ui - 2] if ui >= 2 else red_ops[-1]
        d_ = P.dma("y", "wld", dst[:, 0:wid],
                   wsrc[i * 128:(i + 1) * 128, hh * 4096:hh * 4096 + wid],
                   deps=[prev, red_ops[-1]], tag=f"t2ld{ui}", slot=ui % 2)
        subs = wid // 2048
        mops = []
        for ss in range(subs):
            sl = slice(ss * 2048, (ss + 1) * 2048)
            o1 = P.op("v", lambda v, dst=dst, sl=sl, thp=thp: v.tensor_single_scalar(
                twm1[:, 0:2048], dst[:, sl], thp, ALU.is_gt),
                deps=[d_, o_gbe], tag=f"tm1_{ui}_{ss}")
            o2 = P.op("v", lambda v, dst=dst, sl=sl, thn=thn: v.tensor_single_scalar(
                twm2[:, 0:2048], dst[:, sl], thn, ALU.is_lt),
                deps=[d_, o_gbe], tag=f"tm2_{ui}_{ss}")
            o3 = P.op("v", lambda v, sl=sl: v.tensor_sub(
                twh[:, sl], twm1[:, 0:2048], twm2[:, 0:2048]),
                deps=[o1, o2], tag=f"tmh_{ui}_{ss}")
            mops.append(o3)
        nblk = wid // 128
        evs = []
        prev_dst = tern_done[-1] if tern_done else None
        for j in range(nblk):
            bk = j % 2
            tr = P.op("t", lambda t, j=j, bk=bk: t.transpose(
                pbb[bk][:, 0:128], twh[:, j * 128:(j + 1) * 128], identb[:, :]),
                deps=bank_use(bk, [mops[j // 16], c_idb]), tag=f"ttr{ui}_{j}")
            ev = P.op("s", lambda s, j=j, bk=bk: s.activation(
                tstage[:, j * 128:(j + 1) * 128], pbb[bk][:, 0:128], AF.Copy),
                deps=[tr, prev_dst], tag=f"tte{ui}_{j}")
            bank_release(bk, [ev])
            evs.append(ev)
        agdst = w1ag_in if mat == 1 else w2ag_in
        dd = agdst.rearrange("(j p) o -> p j o", p=128)
        d_st = P.dma("y", "wst",
                     dd[:, hh * 32:hh * 32 + nblk, i * 128:(i + 1) * 128],
                     tstage.rearrange("p (j o) -> p j o", o=128)[:, 0:nblk, :],
                     deps=evs, tag=f"tst{ui}")
        if mat == 1:
            w1_last = d_st
        else:
            w2_last = d_st
        tern_done.append(d_st)

    cc_w1 = P.collective(lambda g: g.collective_compute(
        "AllGather", ALU.bypass, replica_groups=[list(range(NCORES))],
        ins=[w1ag_in.ap().opt()], outs=[w1ag_out.ap().opt()]),
        deps=[w1_last], tag="cc_w1")
    cc_w2 = P.collective(lambda g: g.collective_compute(
        "AllGather", ALU.bypass, replica_groups=[list(range(NCORES))],
        ins=[w2ag_in.ap().opt()], outs=[w2ag_out.ap().opt()]),
        deps=[w2_last], tag="cc_w2")

    dbc_dmas, ua_dmas, xh_dmas = [], [], []
    prev_tile = []
    for pt in range(NPT):
        t0 = pt * PT
        d_x = P.dma("y", "xld", xtm.rearrange("p (tt d) -> p tt d", d=D),
                    x_in[t0:t0 + PT, :].rearrange("(tt p) d -> p tt d", p=128),
                    deps=prev_tile, tag=f"pxl{pt}")
        pk_all = []
        for tt in range(2):
            bns = []
            for cch in range(4):
                o = P.op("v", lambda v, tt=tt, cch=cch: v.bn_stats(
                    stat6[:, cch * 6:(cch + 1) * 6],
                    xtm[:, tt * D + cch * 512:tt * D + (cch + 1) * 512]),
                    deps=[d_x], tag=f"pbn{pt}_{tt}_{cch}")
                bns.append(o)
            o_ag = P.op("v", lambda v, tt=tt: v.bn_aggr(
                statv[:, tt * 6:tt * 6 + 2], stat6[:, 0:24]),
                deps=[bns[-1]], tag=f"pag{pt}_{tt}")
            l1 = P.op("s", lambda s, tt=tt: s.activation(
                statv[:, tt * 6 + 4:tt * 6 + 5], statv[:, tt * 6 + 1:tt * 6 + 2],
                AF.Ln, bias=fconst[:, 0:1]), deps=[o_ag, c_fc], tag=f"pl1{pt}_{tt}")
            e1 = P.op("s", lambda s, tt=tt: s.activation(
                statv[:, tt * 6 + 2:tt * 6 + 3], statv[:, tt * 6 + 4:tt * 6 + 5],
                AF.Exp, scale=fconst[:, 2:3]), deps=[l1], tag=f"pe1{pt}_{tt}")
            vv = P.op("v", lambda v, tt=tt: v.tensor_scalar(
                statv[:, tt * 6 + 5:tt * 6 + 6], statv[:, tt * 6 + 1:tt * 6 + 2],
                1.0 + EPS2, EPS1 * EPS2, ALU.mult, ALU.add),
                deps=[o_ag], tag=f"pvv{pt}_{tt}")
            lC = P.op("s", lambda s, tt=tt: s.activation(
                statv[:, tt * 6 + 5:tt * 6 + 6], statv[:, tt * 6 + 5:tt * 6 + 6],
                AF.Ln), deps=[vv], tag=f"plC{pt}_{tt}")
            eC = P.op("s", lambda s, tt=tt: s.activation(
                statv[:, tt * 6 + 3:tt * 6 + 4], statv[:, tt * 6 + 5:tt * 6 + 6],
                AF.Exp, scale=fconst[:, 2:3]), deps=[lC], tag=f"peC{pt}_{tt}")
            for j, col in ((0, 0), (1, 2), (2, 3)):
                o = P.op("v", lambda v, tt=tt, j=j, col=col: v.tensor_copy(
                    spack[:, tt * 3 + j:tt * 3 + j + 1],
                    statv[:, tt * 6 + col:tt * 6 + col + 1]),
                    deps=[e1, eC], tag=f"ppk{pt}_{tt}_{j}")
                pk_all.append(o)
        tr_s = P.op("t", lambda t: t.transpose(pb[7][0:6, 0:128], spack[:, 0:6],
                                               ident[:, :]),
                    deps=bank_use(7, pk_all + [c_id]), tag=f"ptr{pt}")
        ev_s = P.op("s", lambda s: s.activation(srow[0:6, :], pb[7][0:6, 0:128],
                                                AF.Copy), deps=[tr_s],
                    tag=f"pev{pt}")
        bank_release(7, [ev_s])
        d_rm = None
        for jj in range(3):
            for tt in range(2):
                d_rm = P.dma("y", "srw",
                             srow2[0:1, jj * PT + tt * 128:jj * PT + tt * 128 + 128],
                             srow[tt * 3 + jj:tt * 3 + jj + 1, :],
                             deps=[ev_s, d_rm], tag=f"prm{pt}_{jj}_{tt}")
        reps = []
        for j, dstrep in ((0, mrep), (1, r1rep), (2, rCrep)):
            bc = P.op("t", lambda t, j=j: t.matmul(
                pb[7][:, 0:PT], ones1[0:1, :], srow2[0:1, j * PT:(j + 1) * PT],
                start=True, stop=True), deps=bank_use(7, [d_rm]),
                tag=f"pbc{pt}_{j}")
            evr = P.op("s", lambda s, dstrep=dstrep: s.activation(
                dstrep[:, :], pb[7][:, 0:PT], AF.Copy), deps=[bc],
                tag=f"pbe{pt}_{j}")
            bank_release(7, [evr])
            reps.append(evr)
        trev = []
        for tt in range(2):
            for dc in range(16):
                j = tt * 16 + dc
                bk = 3 + j % 2
                tr = P.op("t", lambda t, tt=tt, dc=dc, bk=bk: t.transpose(
                    pb[bk][:, 0:128],
                    xtm[:, tt * D + dc * 128:tt * D + (dc + 1) * 128],
                    ident[:, :]), deps=bank_use(bk, [d_x, c_id]),
                    tag=f"pxt{pt}_{j}")
                ev = P.op("s", lambda s, tt=tt, dc=dc, bk=bk: s.activation(
                    xfm[:, dc * PT + tt * 128:dc * PT + tt * 128 + 128],
                    pb[bk][:, 0:128], AF.Copy), deps=[tr], tag=f"pxe{pt}_{j}")
                bank_release(bk, [ev])
                trev.append(ev)
        xv = xfm.rearrange("p (dc t) -> p dc t", t=PT)
        mv = mrep.rearrange("p (o t) -> p o t", o=1)
        xvb, mvb = bass.broadcast_tensor_aps(xv, mv)
        o_sub = P.op("v", lambda v: v.tensor_sub(xv, xvb, mvb),
                     deps=trev + [reps[0]], tag=f"psub{pt}")
        uv = ubuf.rearrange("p (dc t) -> p dc t", t=PT)
        r1v = r1rep.rearrange("p (o t) -> p o t", o=1)
        _, r1vb = bass.broadcast_tensor_aps(xv, r1v)
        o_u = P.op("v", lambda v: v.tensor_mul(uv, xvb, r1vb),
                   deps=[o_sub, reps[1]], tag=f"pu{pt}")
        xh1v = xh1b.rearrange("p (dc t) -> p dc t", t=PT)
        rCv = rCrep.rearrange("p (o t) -> p o t", o=1)
        _, rCvb = bass.broadcast_tensor_aps(xv, rCv)
        o_xh = P.op("v", lambda v: v.tensor_mul(xh1v, xvb, rCvb),
                    deps=[o_sub, reps[2]], tag=f"pxh{pt}")
        mms = []
        for dc in range(16):
            mm = P.op("t", lambda t, dc=dc: t.matmul(
                pb[5][0:96, 0:PT], wbct[:, dc * 96:(dc + 1) * 96],
                xfm[:, dc * PT:(dc + 1) * PT], start=(dc == 0), stop=(dc == 15)),
                deps=(bank_use(5, [o_sub, c_wbct]) if dc == 0 else [mms[-1]]),
                tag=f"pdb{pt}_{dc}")
            mms.append(mm)
        o_dbe = P.op("v", lambda v: v.tensor_mul(
            sdtbc[0:96, :], pb[5][0:96, 0:PT], r1rep[0:96, :]),
            deps=[mms[-1], reps[1]], tag=f"pdbe{pt}")
        bank_release(5, [o_dbe])
        d_db = P.dma("y", "dbc", dbag_in[0:96, t0:t0 + PT], sdtbc[0:96, :],
                     deps=[o_dbe], tag=f"pdbs{pt}")
        d_ua = P.dma("y", "uas",
                     ua_in.rearrange("(dc p) t -> p dc t", p=128)[:, :, t0:t0 + PT],
                     uv, deps=[o_u], tag=f"pua{pt}")
        d_xh = P.dma("y", "xsp",
                     xh1sp.rearrange("p (dc t) -> p dc t", t=TOKC)[:, :, t0:t0 + PT],
                     xh1v, deps=[o_xh], tag=f"pxs{pt}")
        dbc_dmas.append(d_db)
        ua_dmas.append(d_ua)
        xh_dmas.append(d_xh)
        prev_tile = [o_u, o_xh, o_dbe, d_ua, d_xh, d_db, mms[-1]]

    cc_db = P.collective(lambda g: g.collective_compute(
        "AllGather", ALU.bypass, replica_groups=[list(range(NCORES))],
        ins=[dbag_in.ap().opt()], outs=[dbag_out.ap().opt()]),
        deps=dbc_dmas, tag="cc_db")
    cc_ua = P.collective(lambda g: g.collective_compute(
        "AllToAll", ALU.bypass, replica_groups=[list(range(NCORES))],
        ins=[ua_in.ap().opt()], outs=[ua_out.ap().opt()]),
        deps=ua_dmas, tag="cc_ua")

    war = {
        "sdl": [cc_db], "sbc": [cc_db], "sbcb": [], "suc": [cc_ua],
        "sout": {0: None, 1: None},
        "scan_pp": {0: None, 1: None},
        "carry_pp": {0: None, 1: None},
        "b_pp": {0: None, 1: None},
        "c_pp": {0: None, 1: None},
        "pool": {0: None, 1: None},
        "aexp_last": None,
        "b_last": None,
    }
    prev_carry = {}
    sdma_s = []
    ffn_prev = []

    scan_tail = []

    def emit_scan_chunk(c):
        import os as _os
        SPART = int(_os.environ.get("KSCANPART", "5"))
        r, off = c // 4, (c % 4) * SC
        d_dl = P.dma("y", "sdl", sdl[:, :],
                     dbag_out[r * 96:r * 96 + RNK, off:off + SC],
                     deps=war["sdl"], tag=f"sdl{c}")
        d_b = P.dma("y", "sbc", sbc[:, 0:SC],
                    dbag_out[r * 96 + RNK:r * 96 + 80, off:off + SC],
                    deps=war["sbc"], tag=f"sBl{c}", slot=0)
        d_c = P.dma("y", "sbc", sbc[:, SC:2 * SC],
                    dbag_out[r * 96 + 80:r * 96 + 96, off:off + SC],
                    deps=war["sbc"], tag=f"sCl{c}", slot=1)
        d_u0 = P.dma("y", "sc2", suc[:, 0:SC],
                     ua_out[r * 256:r * 256 + 128, off:off + SC],
                     deps=war["suc"], tag=f"su0{c}", slot=0)
        d_u1 = P.dma("y", "sc2", suc[:, SC:2 * SC],
                     ua_out[r * 256 + 128:r * 256 + 256, off:off + SC],
                     deps=war["suc"], tag=f"su1{c}", slot=1)
        o_bc16 = P.op("v", lambda v: v.tensor_copy(sbcb[:, :], sbc[:, :]),
                      deps=[d_b, d_c] + war["sbcb"], tag=f"sbc16_{c}")
        dt_ops = []
        for m in range(2):
            sl = slice(m * SC, (m + 1) * SC)
            mm = P.op("t", lambda t, m=m: t.matmul(
                pb[7][:, 0:SC], wdtoutT[:, m * 128:(m + 1) * 128], sdl[:, :],
                start=True, stop=True),
                deps=bank_use(7, [d_dl, c_wdt]), tag=f"sdtm{c}_{m}")
            a1 = P.op("s", lambda s, sl=sl, m=m: s.activation(
                sdta[:, sl], pb[7][:, 0:SC], AF.Abs, bias=bdts[:, m:m + 1]),
                deps=[mm, c_bdt], tag=f"sab{c}_{m}")
            r1 = P.op("v", lambda v, m=m: v.tensor_scalar(
                sdtb2[:, :], pb[7][:, 0:SC], bdts[:, m:m + 1], 0.0,
                ALU.add, ALU.max), deps=[mm, c_bdt, a1], tag=f"srl{c}_{m}")
            bank_release(7, [a1, r1])
            e1 = P.op("s", lambda s, sl=sl: s.activation(
                sdta[:, sl], sdta[:, sl], AF.Exp, scale=-1.0),
                deps=[a1], tag=f"sex{c}_{m}")
            l1 = P.op("s", lambda s, sl=sl: s.activation(
                sdta[:, sl], sdta[:, sl], AF.Ln, bias=1.0),
                deps=[e1], tag=f"sln{c}_{m}")
            o_dt = P.op("v", lambda v, sl=sl: v.tensor_add(
                sdt[:, sl], sdta[:, sl], sdtb2[:, :]),
                deps=[l1, r1, war["aexp_last"]], tag=f"sdt{c}_{m}")
            o_du = P.op("v", lambda v, sl=sl: v.tensor_mul(
                sdtu[:, sl], sdt[:, sl], suc[:, sl]),
                deps=[o_dt, d_u0, d_u1], tag=f"sdu{c}_{m}")
            dt_ops.append((o_dt, o_du))
        yield
        if SPART <= 1:
            scan_tail.append(dt_ops[1][1])
            war["sdl"] = [dt_ops[0][0], dt_ops[1][0]]
            war["sbc"] = [o_bc16]
            war["suc"] = [dt_ops[0][1], dt_ops[1][1]]
            war["sbcb"] = [o_bc16]
            return
        war["sdl"] = [dt_ops[0][0], dt_ops[1][0]]
        war["sbc"] = [o_bc16]
        last_a = None
        last_b = None
        last_bc = None
        cmuls = {0: [], 1: []}
        it = 0
        for n in range(NST):
            ppn = n % 2
            bcB = P.op("t", lambda t, n=n: t.matmul(
                pb[5][:, 0:SC], seln[:, n * 128:(n + 1) * 128],
                sbcb[:, 0:SC], start=True, stop=True),
                deps=bank_use(5, [o_bc16, c_sel]), tag=f"bB{c}_{n}")
            evB = P.op("s", lambda s, ppn=ppn: s.activation(
                sbrep[ppn][:, :], pb[5][:, 0:SC], AF.Copy),
                deps=[bcB] + (war["b_pp"][ppn] or []), tag=f"eB{c}_{n}")
            bank_release(5, [evB])
            bcC = P.op("t", lambda t, n=n: t.matmul(
                pb[6][:, 0:SC], seln[:, n * 128:(n + 1) * 128],
                sbcb[:, SC:2 * SC], start=True, stop=True),
                deps=bank_use(6, [o_bc16, c_sel]), tag=f"bC{c}_{n}")
            evC = P.op("s", lambda s, ppn=ppn: s.activation(
                sbrep[2 + ppn][:, :], pb[6][:, 0:SC], AF.Copy),
                deps=[bcC] + (war["c_pp"][ppn] or []), tag=f"eC{c}_{n}")
            bank_release(6, [evC])
            last_bc = bcC
            breaders, creaders = [], []
            for m in range(2):
                g = m * NST + n
                sl = slice(m * SC, (m + 1) * SC)
                pp = it % 2
                it += 1
                o_a = P.op("s", lambda s, sl=sl, m=m, n=n, pp=pp: s.activation(
                    sab[pp][:, :], sdt[:, sl], AF.Exp,
                    scale=aneg[:, m * 16 + n:m * 16 + n + 1]),
                    deps=[dt_ops[m][0], o_aneg, war["scan_pp"][pp]],
                    tag=f"a{c}_{g}")
                last_a = o_a
                o_b = P.op("v", lambda v, sl=sl, ppn=ppn: v.tensor_mul(
                    sbb[:, :], sdtu[:, sl], sbrep[ppn][:, :]),
                    deps=[dt_ops[m][1], evB], tag=f"b{c}_{g}")
                last_b = o_b
                breaders.append(o_b)
                if c == 0 or c == 16:
                    init = zcol[:, 0:1]
                    ideps = [o_a, o_b, o_zc, war["carry_pp"][pp]]
                else:
                    init = scarry[:, g:g + 1]
                    ideps = [o_a, o_b, prev_carry.get((c - 1, g)),
                             war["carry_pp"][pp]]
                o_x = P.op("v", lambda v, pp=pp, init=init: v.tensor_tensor_scan(
                    sxb[pp][:, :], sab[pp][:, :], sbb[:, :], init,
                    ALU.mult, ALU.add), deps=ideps, tag=f"x{c}_{g}")
                war["scan_pp"][pp] = o_x
                o_cr = P.op("g", lambda gp, pp=pp, g=g: gp.tensor_copy(
                    scarry[:, g:g + 1], sxb[pp][:, SC - 1:SC]),
                    deps=[o_x], tag=f"cr{c}_{g}")
                prev_carry[(c, g)] = o_cr
                war["carry_pp"][pp] = o_cr
                o_cm = P.op("v", lambda v, pp=pp, n=n, m=m, ppn=ppn: v.tensor_mul(
                    cx[m][:, n * SC:(n + 1) * SC],
                    sxb[pp][:, :], sbrep[2 + ppn][:, :]),
                    deps=[o_x, evC, war["pool"][m]], tag=f"cm{c}_{g}")
                creaders.append(o_cm)
                cmuls[m].append(o_cm)
            war["b_pp"][ppn] = breaders
            war["c_pp"][ppn] = creaders
            if n % 2 == 1:
                yield
        for m in range(2):
            sl = slice(m * SC, (m + 1) * SC)
            yv = yred[:, sl]
            a0 = P.op("g", lambda gp, yv=yv, m=m: gp.tensor_add(
                yv, cx[m][:, 0:SC], cx[m][:, SC:2 * SC]),
                deps=[cmuls[m][0], cmuls[m][1]], tag=f"yr{c}_{m}_0")
            last = a0
            for n in range(2, NST):
                last = P.op("g", lambda gp, yv=yv, m=m, n=n: gp.tensor_add(
                    yv, yv, cx[m][:, n * SC:(n + 1) * SC]),
                    deps=[last, cmuls[m][n]], tag=f"yr{c}_{m}_{n}")
            o_st = P.op("v", lambda gp, sl=sl, m=m: gp.scalar_tensor_tensor(
                sout[:, sl], suc[:, sl], dsl[:, m:m + 1], yred[:, sl],
                ALU.mult, ALU.add),
                deps=[last, c_dsl, d_u0, d_u1, war["sout"][m]],
                tag=f"ys{c}_{m}")
            war["pool"][m] = o_st
            d_s = P.dma("y", "sst",
                        sa_in[r * 256 + m * 128:r * 256 + (m + 1) * 128,
                              off:off + SC],
                        sout[:, sl], deps=[o_st], tag=f"ss{c}_{m}", slot=m)
            war["sout"][m] = d_s
            sdma_s.append(d_s)
        war["aexp_last"] = last_a
        war["b_last"] = last_b
        war["suc"] = [dt_ops[0][1], dt_ops[1][1], war["pool"][0], war["pool"][1]]
        war["sbcb"] = [last_bc]

    def emit_ffn_tile(f):
        nonlocal ffn_prev, g_last
        t0 = f * FT
        d_xh = P.dma("y", "xht", xh1t.rearrange("p (dc t) -> p dc t", t=FT),
                     xh1sp.rearrange("p (dc t) -> p dc t", t=TOKC)[
                         :, :, t0:t0 + FT],
                     deps=xh_dmas + [cc_w1, cc_w2] + ffn_prev, tag=f"fxh{f}")
        w1v = w1ag_out.rearrange("(r dc p) o -> p r dc o", p=128, dc=16)
        evacs = []
        wprev = [None, None]
        for oc in range(64):
            r_, oc_ = oc // 8, oc % 8
            half = oc % 2
            d_w = P.dma("y", "w1s",
                        w1buf[half].rearrange("p (dc o) -> p dc o", o=128),
                        w1v[:, r_, :, oc_ * 128:(oc_ + 1) * 128],
                        deps=[wprev[half], cc_w1], tag=f"fw1{f}_{oc}", slot=half)
            mms = []
            for kb in range(16):
                dep = ([d_w, d_xh] + bank_use(half, [])) if kb == 0 else [mms[-1]]
                mm = P.op("t", lambda t, half=half, kb=kb: t.matmul(
                    pb[half][:, 0:FT], w1buf[half][:, kb * 128:(kb + 1) * 128],
                    xh1t[:, kb * FT:(kb + 1) * FT],
                    start=(kb == 0), stop=(kb == 15)), deps=dep,
                    tag=f"fm1{f}_{oc}_{kb}")
                mms.append(mm)
            wprev[half] = mms[-1]
            if oc % 8 == 7:
                yield
            ev = P.op("v", lambda v, oc=oc, half=half: v.tensor_scalar_mul(
                g1[:, oc * FT:(oc + 1) * FT], pb[half][:, 0:FT], gam[:, 4:5]),
                deps=[mms[-1], o_gbe], tag=f"fe1{f}_{oc}")
            bank_release(half, [ev])
            evacs.append(ev)
        o_gel = P.op("s", lambda s: s.activation(g1[:, :], g1[:, :], AF.Gelu),
                     deps=[evacs[-1]], tag=f"fgel{f}")
        summs = []
        for oc in range(64):
            dep = (bank_use(2, [o_gel, c_onb])) if oc == 0 else [summs[-1]]
            mm = P.op("t", lambda t, oc=oc: t.matmul(
                pb[2][0:1, 0:FT], onesb[:, 0:1], g1[:, oc * FT:(oc + 1) * FT],
                start=(oc == 0), stop=(oc == 63)), deps=dep, tag=f"fsm{f}_{oc}")
            summs.append(mm)
        yield
        ev_sum = P.op("v", lambda v: v.tensor_copy(strow[0:1, 0:FT],
                                                   pb[2][0:1, 0:FT]),
                      deps=[summs[-1], evacs[-1]], tag=f"fsev{f}")
        sqmms, sq_ops = [], []
        for oc in range(64):
            half = oc % 2
            o_sq = P.op("v", lambda v, oc=oc, half=half: v.tensor_mul(
                sqc[half][:, :], g1[:, oc * FT:(oc + 1) * FT],
                g1[:, oc * FT:(oc + 1) * FT]),
                deps=[ev_sum] if oc == 0 else ([sqmms[-2]] if oc >= 2 else []),
                tag=f"fsq{f}_{oc}")
            dep = [o_sq] + ([ev_sum] if oc == 0 else [sqmms[-1]])
            mm = P.op("t", lambda t, half=half, oc=oc: t.matmul(
                pb[2][0:1, 0:FT], onesb[:, 0:1], sqc[half][:, :],
                start=(oc == 0), stop=(oc == 63)), deps=dep, tag=f"fqm{f}_{oc}")
            sqmms.append(mm)
            sq_ops.append(o_sq)
        ev_sq = P.op("v", lambda v: v.tensor_copy(strow[0:1, FT:2 * FT],
                                                  pb[2][0:1, 0:FT]),
                     deps=[sqmms[-1]], tag=f"fqev{f}")
        bank_release(2, [ev_sq])
        o_m3 = P.op("v", lambda v: v.tensor_scalar_mul(
            strow[0:1, 2 * FT:3 * FT], strow[0:1, 0:FT], 1.0 / HID),
            deps=[ev_sum], tag=f"fm3{f}")
        o_s2 = P.op("s", lambda s: s.activation(
            strow[0:1, 3 * FT:4 * FT], strow[0:1, 2 * FT:3 * FT], AF.Square),
            deps=[o_m3], tag=f"fs2{f}")
        o_e2 = P.op("v", lambda v: v.tensor_scalar_mul(
            strow[0:1, 4 * FT:5 * FT], strow[0:1, FT:2 * FT], 1.0 / HID),
            deps=[ev_sq], tag=f"fe2{f}")
        o_var = P.op("v", lambda v: v.tensor_sub(
            strow[0:1, 3 * FT:4 * FT], strow[0:1, 4 * FT:5 * FT],
            strow[0:1, 3 * FT:4 * FT]), deps=[o_s2, o_e2], tag=f"fvar{f}")
        o_ln3 = P.op("s", lambda s: s.activation(
            strow[0:1, 4 * FT:5 * FT], strow[0:1, 3 * FT:4 * FT], AF.Ln,
            bias=fconst[0:1, 1:2]), deps=[o_var, c_fc], tag=f"fln{f}")
        o_r3 = P.op("s", lambda s: s.activation(
            strow[0:1, 5 * FT:6 * FT], strow[0:1, 4 * FT:5 * FT], AF.Exp,
            scale=fconst[0:1, 2:3]), deps=[o_ln3], tag=f"fr3{f}")
        bc_m = P.op("t", lambda t: t.matmul(
            pb[7][:, 0:FT], ones1[0:1, :], strow[0:1, 2 * FT:3 * FT],
            start=True, stop=True), deps=bank_use(7, [o_m3]), tag=f"fbm{f}")
        ev_m = P.op("s", lambda s: s.activation(m3rep[:, :], pb[7][:, 0:FT],
                                                AF.Copy),
                    deps=[bc_m] + ffn_prev, tag=f"fbme{f}")
        bank_release(7, [ev_m])
        bc_r = P.op("t", lambda t: t.matmul(
            pb[7][:, 0:FT], ones1[0:1, :], strow[0:1, 5 * FT:6 * FT],
            start=True, stop=True), deps=bank_use(7, [o_r3]), tag=f"fbr{f}")
        ev_r = P.op("s", lambda s: s.activation(r3rep[:, :], pb[7][:, 0:FT],
                                                AF.Copy),
                    deps=[bc_r] + ffn_prev, tag=f"fbre{f}")
        bank_release(7, [ev_r])
        g1v = g1.rearrange("p (oc t) -> p oc t", t=FT)
        m3v = m3rep.rearrange("p (o t) -> p o t", o=1)
        g1vb, m3vb = bass.broadcast_tensor_aps(g1v, m3v)
        o_x2a = P.op("v", lambda v: v.tensor_sub(g1v, g1vb, m3vb),
                     deps=[ev_m, ev_sq], tag=f"fx2a{f}")
        r3v = r3rep.rearrange("p (o t) -> p o t", o=1)
        _, r3vb = bass.broadcast_tensor_aps(g1v, r3v)
        o_x2b = P.op("v", lambda v: v.tensor_mul(g1v, g1vb, r3vb),
                     deps=[o_x2a, ev_r], tag=f"fx2b{f}")
        yield
        w2v = w2ag_out.rearrange("(r kk p) o -> p r kk o", p=128, kk=64)
        ev2s = []
        w2prev = [None, None]
        for oc2 in range(16):
            r_, oc_ = oc2 // 2, oc2 % 2
            bk = 3 + oc2 % 2
            dws = []
            mms = []
            for q in range(4):
                d_w = P.dma("y", "w2s",
                            w2buf[q % 2].rearrange("p (kk o) -> p kk o", o=128),
                            w2v[:, r_, q * 16:(q + 1) * 16,
                                oc_ * 128:(oc_ + 1) * 128],
                            deps=[w2prev[q % 2]] +
                                 ([cc_w2] if oc2 == 0 else []) +
                                 ([mms[(q - 2) * 16 + 15]] if q >= 2 else []),
                            tag=f"fw2{f}_{oc2}_{q}", slot=q % 2)
                dws.append(d_w)
                for kk in range(16):
                    kb = q * 16 + kk
                    if kb == 0:
                        dep = [d_w, o_x2b] + bank_use(bk, [])
                    elif kk == 0:
                        dep = [d_w, mms[-1]]
                    else:
                        dep = [mms[-1]]
                    mm = P.op("t", lambda t, bk=bk, kb=kb, q=q, kk=kk: t.matmul(
                        pb[bk][:, 0:FT],
                        w2buf[q % 2][:, kk * 128:(kk + 1) * 128],
                        g1[:, kb * FT:(kb + 1) * FT],
                        start=(kb == 0), stop=(kb == 63)), deps=dep,
                        tag=f"fm2{f}_{oc2}_{kb}")
                    mms.append(mm)
            w2prev = [mms[47], mms[63]]
            ev = P.op("v", lambda v, oc2=oc2, bk=bk: v.tensor_scalar_mul(
                o2n[:, oc2 * FT:(oc2 + 1) * FT], pb[bk][:, 0:FT], gam[:, 5:6]),
                deps=[mms[-1], o_gbe], tag=f"fe2{f}_{oc2}")
            bank_release(bk, [ev])
            ev2s.append(ev)
            if oc2 % 2 == 1:
                yield
        o_sil = P.op("s", lambda s: s.activation(o2n[:, :], o2n[:, :], AF.Silu),
                     deps=[ev2s[-1]], tag=f"fsil{f}")
        d_g = P.dma("y", "gsp",
                    gsp.rearrange("p (dc t) -> p dc t", t=TOKC)[:, :, t0:t0 + FT],
                    o2n.rearrange("p (dc t) -> p dc t", t=FT),
                    deps=[o_sil], tag=f"fgs{f}")
        ffn_prev = [o_sil, d_g, o_x2b, ev2s[-1]]
        g_last = d_g
        yield

    import os
    PH = os.environ.get("KPHASES", "all")
    if PH == "prep":
        P.op("y", lambda y: y.nop(), deps=[cc_ua, cc_db, cc_w1, cc_w2],
             tag="end_prep")
        with allow_ici_all_to_all():
            P.emit(nc, stack)
        return nc, stack, len(P.all)

    g_last = None
    NCH = int(os.environ.get("KNCHUNK", "32"))

    def ffn_source():
        for f in range(NFT):
            yield from emit_ffn_tile(f)

    gen = ffn_source() if PH != "scan" else iter(())
    n_ffn_yields = 4 * (8 + 3 + 8 + 1)
    n_scan_yields = NCH * 9
    ratio = float(os.environ.get("KILV", "1")) * n_ffn_yields / max(n_scan_yields, 1)
    acc = 0.0
    for c in range(NCH):
        for _ in emit_scan_chunk(c):
            acc += ratio
            while acc >= 1.0:
                next(gen, None)
                acc -= 1.0
    for _ in gen:
        pass

    cc_sa = None
    if sdma_s:
        cc_sa = P.collective(lambda g: g.collective_compute(
            "AllToAll", ALU.bypass, replica_groups=[list(range(NCORES))],
            ins=[sa_in.ap().opt()], outs=[sa_out.ap().opt()]),
            deps=sdma_s, tag="cc_sa")

    if PH in ("scan", "ffn"):
        enddeps = ([cc_sa] if sdma_s else scan_tail[-4:]) + \
                  ([g_last] if g_last else [])
        P.op("y", lambda y: y.nop(), deps=enddeps, tag="end_part")
        with allow_ici_all_to_all():
            P.emit(nc, stack)
        return nc, stack, len(P.all)

    out_dmas = []
    prev_fin = [cc_sa, g_last] + ffn_prev
    for f in range(NFT):
        t0 = f * FT
        d_s = P.dma("y", "fin", fs.rearrange("p (dc t) -> p dc t", t=FT),
                    sa_out.rearrange("(dc p) t -> p dc t", p=128)[
                        :, :, t0:t0 + FT], deps=prev_fin, tag=f"zs{f}")
        d_g2 = P.dma("y", "fin", fg.rearrange("p (dc t) -> p dc t", t=FT),
                    gsp.rearrange("p (dc t) -> p dc t", t=TOKC)[:, :, t0:t0 + FT],
                    deps=prev_fin, tag=f"zg{f}", slot=1)
        d_x = P.dma("y", "fin", fxt.rearrange("p (tt d) -> p tt d", d=D),
                    x_in[t0:t0 + FT, :].rearrange("(tt p) d -> p tt d", p=128),
                    deps=prev_fin, tag=f"zx{f}", slot=2)
        o_sg = P.op("v", lambda v: v.tensor_mul(fsg[:, :], fs[:, :], fg[:, :]),
                    deps=[d_s, d_g2], tag=f"zsg{f}")
        trevs = []
        for dc in range(16):
            for tt in range(4):
                j = dc * 4 + tt
                bk = j % 2
                tr = P.op("t", lambda t, dc=dc, tt=tt, bk=bk: t.transpose(
                    pbb[bk][:, 0:128],
                    fsg[:, dc * FT + tt * 128:dc * FT + tt * 128 + 128],
                    identb[:, :]), deps=bank_use(bk, [o_sg, c_idb]),
                    tag=f"ztr{f}_{j}")
                ev = P.op("s", lambda s, dc=dc, tt=tt, bk=bk: s.activation(
                    fsgtm[:, tt * D + dc * 128:tt * D + (dc + 1) * 128],
                    pbb[bk][:, 0:128], AF.Copy), deps=[tr], tag=f"zte{f}_{j}")
                bank_release(bk, [ev])
                trevs.append(ev)
        o_out = P.op("v", lambda v: v.tensor_add(fxt[:, :], fxt[:, :],
                                                 fsgtm[:, :]),
                     deps=[trevs[-1], d_x], tag=f"zout{f}")
        d_o = P.dma("y", "out",
                    out_ext[t0:t0 + FT, :].rearrange("(tt p) d -> p tt d", p=128),
                    fxt.rearrange("p (tt d) -> p tt d", d=D),
                    deps=[o_out], tag=f"zo{f}")
        out_dmas.append(d_o)
        prev_fin = [o_out, d_o, o_sg]

    P.op("y", lambda y: y.nop(), deps=out_dmas, tag="end")

    with allow_ici_all_to_all():
        P.emit(nc, stack)
    n_ops = len(P.all)
    return nc, stack, n_ops


_CACHE = {}


def kernel(L0_attn_input, L0_attn_W_dt_in, L0_attn_W_dt_out, L0_attn_b_dt_out,
           L0_attn_W_B, L0_attn_W_C, L0_attn_D, L0_attn_A_log,
           L0_ffn_w_fc, L0_ffn_w_proj):
    import ml_dtypes
    x = np.ascontiguousarray(np.asarray(L0_attn_input, np.float32)
                             .reshape(TTOT, D))
    w_fc = np.ascontiguousarray(np.asarray(L0_ffn_w_fc, np.float32))
    w_proj = np.ascontiguousarray(np.asarray(L0_ffn_w_proj, np.float32))
    wdtin = np.asarray(L0_attn_W_dt_in, np.float32)
    wdtout = np.asarray(L0_attn_W_dt_out, np.float32)
    bdt = np.asarray(L0_attn_b_dt_out, np.float32)
    wB = np.asarray(L0_attn_W_B, np.float32)
    wC = np.asarray(L0_attn_W_C, np.float32)
    Dv = np.asarray(L0_attn_D, np.float32)
    alog = np.asarray(L0_attn_A_log, np.float32)

    wbct = np.ascontiguousarray(np.concatenate([wdtin, wB, wC], axis=0).T)
    ident = np.eye(128, dtype=np.float32)
    identb = ident.astype(ml_dtypes.bfloat16)
    seln = np.zeros((NST, NST * 128), dtype=ml_dtypes.bfloat16)
    for n in range(NST):
        seln[n, n * 128:(n + 1) * 128] = 1.0
    onesb = np.ones((128, 1), dtype=ml_dtypes.bfloat16)
    ones1 = np.ones((1, 128), dtype=np.float32)
    fconst = np.tile(np.array([[EPS1, EPS3, -0.5, 1.0]], np.float32), (128, 1))

    if "nc" not in _CACHE:
        _CACHE["nc"] = build_program()
    nc, _stack, _n = _CACHE["nc"]
    in_maps = make_in_maps(x, w_fc, w_proj, wbct, wdtout, bdt, Dv, alog,
                           ident, identb, seln, onesb, ones1, fconst)
    with allow_ici_all_to_all():
        res = run_bass_kernel_spmd(nc, in_maps, core_ids=list(range(NCORES)),
                                   cleanup_stale_runs=True)
    out = np.concatenate([r["out"] for r in res.results], axis=0)
    return out.reshape(2, TTOT // 2, D)


def make_in_maps(x, w_fc, w_proj, wbct, wdtout, bdt, Dv, alog,
                 ident, identb, seln, onesb, ones1, fconst):
    in_maps = []
    for k in range(NCORES):
        hs = slice(HSL * k, HSL * (k + 1))
        in_maps.append(dict(
            x=np.ascontiguousarray(x[TOKC * k:TOKC * (k + 1)]),
            w1s=np.ascontiguousarray(w_fc[(HID // NCORES) * k:
                                          (HID // NCORES) * (k + 1)]),
            w2s=np.ascontiguousarray(w_proj[(D // NCORES) * k:
                                            (D // NCORES) * (k + 1)]),
            wbct=wbct,
            wdtoutT=np.ascontiguousarray(wdtout[hs, :].T),
            bdts=np.ascontiguousarray(bdt[hs].reshape(2, 128).T),
            dsl=np.ascontiguousarray(Dv[hs].reshape(2, 128).T),
            alogs=np.ascontiguousarray(
                alog[hs, :].reshape(2, 128, NST).transpose(1, 0, 2)
                .reshape(128, 2 * NST)),
            ident=ident, identb=identb, seln=seln, onesb=onesb, ones1=ones1,
            fconst=fconst,
        ))
    return in_maps


# version 09
# speedup vs baseline: 1.7432x; 1.0511x over previous
import numpy as np
from contextlib import ExitStack

import concourse.bass as bass
import concourse.mybir as mybir
from concourse.bass_utils import run_bass_kernel_spmd
from concourse.aot_env import allow_ici_all_to_all

F32 = mybir.dt.float32
BF16 = mybir.dt.bfloat16
AF = mybir.ActivationFunctionType
ALU = mybir.AluOpType
AX = mybir.AxisListType

NCORES = 8
D = 2048
HID = 8192
TOKC = 2048
TTOT = 16384
HSL = 256
NST = 16
RNK = 64
SC = 512
NSC = TTOT // SC
FT = 512
NFT = TOKC // FT
PT = 256
NPT = TOKC // PT

EPS1 = 1e-5
EPS2 = 1e-6
EPS3 = 1e-6
QEPS = 1e-6

GAMMA_SCALE = None


class Op:
    __slots__ = ("eng", "fn", "deps", "sem_key", "inc", "need_inc", "tag")

    def __init__(self, eng, fn, deps, sem_key, inc, tag):
        self.eng, self.fn, self.deps = eng, fn, deps
        self.sem_key, self.inc, self.tag = sem_key, inc, tag
        self.need_inc = False


class Prog:
    ENGINES = ("v", "s", "t", "g", "y")

    def __init__(self):
        self.streams = {e: [] for e in self.ENGINES}
        self.all = []
        self._ncc = 0

    def op(self, eng, fn, deps=(), tag=""):
        o = Op(eng, fn, [d for d in deps if d is not None], eng, 1, tag)
        self.streams[eng].append(o)
        self.all.append(o)
        return o

    def dma(self, eng, stream, out, in_, deps=(), tag="", slot=0):
        def fn(E, out=out, in_=in_):
            return E.dma_start(out=out, in_=in_)
        o = Op(eng, fn, [d for d in deps if d is not None],
               "q" + eng + stream + str(slot), 16, tag)
        self.streams[eng].append(o)
        self.all.append(o)
        return o

    def collective(self, fn, deps=(), tag="cc"):
        key = "cc%d" % self._ncc
        self._ncc += 1
        o = Op("g", fn, [d for d in deps if d is not None], key, 1, tag)
        self.streams["g"].append(o)
        self.all.append(o)
        return o

    def finalize(self):
        import os
        strict = os.environ.get("KSTRICT", "0") == "1"
        pos = {}
        for e in self.ENGINES:
            for i, o in enumerate(self.streams[e]):
                pos[id(o)] = (e, i)
        if strict:
            for e in self.ENGINES:
                st = self.streams[e]
                for i in range(1, len(st)):
                    st[i].deps.append(st[i - 1])
        else:
            for o in self.all:
                oe, oi = pos[id(o)]
                import os as _o
                relax = _o.environ.get("KRELAX", "tysg")
                if oe in relax:
                    o.deps = [d for d in o.deps
                              if not (d.sem_key == d.eng == oe
                                      and pos[id(d)][1] < oi)]
        for o in self.all:
            for d in o.deps:
                d.need_inc = True
        for o in self.all:
            if o.inc == 16 or o.sem_key.startswith("cc"):
                o.need_inc = True
        counters = {}
        self.val = {}
        for e in self.ENGINES:
            for o in self.streams[e]:
                if o.need_inc:
                    counters[o.sem_key] = counters.get(o.sem_key, 0) + o.inc
                    self.val[o] = counters[o.sem_key]
        self.sem_keys = sorted(counters)

    def check(self):
        ptr = {e: 0 for e in self.ENGINES}
        sem = {k: 0 for k in self.sem_keys}
        done, total, progress = 0, len(self.all), True
        while progress:
            progress = False
            for e in self.ENGINES:
                while ptr[e] < len(self.streams[e]):
                    o = self.streams[e][ptr[e]]
                    if all(sem[d.sem_key] >= self.val[d] for d in o.deps):
                        if o.need_inc:
                            sem[o.sem_key] += o.inc
                        ptr[e] += 1
                        done += 1
                        progress = True
                    else:
                        break
        if done != total:
            stuck = []
            for e in self.ENGINES:
                if ptr[e] < len(self.streams[e]):
                    o = self.streams[e][ptr[e]]
                    why = [(d.tag, d.eng, self.val[d], sem[d.sem_key])
                           for d in o.deps if sem[d.sem_key] < self.val[d]]
                    stuck.append((e, o.tag, why[:2]))
            raise RuntimeError(f"deadlock {done}/{total}: {stuck}")

    def emit(self, nc, stack):
        self.finalize()
        self.check()
        sems = {k: stack.enter_context(nc.semaphore("sm_" + k))
                for k in self.sem_keys}
        block = stack.enter_context(nc.Block())
        prog = self

        def section(ekey):
            def run(E):
                waited = {}
                for o in prog.streams[ekey]:
                    for d in o.deps:
                        v = prog.val[d]
                        if waited.get(d.sem_key, -1) < v:
                            E.wait_ge(sems[d.sem_key], v)
                            waited[d.sem_key] = v
                    inst = o.fn(E)
                    if o.need_inc:
                        inst.then_inc(sems[o.sem_key], o.inc)
            return run

        block.vector(section("v"))
        block.scalar(section("s"))
        block.tensor(section("t"))
        block.gpsimd(section("g"))
        block.sync(section("y"))


class Carver:
    def __init__(self, tensor, nbytes, start=0):
        self.t, self.nbytes, self.cur = tensor, nbytes, start

    def take(self, rows, cols, dt):
        esz = 4 if dt == F32 else 2
        start = (self.cur + 3) // 4 * 4
        self.cur = start + cols * esz
        assert self.cur <= self.nbytes, f"arena overflow {self.cur}>{self.nbytes}"
        if dt == F32:
            return self.t.ap()[0:rows, start // 4:start // 4 + cols]
        return self.t.ap().bitcast(BF16)[0:rows, start // 2:start // 2 + cols]


def build_program():
    nc = bass.Bass("TRN2", target_bir_lowering=False, debug=False)
    P = Prog()

    x_in = nc.dram_tensor("x", [TOKC, D], F32, kind="ExternalInput").ap()
    w1s_in = nc.dram_tensor("w1s", [HID // NCORES, D], F32, kind="ExternalInput").ap()
    w2s_in = nc.dram_tensor("w2s", [D // NCORES, HID], F32, kind="ExternalInput").ap()
    wbct_in = nc.dram_tensor("wbct", [D, 96], F32, kind="ExternalInput").ap()
    wdtoutT_in = nc.dram_tensor("wdtoutT", [RNK, HSL], F32, kind="ExternalInput").ap()
    bdts_in = nc.dram_tensor("bdts", [128, 2], F32, kind="ExternalInput").ap()
    dsl_in = nc.dram_tensor("dsl", [128, 2], F32, kind="ExternalInput").ap()
    alogs_in = nc.dram_tensor("alogs", [128, 32], F32, kind="ExternalInput").ap()
    ident_in = nc.dram_tensor("ident", [128, 128], F32, kind="ExternalInput").ap()
    identb_in = nc.dram_tensor("identb", [128, 128], BF16, kind="ExternalInput").ap()
    seln_in = nc.dram_tensor("seln", [NST, NST * 128], BF16, kind="ExternalInput").ap()
    onesb_in = nc.dram_tensor("onesb", [128, 1], BF16, kind="ExternalInput").ap()
    ones1_in = nc.dram_tensor("ones1", [1, 128], F32, kind="ExternalInput").ap()
    fconst_in = nc.dram_tensor("fconst", [128, 4], F32, kind="ExternalInput").ap()
    out_ext = nc.dram_tensor("out", [TOKC, D], F32, kind="ExternalOutput").ap()

    ar_in = nc.dram_tensor("ar_in", [1, 2], F32)
    ar_out = nc.dram_tensor("ar_out", [1, 2], F32)
    w1ag_in = nc.dram_tensor("w1ag_in", [D, HID // NCORES], BF16)
    w1ag_out = nc.dram_tensor("w1ag_out", [NCORES * D, HID // NCORES], BF16)
    w2ag_in = nc.dram_tensor("w2ag_in", [HID, D // NCORES], BF16)
    w2ag_out = nc.dram_tensor("w2ag_out", [NCORES * HID, D // NCORES], BF16)
    dbag_in = nc.dram_tensor("dbag_in", [96, TOKC], F32)
    dbag_out = nc.dram_tensor("dbag_out", [NCORES * 96, TOKC], F32)
    ua_in = nc.dram_tensor("ua_in", [D, TOKC], BF16)
    ua_out = nc.dram_tensor("ua_out", [D, TOKC], BF16)
    sa_in = nc.dram_tensor("sa_in", [D, TOKC], BF16)
    sa_out = nc.dram_tensor("sa_out", [D, TOKC], BF16)
    xh1sp = nc.dram_tensor("xh1sp", [128, 16 * TOKC], BF16)
    gsp = nc.dram_tensor("gsp", [128, 16 * TOKC], BF16)

    stack = ExitStack()

    def sbuf(name, shape, dt=F32):
        return stack.enter_context(nc.sbuf_tensor(name, shape, dt))

    wbct = sbuf("s_wbct", [128, 16 * 96])
    wdtoutT = sbuf("s_wdtoutT", [RNK, HSL])
    bdts = sbuf("s_bdts", [128, 2])
    dsl = sbuf("s_dsl", [128, 2])
    alogs = sbuf("s_alogs", [128, 32])
    aneg = sbuf("aneg", [128, 32])
    ident = sbuf("s_ident", [128, 128])
    identb = sbuf("s_identb", [128, 128], BF16)
    seln = sbuf("s_seln", [NST, NST * 128], BF16)
    onesb = sbuf("s_onesb", [128, 1], BF16)
    ones1 = sbuf("s_ones1", [1, 128])
    fconst = sbuf("s_fconst", [128, 4])
    gam = sbuf("gam", [128, 6])
    onescol = sbuf("s_onescol", [128, 1])
    grow = sbuf("grow", [1, 16])
    tpart = sbuf("tpart", [128, 16])
    spack = sbuf("spack", [128, 8])
    srow = sbuf("srow", [8, 128])
    sdl = sbuf("sdl", [RNK, SC])
    sbc = sbuf("sbc", [NST, 2 * SC])
    sbcb = sbuf("sbcb", [NST, 2 * SC], BF16)
    sdt = sbuf("sdt", [128, 2 * SC])
    sdta = sbuf("sdta", [128, 2 * SC])
    sdtb2 = sbuf("sdtb2", [128, SC])
    sdtu = sbuf("sdtu", [128, 2 * SC], BF16)
    suc = sbuf("suc", [128, 2 * SC], BF16)
    sab = [sbuf(f"sab{i}", [128, SC]) for i in range(4)]
    sbb = sbuf("sbb", [128, 2 * SC], BF16)
    sbrep = [sbuf(f"sbrep{i}", [128, SC], BF16) for i in range(4)]
    sxb = [sbuf(f"sxb{i}", [128, SC]) for i in range(2)]
    cx = [sbuf(f"cx{i}", [128, SC * NST], BF16) for i in range(2)]
    yred = sbuf("yred", [128, 2 * SC])
    sout = sbuf("sout", [128, 2 * SC], BF16)
    scarry = sbuf("scarry", [128, 32])
    zcol = sbuf("zcol", [128, 1])

    ARENA_BYTES = 119040
    arena_t = sbuf("arena", [128, ARENA_BYTES // 4])
    At = Carver(arena_t, ARENA_BYTES)
    twa = At.take(128, 2048, F32)
    twb = At.take(128, 2048, F32)
    twm1 = At.take(128, 2048, F32)
    twm2 = At.take(128, 2048, F32)
    twh = At.take(128, 2048, BF16)
    tstage = At.take(128, 16 * 128, BF16)
    Ap = Carver(arena_t, ARENA_BYTES, start=At.cur)
    xtm = Ap.take(128, 2 * D, F32)
    xfm = Ap.take(128, 16 * PT, F32)
    ubuf = Ap.take(128, 16 * PT, BF16)
    xh1b = Ap.take(128, 16 * PT, BF16)
    mrep = Ap.take(128, PT, F32)
    r1rep = Ap.take(128, PT, F32)
    rCrep = Ap.take(128, PT, F32)
    srow2 = Ap.take(1, 3 * PT, F32)
    stat6 = Ap.take(128, 24, F32)
    statv = Ap.take(128, 2 * 6, F32)
    sdtbc = Ap.take(96, PT, F32)
    Af = Carver(arena_t, ARENA_BYTES)
    g1 = Af.take(128, 64 * FT, BF16)
    xh1t = Af.take(128, 16 * FT, BF16)
    o2n = xh1t
    w1buf = [Af.take(128, 16 * 128, BF16) for _ in range(2)]
    w2buf = [Af.take(128, 16 * 128, BF16) for _ in range(2)]
    sqc = [Af.take(128, FT, BF16) for _ in range(2)]
    strow = Af.take(1, 6 * FT, F32)
    m3rep = Af.take(128, FT, BF16)
    r3rep = Af.take(128, FT, BF16)
    Az = Carver(arena_t, ARENA_BYTES)
    fs = Az.take(128, 16 * FT, BF16)
    fg = Az.take(128, 16 * FT, BF16)
    fsg = Az.take(128, 16 * FT, BF16)
    fsgtm = Az.take(128, 4 * D, BF16)
    fxt = Az.take(128, 4 * D, F32)

    pb = [stack.enter_context(nc.psum_tensor(f"pb{i}", [128, 512], F32))
          for i in range(8)]
    pbb = [p.ap().bitcast(BF16) for p in pb]
    bank_free = {i: [] for i in range(8)}

    def bank_use(i, deps):
        return list(deps) + bank_free[i]

    def bank_release(i, consumers):
        bank_free[i] = list(consumers)

    c_wbct = P.dma("y", "cst", wbct.ap().rearrange("p (c r) -> p c r", r=96),
                   wbct_in.rearrange("(c p) r -> p c r", p=128), tag="c_wbct")
    c_wdt = P.dma("y", "cst", wdtoutT[:, :], wdtoutT_in, deps=[c_wbct],
                  tag="c_wdt")
    c_bdt = P.dma("y", "cst", bdts[:, :], bdts_in, deps=[c_wdt], tag="c_bdt")
    c_dsl = P.dma("y", "cst", dsl[:, :], dsl_in, deps=[c_bdt], tag="c_dsl")
    c_alog = P.dma("y", "cst", alogs[:, :], alogs_in, deps=[c_dsl], tag="c_alog")
    c_id = P.dma("y", "cst", ident[:, :], ident_in, deps=[c_alog], tag="c_id")
    c_idb = P.dma("y", "cst", identb[:, :], identb_in, deps=[c_id], tag="c_idb")
    c_sel = P.dma("y", "cst", seln[:, :], seln_in, deps=[c_idb], tag="c_sel")
    c_onb = P.dma("y", "cst", onesb[:, :], onesb_in, deps=[c_sel], tag="c_onb")
    c_on1 = P.dma("y", "cst", ones1[:, :], ones1_in, deps=[c_onb], tag="c_on1")
    c_fc = P.dma("y", "cst", fconst[:, :], fconst_in, deps=[c_on1], tag="c_fc")

    o_zc = P.op("v", lambda v: v.memset(zcol[:, :], 0.0), tag="zcol")
    o_onec = P.op("v", lambda v: v.memset(onescol[:, :], 1.0), tag="onec")
    o_aexp = P.op("s", lambda s: s.activation(aneg[:, :], alogs[:, :], AF.Exp),
                  deps=[c_alog], tag="aexp")
    o_aneg = P.op("v", lambda v: v.tensor_scalar_mul(aneg[:, :], aneg[:, :], -1.0),
                  deps=[o_aexp], tag="aneg")

    tern_units = [(1, i, 0, 2048, w1s_in) for i in range(8)] + \
                 [(2, i, hh, 2048, w2s_in) for i in range(2) for hh in range(4)]

    red_ops = []
    for ui, (mat, i, hh, wid, wsrc) in enumerate(tern_units):
        dst = twa if ui % 2 == 0 else twb
        prev = red_ops[ui - 2] if ui >= 2 else None
        d_ = P.dma("y", "wld", dst[:, 0:wid],
                   wsrc[i * 128:(i + 1) * 128, hh * 2048:hh * 2048 + wid],
                   deps=[prev], tag=f"tld{ui}", slot=ui % 2)
        r = P.op("v", lambda v, dst=dst, wid=wid, ui=ui: v.reduce_sum(
            tpart[:, ui:ui + 1], dst[:, 0:wid], axis=AX.X,
            apply_absolute_value=True), deps=[d_], tag=f"tred{ui}")
        red_ops.append(r)

    o_gmm = P.op("t", lambda t: t.matmul(pb[7][0:1, 0:16], onescol[:, :],
                                         tpart[:, 0:16], start=True, stop=True),
                 deps=bank_use(7, [red_ops[-1], o_onec]), tag="gmm")
    o_gr1 = P.op("v", lambda v: v.reduce_sum(grow[0:1, 0:1], pb[7][0:1, 0:8],
                                             axis=AX.X), deps=[o_gmm], tag="gr1")
    o_gr2 = P.op("v", lambda v: v.reduce_sum(grow[0:1, 1:2], pb[7][0:1, 8:16],
                                             axis=AX.X), deps=[o_gmm], tag="gr2")
    bank_release(7, [o_gr1, o_gr2])
    d_gar = P.dma("y", "ccs", ar_in[0:1, :], grow[0:1, 0:2],
                  deps=[o_gr1, o_gr2], tag="gar_st")
    cc_ar = P.collective(lambda g: g.collective_compute(
        "AllReduce", ALU.add, replica_groups=[list(range(NCORES))],
        ins=[ar_in.ap().opt()], outs=[ar_out.ap().opt()]),
        deps=[d_gar], tag="cc_ar")
    d_gld = P.dma("y", "ccs", grow[0:1, 2:4], ar_out[0:1, :], deps=[cc_ar],
                  tag="gar_ld")
    o_gth = P.op("v", lambda v: v.tensor_scalar(
        grow[0:1, 4:6], grow[0:1, 2:4], 1.0 / (HID * D), QEPS, ALU.mult, ALU.add),
        deps=[d_gld], tag="gth")
    o_gth2 = P.op("v", lambda v: v.tensor_scalar_mul(
        grow[0:1, 4:6], grow[0:1, 4:6], 0.5), deps=[o_gth], tag="gth2")
    o_gthn = P.op("v", lambda v: v.tensor_scalar_mul(
        grow[0:1, 6:8], grow[0:1, 4:6], -1.0), deps=[o_gth2], tag="gthn")
    o_gm1 = P.op("v", lambda v: v.tensor_scalar_mul(
        grow[0:1, 8:10], grow[0:1, 2:4], 1.0 / (HID * D)), deps=[d_gld],
        tag="gm1")
    o_gbc = P.op("t", lambda t: t.matmul(pb[7][:, 0:6], ones1[0:1, :],
                                         grow[0:1, 4:10], start=True, stop=True),
                 deps=bank_use(7, [o_gthn, o_gm1]), tag="gbc")
    o_gbe = P.op("s", lambda s: s.activation(gam[:, 0:6], pb[7][:, 0:6], AF.Copy),
                 deps=[o_gbc], tag="gbe")
    bank_release(7, [o_gbe])

    tern_done = []
    w1_last = w2_last = None
    for ui, (mat, i, hh, wid, wsrc) in enumerate(tern_units):
        thp = gam[:, mat - 1:mat]
        thn = gam[:, mat + 1:mat + 2]
        dst = twa if ui % 2 == 0 else twb
        prev = tern_done[ui - 2] if ui >= 2 else red_ops[-1]
        d_ = P.dma("y", "wld", dst[:, 0:wid],
                   wsrc[i * 128:(i + 1) * 128, hh * 2048:hh * 2048 + wid],
                   deps=[prev, red_ops[-1]], tag=f"t2ld{ui}", slot=ui % 2)
        subs = wid // 2048
        mops = []
        for ss in range(subs):
            sl = slice(ss * 2048, (ss + 1) * 2048)
            o1 = P.op("v", lambda v, dst=dst, sl=sl, thp=thp: v.tensor_single_scalar(
                twm1[:, 0:2048], dst[:, sl], thp, ALU.is_gt),
                deps=[d_, o_gbe], tag=f"tm1_{ui}_{ss}")
            o2 = P.op("v", lambda v, dst=dst, sl=sl, thn=thn: v.tensor_single_scalar(
                twm2[:, 0:2048], dst[:, sl], thn, ALU.is_lt),
                deps=[d_, o_gbe], tag=f"tm2_{ui}_{ss}")
            o3 = P.op("v", lambda v, sl=sl: v.tensor_sub(
                twh[:, sl], twm1[:, 0:2048], twm2[:, 0:2048]),
                deps=[o1, o2], tag=f"tmh_{ui}_{ss}")
            mops.append(o3)
        nblk = wid // 128
        evs = []
        prev_dst = tern_done[-1] if tern_done else None
        for j in range(nblk):
            bk = j % 2
            tr = P.op("t", lambda t, j=j, bk=bk: t.transpose(
                pbb[bk][:, 0:128], twh[:, j * 128:(j + 1) * 128], identb[:, :]),
                deps=bank_use(bk, [mops[j // 16], c_idb]), tag=f"ttr{ui}_{j}")
            ev = P.op("s", lambda s, j=j, bk=bk: s.activation(
                tstage[:, j * 128:(j + 1) * 128], pbb[bk][:, 0:128], AF.Copy),
                deps=[tr, prev_dst], tag=f"tte{ui}_{j}")
            bank_release(bk, [ev])
            evs.append(ev)
        agdst = w1ag_in if mat == 1 else w2ag_in
        dd = agdst.rearrange("(j p) o -> p j o", p=128)
        d_st = P.dma("y", "wst",
                     dd[:, hh * 16:hh * 16 + nblk, i * 128:(i + 1) * 128],
                     tstage.rearrange("p (j o) -> p j o", o=128)[:, 0:nblk, :],
                     deps=evs, tag=f"tst{ui}")
        if mat == 1:
            w1_last = d_st
        else:
            w2_last = d_st
        tern_done.append(d_st)

    cc_w1 = P.collective(lambda g: g.collective_compute(
        "AllGather", ALU.bypass, replica_groups=[list(range(NCORES))],
        ins=[w1ag_in.ap().opt()], outs=[w1ag_out.ap().opt()]),
        deps=[w1_last], tag="cc_w1")
    cc_w2 = P.collective(lambda g: g.collective_compute(
        "AllGather", ALU.bypass, replica_groups=[list(range(NCORES))],
        ins=[w2ag_in.ap().opt()], outs=[w2ag_out.ap().opt()]),
        deps=[w2_last], tag="cc_w2")

    dbc_dmas, ua_dmas, xh_dmas = [], [], []
    prev_tile = []
    for pt in range(NPT):
        t0 = pt * PT
        d_x = P.dma("y", "xld", xtm.rearrange("p (tt d) -> p tt d", d=D),
                    x_in[t0:t0 + PT, :].rearrange("(tt p) d -> p tt d", p=128),
                    deps=prev_tile, tag=f"pxl{pt}")
        pk_all = []
        for tt in range(2):
            bns = []
            for cch in range(4):
                o = P.op("v", lambda v, tt=tt, cch=cch: v.bn_stats(
                    stat6[:, cch * 6:(cch + 1) * 6],
                    xtm[:, tt * D + cch * 512:tt * D + (cch + 1) * 512]),
                    deps=[d_x], tag=f"pbn{pt}_{tt}_{cch}")
                bns.append(o)
            o_ag = P.op("v", lambda v, tt=tt: v.bn_aggr(
                statv[:, tt * 6:tt * 6 + 2], stat6[:, 0:24]),
                deps=[bns[-1]], tag=f"pag{pt}_{tt}")
            l1 = P.op("s", lambda s, tt=tt: s.activation(
                statv[:, tt * 6 + 4:tt * 6 + 5], statv[:, tt * 6 + 1:tt * 6 + 2],
                AF.Ln, bias=fconst[:, 0:1]), deps=[o_ag, c_fc], tag=f"pl1{pt}_{tt}")
            e1 = P.op("s", lambda s, tt=tt: s.activation(
                statv[:, tt * 6 + 2:tt * 6 + 3], statv[:, tt * 6 + 4:tt * 6 + 5],
                AF.Exp, scale=fconst[:, 2:3]), deps=[l1], tag=f"pe1{pt}_{tt}")
            vv = P.op("v", lambda v, tt=tt: v.tensor_scalar(
                statv[:, tt * 6 + 5:tt * 6 + 6], statv[:, tt * 6 + 1:tt * 6 + 2],
                1.0 + EPS2, EPS1 * EPS2, ALU.mult, ALU.add),
                deps=[o_ag], tag=f"pvv{pt}_{tt}")
            lC = P.op("s", lambda s, tt=tt: s.activation(
                statv[:, tt * 6 + 5:tt * 6 + 6], statv[:, tt * 6 + 5:tt * 6 + 6],
                AF.Ln), deps=[vv], tag=f"plC{pt}_{tt}")
            eC = P.op("s", lambda s, tt=tt: s.activation(
                statv[:, tt * 6 + 3:tt * 6 + 4], statv[:, tt * 6 + 5:tt * 6 + 6],
                AF.Exp, scale=fconst[:, 2:3]), deps=[lC], tag=f"peC{pt}_{tt}")
            for j, col in ((0, 0), (1, 2), (2, 3)):
                o = P.op("v", lambda v, tt=tt, j=j, col=col: v.tensor_copy(
                    spack[:, tt * 3 + j:tt * 3 + j + 1],
                    statv[:, tt * 6 + col:tt * 6 + col + 1]),
                    deps=[e1, eC], tag=f"ppk{pt}_{tt}_{j}")
                pk_all.append(o)
        tr_s = P.op("t", lambda t: t.transpose(pb[7][0:6, 0:128], spack[:, 0:6],
                                               ident[:, :]),
                    deps=bank_use(7, pk_all + [c_id]), tag=f"ptr{pt}")
        ev_s = P.op("s", lambda s: s.activation(srow[0:6, :], pb[7][0:6, 0:128],
                                                AF.Copy), deps=[tr_s],
                    tag=f"pev{pt}")
        bank_release(7, [ev_s])
        d_rm = None
        for jj in range(3):
            for tt in range(2):
                d_rm = P.dma("y", "srw",
                             srow2[0:1, jj * PT + tt * 128:jj * PT + tt * 128 + 128],
                             srow[tt * 3 + jj:tt * 3 + jj + 1, :],
                             deps=[ev_s, d_rm], tag=f"prm{pt}_{jj}_{tt}")
        reps = []
        for j, dstrep in ((0, mrep), (1, r1rep), (2, rCrep)):
            bc = P.op("t", lambda t, j=j: t.matmul(
                pb[7][:, 0:PT], ones1[0:1, :], srow2[0:1, j * PT:(j + 1) * PT],
                start=True, stop=True), deps=bank_use(7, [d_rm]),
                tag=f"pbc{pt}_{j}")
            evr = P.op("s", lambda s, dstrep=dstrep: s.activation(
                dstrep[:, :], pb[7][:, 0:PT], AF.Copy), deps=[bc],
                tag=f"pbe{pt}_{j}")
            bank_release(7, [evr])
            reps.append(evr)
        trev = []
        for tt in range(2):
            for dc in range(16):
                j = tt * 16 + dc
                bk = 3 + j % 2
                tr = P.op("t", lambda t, tt=tt, dc=dc, bk=bk: t.transpose(
                    pb[bk][:, 0:128],
                    xtm[:, tt * D + dc * 128:tt * D + (dc + 1) * 128],
                    ident[:, :]), deps=bank_use(bk, [d_x, c_id]),
                    tag=f"pxt{pt}_{j}")
                ev = P.op("s", lambda s, tt=tt, dc=dc, bk=bk: s.activation(
                    xfm[:, dc * PT + tt * 128:dc * PT + tt * 128 + 128],
                    pb[bk][:, 0:128], AF.Copy), deps=[tr], tag=f"pxe{pt}_{j}")
                bank_release(bk, [ev])
                trev.append(ev)
        xv = xfm.rearrange("p (dc t) -> p dc t", t=PT)
        mv = mrep.rearrange("p (o t) -> p o t", o=1)
        xvb, mvb = bass.broadcast_tensor_aps(xv, mv)
        o_sub = P.op("v", lambda v: v.tensor_sub(xv, xvb, mvb),
                     deps=trev + [reps[0]], tag=f"psub{pt}")
        uv = ubuf.rearrange("p (dc t) -> p dc t", t=PT)
        r1v = r1rep.rearrange("p (o t) -> p o t", o=1)
        _, r1vb = bass.broadcast_tensor_aps(xv, r1v)
        o_u = P.op("v", lambda v: v.tensor_mul(uv, xvb, r1vb),
                   deps=[o_sub, reps[1]], tag=f"pu{pt}")
        xh1v = xh1b.rearrange("p (dc t) -> p dc t", t=PT)
        rCv = rCrep.rearrange("p (o t) -> p o t", o=1)
        _, rCvb = bass.broadcast_tensor_aps(xv, rCv)
        o_xh = P.op("v", lambda v: v.tensor_mul(xh1v, xvb, rCvb),
                    deps=[o_sub, reps[2]], tag=f"pxh{pt}")
        mms = []
        for dc in range(16):
            mm = P.op("t", lambda t, dc=dc: t.matmul(
                pb[5][0:96, 0:PT], wbct[:, dc * 96:(dc + 1) * 96],
                xfm[:, dc * PT:(dc + 1) * PT], start=(dc == 0), stop=(dc == 15)),
                deps=(bank_use(5, [o_sub, c_wbct]) if dc == 0 else [mms[-1]]),
                tag=f"pdb{pt}_{dc}")
            mms.append(mm)
        o_dbe = P.op("v", lambda v: v.tensor_mul(
            sdtbc[0:96, :], pb[5][0:96, 0:PT], r1rep[0:96, :]),
            deps=[mms[-1], reps[1]], tag=f"pdbe{pt}")
        bank_release(5, [o_dbe])
        d_db = P.dma("y", "dbc", dbag_in[0:96, t0:t0 + PT], sdtbc[0:96, :],
                     deps=[o_dbe], tag=f"pdbs{pt}")
        d_ua = P.dma("y", "uas",
                     ua_in.rearrange("(dc p) t -> p dc t", p=128)[:, :, t0:t0 + PT],
                     uv, deps=[o_u], tag=f"pua{pt}")
        d_xh = P.dma("y", "xsp",
                     xh1sp.rearrange("p (dc t) -> p dc t", t=TOKC)[:, :, t0:t0 + PT],
                     xh1v, deps=[o_xh], tag=f"pxs{pt}")
        dbc_dmas.append(d_db)
        ua_dmas.append(d_ua)
        xh_dmas.append(d_xh)
        prev_tile = [o_u, o_xh, o_dbe, d_ua, d_xh, d_db, mms[-1]]

    cc_db = P.collective(lambda g: g.collective_compute(
        "AllGather", ALU.bypass, replica_groups=[list(range(NCORES))],
        ins=[dbag_in.ap().opt()], outs=[dbag_out.ap().opt()]),
        deps=dbc_dmas, tag="cc_db")
    cc_ua = P.collective(lambda g: g.collective_compute(
        "AllToAll", ALU.bypass, replica_groups=[list(range(NCORES))],
        ins=[ua_in.ap().opt()], outs=[ua_out.ap().opt()]),
        deps=ua_dmas, tag="cc_ua")

    war = {
        "sdl": [cc_db], "sbc": [cc_db], "sbcb": [], "suc": [cc_ua],
        "sout": {0: None, 1: None},
        "scan_pp": {0: None, 1: None, 2: None, 3: None},
        "xb_pp": {0: None, 1: None},
        "carry_pp": {0: None, 1: None},
        "b_pp": {0: None, 1: None},
        "c_pp": {0: None, 1: None},
        "pool": {0: None, 1: None},
        "aexp_last": None,
        "b_last": None,
    }
    prev_carry = {}
    sdma_s = []
    ffn_prev = []

    scan_tail = []

    def emit_scan_chunk(c):
        import os as _os
        SPART = int(_os.environ.get("KSCANPART", "5"))
        r, off = c // 4, (c % 4) * SC
        d_dl = P.dma("y", "sdl", sdl[:, :],
                     dbag_out[r * 96:r * 96 + RNK, off:off + SC],
                     deps=war["sdl"], tag=f"sdl{c}")
        d_b = P.dma("y", "sbc", sbc[:, 0:SC],
                    dbag_out[r * 96 + RNK:r * 96 + 80, off:off + SC],
                    deps=war["sbc"], tag=f"sBl{c}", slot=0)
        d_c = P.dma("y", "sbc", sbc[:, SC:2 * SC],
                    dbag_out[r * 96 + 80:r * 96 + 96, off:off + SC],
                    deps=war["sbc"], tag=f"sCl{c}", slot=1)
        d_u0 = P.dma("y", "sc2", suc[:, 0:SC],
                     ua_out[r * 256:r * 256 + 128, off:off + SC],
                     deps=war["suc"], tag=f"su0{c}", slot=0)
        d_u1 = P.dma("y", "sc2", suc[:, SC:2 * SC],
                     ua_out[r * 256 + 128:r * 256 + 256, off:off + SC],
                     deps=war["suc"], tag=f"su1{c}", slot=1)
        o_bc16 = P.op("v", lambda v: v.tensor_copy(sbcb[:, :], sbc[:, :]),
                      deps=[d_b, d_c] + war["sbcb"], tag=f"sbc16_{c}")
        dt_ops = []
        for m in range(2):
            sl = slice(m * SC, (m + 1) * SC)
            mm = P.op("t", lambda t, m=m: t.matmul(
                pb[7][:, 0:SC], wdtoutT[:, m * 128:(m + 1) * 128], sdl[:, :],
                start=True, stop=True),
                deps=bank_use(7, [d_dl, c_wdt]), tag=f"sdtm{c}_{m}")
            a1 = P.op("s", lambda s, sl=sl, m=m: s.activation(
                sdta[:, sl], pb[7][:, 0:SC], AF.Abs, bias=bdts[:, m:m + 1]),
                deps=[mm, c_bdt], tag=f"sab{c}_{m}")
            r1 = P.op("v", lambda v, m=m: v.tensor_scalar(
                sdtb2[:, :], pb[7][:, 0:SC], bdts[:, m:m + 1], 0.0,
                ALU.add, ALU.max), deps=[mm, c_bdt, a1], tag=f"srl{c}_{m}")
            bank_release(7, [a1, r1])
            e1 = P.op("s", lambda s, sl=sl: s.activation(
                sdta[:, sl], sdta[:, sl], AF.Exp, scale=-1.0),
                deps=[a1], tag=f"sex{c}_{m}")
            l1 = P.op("s", lambda s, sl=sl: s.activation(
                sdta[:, sl], sdta[:, sl], AF.Ln, bias=1.0),
                deps=[e1], tag=f"sln{c}_{m}")
            o_dt = P.op("v", lambda v, sl=sl: v.tensor_add(
                sdt[:, sl], sdta[:, sl], sdtb2[:, :]),
                deps=[l1, r1, war["aexp_last"]], tag=f"sdt{c}_{m}")
            o_du = P.op("v", lambda v, sl=sl: v.tensor_mul(
                sdtu[:, sl], sdt[:, sl], suc[:, sl]),
                deps=[o_dt, d_u0, d_u1], tag=f"sdu{c}_{m}")
            dt_ops.append((o_dt, o_du))
        yield
        if SPART <= 1:
            scan_tail.append(dt_ops[1][1])
            war["sdl"] = [dt_ops[0][0], dt_ops[1][0]]
            war["sbc"] = [o_bc16]
            war["suc"] = [dt_ops[0][1], dt_ops[1][1]]
            war["sbcb"] = [o_bc16]
            return
        war["sdl"] = [dt_ops[0][0], dt_ops[1][0]]
        war["sbc"] = [o_bc16]
        last_a = None
        last_b = None
        last_bc = None
        cmuls = {0: [], 1: []}
        it = 0
        for n in range(NST):
            ppn = n % 2
            bcB = P.op("t", lambda t, n=n: t.matmul(
                pb[5][:, 0:SC], seln[:, n * 128:(n + 1) * 128],
                sbcb[:, 0:SC], start=True, stop=True),
                deps=bank_use(5, [o_bc16, c_sel]), tag=f"bB{c}_{n}")
            evB = P.op("s", lambda s, ppn=ppn: s.activation(
                sbrep[ppn][:, :], pb[5][:, 0:SC], AF.Copy),
                deps=[bcB] + (war["b_pp"][ppn] or []), tag=f"eB{c}_{n}")
            bank_release(5, [evB])
            bcC = P.op("t", lambda t, n=n: t.matmul(
                pb[6][:, 0:SC], seln[:, n * 128:(n + 1) * 128],
                sbcb[:, SC:2 * SC], start=True, stop=True),
                deps=bank_use(6, [o_bc16, c_sel]), tag=f"bC{c}_{n}")
            evC = P.op("s", lambda s, ppn=ppn: s.activation(
                sbrep[2 + ppn][:, :], pb[6][:, 0:SC], AF.Copy),
                deps=[bcC] + (war["c_pp"][ppn] or []), tag=f"eC{c}_{n}")
            bank_release(6, [evC])
            last_bc = bcC
            duv = sdtu.ap().rearrange("p (m t) -> p m t", m=2)
            brv = sbrep[ppn].ap().rearrange("p (o t) -> p o t", o=1)
            duvb, brvb = bass.broadcast_tensor_aps(duv, brv)
            o_b = P.op("v", lambda v, duvb=duvb, brvb=brvb: v.tensor_mul(
                sbb.ap().rearrange("p (m t) -> p m t", m=2), duvb, brvb),
                deps=[dt_ops[0][1], dt_ops[1][1], evB], tag=f"b{c}_{n}")
            last_b = o_b
            war["b_pp"][ppn] = [o_b]
            creaders = []
            for m in range(2):
                g = m * NST + n
                sl = slice(m * SC, (m + 1) * SC)
                pp = it % 4
                xp = it % 2
                it += 1
                o_a = P.op("s", lambda s, sl=sl, m=m, n=n, pp=pp: s.activation(
                    sab[pp][:, :], sdt[:, sl], AF.Exp,
                    scale=aneg[:, m * 16 + n:m * 16 + n + 1]),
                    deps=[dt_ops[m][0], o_aneg, war["scan_pp"][pp]],
                    tag=f"a{c}_{g}")
                last_a = o_a
                if c == 0 or c == 16:
                    init = zcol[:, 0:1]
                    ideps = [o_a, o_b, o_zc, war["carry_pp"][xp],
                             war["xb_pp"][xp]]
                else:
                    init = scarry[:, g:g + 1]
                    ideps = [o_a, o_b, prev_carry.get((c - 1, g)),
                             war["carry_pp"][xp], war["xb_pp"][xp]]
                o_x = P.op("v", lambda v, pp=pp, xp=xp, sl=sl, init=init:
                           v.tensor_tensor_scan(
                    sxb[xp][:, :], sab[pp][:, :], sbb[:, sl], init,
                    ALU.mult, ALU.add), deps=ideps, tag=f"x{c}_{g}")
                war["scan_pp"][pp] = o_x
                o_cr = P.op("g", lambda gp, xp=xp, g=g: gp.tensor_copy(
                    scarry[:, g:g + 1], sxb[xp][:, SC - 1:SC]),
                    deps=[o_x], tag=f"cr{c}_{g}")
                prev_carry[(c, g)] = o_cr
                war["carry_pp"][xp] = o_cr
                o_cm = P.op("v", lambda v, xp=xp, n=n, m=m, ppn=ppn: v.tensor_mul(
                    cx[m][:, n * SC:(n + 1) * SC],
                    sxb[xp][:, :], sbrep[2 + ppn][:, :]),
                    deps=[o_x, evC, war["pool"][m]], tag=f"cm{c}_{g}")
                war["xb_pp"][xp] = o_cm
                creaders.append(o_cm)
                cmuls[m].append(o_cm)
            war["c_pp"][ppn] = creaders
            if n % 2 == 1:
                yield
        for m in range(2):
            sl = slice(m * SC, (m + 1) * SC)
            yv = yred[:, sl]
            a0 = P.op("g", lambda gp, yv=yv, m=m: gp.tensor_add(
                yv, cx[m][:, 0:SC], cx[m][:, SC:2 * SC]),
                deps=[cmuls[m][0], cmuls[m][1]], tag=f"yr{c}_{m}_0")
            last = a0
            for n in range(2, NST):
                last = P.op("g", lambda gp, yv=yv, m=m, n=n: gp.tensor_add(
                    yv, yv, cx[m][:, n * SC:(n + 1) * SC]),
                    deps=[last, cmuls[m][n]], tag=f"yr{c}_{m}_{n}")
            o_st = P.op("v", lambda gp, sl=sl, m=m: gp.scalar_tensor_tensor(
                sout[:, sl], suc[:, sl], dsl[:, m:m + 1], yred[:, sl],
                ALU.mult, ALU.add),
                deps=[last, c_dsl, d_u0, d_u1, war["sout"][m]],
                tag=f"ys{c}_{m}")
            war["pool"][m] = o_st
            d_s = P.dma("y", "sst",
                        sa_in[r * 256 + m * 128:r * 256 + (m + 1) * 128,
                              off:off + SC],
                        sout[:, sl], deps=[o_st], tag=f"ss{c}_{m}", slot=m)
            war["sout"][m] = d_s
            sdma_s.append(d_s)
        war["aexp_last"] = last_a
        war["b_last"] = last_b
        war["suc"] = [dt_ops[0][1], dt_ops[1][1], war["pool"][0], war["pool"][1]]
        war["sbcb"] = [last_bc]

    def emit_ffn_tile(f):
        nonlocal ffn_prev, g_last
        t0 = f * FT
        d_xh = P.dma("y", "xht", xh1t.rearrange("p (dc t) -> p dc t", t=FT),
                     xh1sp.rearrange("p (dc t) -> p dc t", t=TOKC)[
                         :, :, t0:t0 + FT],
                     deps=xh_dmas + [cc_w1, cc_w2] + ffn_prev, tag=f"fxh{f}")
        w1v = w1ag_out.rearrange("(r dc p) o -> p r dc o", p=128, dc=16)
        evacs = []
        wprev = [None, None]
        for oc in range(64):
            r_, oc_ = oc // 8, oc % 8
            half = oc % 2
            d_w = P.dma("y", "w1s",
                        w1buf[half].rearrange("p (dc o) -> p dc o", o=128),
                        w1v[:, r_, :, oc_ * 128:(oc_ + 1) * 128],
                        deps=[wprev[half], cc_w1], tag=f"fw1{f}_{oc}", slot=half)
            mms = []
            for kb in range(16):
                dep = ([d_w, d_xh] + bank_use(half, [])) if kb == 0 else [mms[-1]]
                mm = P.op("t", lambda t, half=half, kb=kb: t.matmul(
                    pb[half][:, 0:FT], w1buf[half][:, kb * 128:(kb + 1) * 128],
                    xh1t[:, kb * FT:(kb + 1) * FT],
                    start=(kb == 0), stop=(kb == 15)), deps=dep,
                    tag=f"fm1{f}_{oc}_{kb}")
                mms.append(mm)
            wprev[half] = mms[-1]
            if oc % 8 == 7:
                yield
            ev = P.op("v", lambda v, oc=oc, half=half: v.tensor_scalar_mul(
                g1[:, oc * FT:(oc + 1) * FT], pb[half][:, 0:FT], gam[:, 4:5]),
                deps=[mms[-1], o_gbe], tag=f"fe1{f}_{oc}")
            bank_release(half, [ev])
            evacs.append(ev)
        o_gel = P.op("s", lambda s: s.activation(g1[:, :], g1[:, :], AF.Gelu),
                     deps=[evacs[-1]], tag=f"fgel{f}")
        summs = []
        for oc in range(64):
            dep = (bank_use(2, [o_gel, c_onb])) if oc == 0 else [summs[-1]]
            mm = P.op("t", lambda t, oc=oc: t.matmul(
                pb[2][0:1, 0:FT], onesb[:, 0:1], g1[:, oc * FT:(oc + 1) * FT],
                start=(oc == 0), stop=(oc == 63)), deps=dep, tag=f"fsm{f}_{oc}")
            summs.append(mm)
        yield
        ev_sum = P.op("v", lambda v: v.tensor_copy(strow[0:1, 0:FT],
                                                   pb[2][0:1, 0:FT]),
                      deps=[summs[-1], evacs[-1]], tag=f"fsev{f}")
        sqmms, sq_ops = [], []
        for oc in range(64):
            half = oc % 2
            o_sq = P.op("v", lambda v, oc=oc, half=half: v.tensor_mul(
                sqc[half][:, :], g1[:, oc * FT:(oc + 1) * FT],
                g1[:, oc * FT:(oc + 1) * FT]),
                deps=[ev_sum] if oc == 0 else ([sqmms[-2]] if oc >= 2 else []),
                tag=f"fsq{f}_{oc}")
            dep = [o_sq] + ([ev_sum] if oc == 0 else [sqmms[-1]])
            mm = P.op("t", lambda t, half=half, oc=oc: t.matmul(
                pb[2][0:1, 0:FT], onesb[:, 0:1], sqc[half][:, :],
                start=(oc == 0), stop=(oc == 63)), deps=dep, tag=f"fqm{f}_{oc}")
            sqmms.append(mm)
            sq_ops.append(o_sq)
        ev_sq = P.op("v", lambda v: v.tensor_copy(strow[0:1, FT:2 * FT],
                                                  pb[2][0:1, 0:FT]),
                     deps=[sqmms[-1]], tag=f"fqev{f}")
        bank_release(2, [ev_sq])
        o_m3 = P.op("v", lambda v: v.tensor_scalar_mul(
            strow[0:1, 2 * FT:3 * FT], strow[0:1, 0:FT], 1.0 / HID),
            deps=[ev_sum], tag=f"fm3{f}")
        o_s2 = P.op("s", lambda s: s.activation(
            strow[0:1, 3 * FT:4 * FT], strow[0:1, 2 * FT:3 * FT], AF.Square),
            deps=[o_m3], tag=f"fs2{f}")
        o_e2 = P.op("v", lambda v: v.tensor_scalar_mul(
            strow[0:1, 4 * FT:5 * FT], strow[0:1, FT:2 * FT], 1.0 / HID),
            deps=[ev_sq], tag=f"fe2{f}")
        o_var = P.op("v", lambda v: v.tensor_sub(
            strow[0:1, 3 * FT:4 * FT], strow[0:1, 4 * FT:5 * FT],
            strow[0:1, 3 * FT:4 * FT]), deps=[o_s2, o_e2], tag=f"fvar{f}")
        o_ln3 = P.op("s", lambda s: s.activation(
            strow[0:1, 4 * FT:5 * FT], strow[0:1, 3 * FT:4 * FT], AF.Ln,
            bias=fconst[0:1, 1:2]), deps=[o_var, c_fc], tag=f"fln{f}")
        o_r3 = P.op("s", lambda s: s.activation(
            strow[0:1, 5 * FT:6 * FT], strow[0:1, 4 * FT:5 * FT], AF.Exp,
            scale=fconst[0:1, 2:3]), deps=[o_ln3], tag=f"fr3{f}")
        bc_m = P.op("t", lambda t: t.matmul(
            pb[7][:, 0:FT], ones1[0:1, :], strow[0:1, 2 * FT:3 * FT],
            start=True, stop=True), deps=bank_use(7, [o_m3]), tag=f"fbm{f}")
        ev_m = P.op("s", lambda s: s.activation(m3rep[:, :], pb[7][:, 0:FT],
                                                AF.Copy),
                    deps=[bc_m] + ffn_prev, tag=f"fbme{f}")
        bank_release(7, [ev_m])
        bc_r = P.op("t", lambda t: t.matmul(
            pb[7][:, 0:FT], ones1[0:1, :], strow[0:1, 5 * FT:6 * FT],
            start=True, stop=True), deps=bank_use(7, [o_r3]), tag=f"fbr{f}")
        ev_r = P.op("s", lambda s: s.activation(r3rep[:, :], pb[7][:, 0:FT],
                                                AF.Copy),
                    deps=[bc_r] + ffn_prev, tag=f"fbre{f}")
        bank_release(7, [ev_r])
        g1v = g1.rearrange("p (oc t) -> p oc t", t=FT)
        m3v = m3rep.rearrange("p (o t) -> p o t", o=1)
        g1vb, m3vb = bass.broadcast_tensor_aps(g1v, m3v)
        o_x2a = P.op("v", lambda v: v.tensor_sub(g1v, g1vb, m3vb),
                     deps=[ev_m, ev_sq], tag=f"fx2a{f}")
        r3v = r3rep.rearrange("p (o t) -> p o t", o=1)
        _, r3vb = bass.broadcast_tensor_aps(g1v, r3v)
        o_x2b = P.op("v", lambda v: v.tensor_mul(g1v, g1vb, r3vb),
                     deps=[o_x2a, ev_r], tag=f"fx2b{f}")
        yield
        w2v = w2ag_out.rearrange("(r kk p) o -> p r kk o", p=128, kk=64)
        ev2s = []
        w2prev = [None, None]
        for oc2 in range(16):
            r_, oc_ = oc2 // 2, oc2 % 2
            bk = 3 + oc2 % 2
            dws = []
            mms = []
            for q in range(4):
                d_w = P.dma("y", "w2s",
                            w2buf[q % 2].rearrange("p (kk o) -> p kk o", o=128),
                            w2v[:, r_, q * 16:(q + 1) * 16,
                                oc_ * 128:(oc_ + 1) * 128],
                            deps=[w2prev[q % 2]] +
                                 ([cc_w2] if oc2 == 0 else []) +
                                 ([mms[(q - 2) * 16 + 15]] if q >= 2 else []),
                            tag=f"fw2{f}_{oc2}_{q}", slot=q % 2)
                dws.append(d_w)
                for kk in range(16):
                    kb = q * 16 + kk
                    if kb == 0:
                        dep = [d_w, o_x2b] + bank_use(bk, [])
                    elif kk == 0:
                        dep = [d_w, mms[-1]]
                    else:
                        dep = [mms[-1]]
                    mm = P.op("t", lambda t, bk=bk, kb=kb, q=q, kk=kk: t.matmul(
                        pb[bk][:, 0:FT],
                        w2buf[q % 2][:, kk * 128:(kk + 1) * 128],
                        g1[:, kb * FT:(kb + 1) * FT],
                        start=(kb == 0), stop=(kb == 63)), deps=dep,
                        tag=f"fm2{f}_{oc2}_{kb}")
                    mms.append(mm)
            w2prev = [mms[47], mms[63]]
            ev = P.op("v", lambda v, oc2=oc2, bk=bk: v.tensor_scalar_mul(
                o2n[:, oc2 * FT:(oc2 + 1) * FT], pb[bk][:, 0:FT], gam[:, 5:6]),
                deps=[mms[-1], o_gbe], tag=f"fe2{f}_{oc2}")
            bank_release(bk, [ev])
            ev2s.append(ev)
            if oc2 % 2 == 1:
                yield
        o_sil = P.op("s", lambda s: s.activation(o2n[:, :], o2n[:, :], AF.Silu),
                     deps=[ev2s[-1]], tag=f"fsil{f}")
        d_g = P.dma("y", "gsp",
                    gsp.rearrange("p (dc t) -> p dc t", t=TOKC)[:, :, t0:t0 + FT],
                    o2n.rearrange("p (dc t) -> p dc t", t=FT),
                    deps=[o_sil], tag=f"fgs{f}")
        ffn_prev = [o_sil, d_g, o_x2b, ev2s[-1]]
        g_last = d_g
        yield

    import os
    PH = os.environ.get("KPHASES", "all")
    if PH == "prep":
        P.op("y", lambda y: y.nop(), deps=[cc_ua, cc_db, cc_w1, cc_w2],
             tag="end_prep")
        with allow_ici_all_to_all():
            P.emit(nc, stack)
        return nc, stack, len(P.all)

    g_last = None
    NCH = int(os.environ.get("KNCHUNK", "32"))

    def ffn_source():
        for f in range(NFT):
            yield from emit_ffn_tile(f)

    gen = ffn_source() if PH != "scan" else iter(())
    n_ffn_yields = 4 * (8 + 3 + 8 + 1)
    n_scan_yields = NCH * 9
    ratio = float(os.environ.get("KILV", "1")) * n_ffn_yields / max(n_scan_yields, 1)
    acc = 0.0
    for c in range(NCH):
        for _ in emit_scan_chunk(c):
            acc += ratio
            while acc >= 1.0:
                next(gen, None)
                acc -= 1.0
    for _ in gen:
        pass

    cc_sa = None
    if sdma_s:
        cc_sa = P.collective(lambda g: g.collective_compute(
            "AllToAll", ALU.bypass, replica_groups=[list(range(NCORES))],
            ins=[sa_in.ap().opt()], outs=[sa_out.ap().opt()]),
            deps=sdma_s, tag="cc_sa")

    if PH in ("scan", "ffn"):
        enddeps = ([cc_sa] if sdma_s else scan_tail[-4:]) + \
                  ([g_last] if g_last else [])
        P.op("y", lambda y: y.nop(), deps=enddeps, tag="end_part")
        with allow_ici_all_to_all():
            P.emit(nc, stack)
        return nc, stack, len(P.all)

    out_dmas = []
    prev_fin = [cc_sa, g_last] + ffn_prev
    for f in range(NFT):
        t0 = f * FT
        d_s = P.dma("y", "fin", fs.rearrange("p (dc t) -> p dc t", t=FT),
                    sa_out.rearrange("(dc p) t -> p dc t", p=128)[
                        :, :, t0:t0 + FT], deps=prev_fin, tag=f"zs{f}")
        d_g2 = P.dma("y", "fin", fg.rearrange("p (dc t) -> p dc t", t=FT),
                    gsp.rearrange("p (dc t) -> p dc t", t=TOKC)[:, :, t0:t0 + FT],
                    deps=prev_fin, tag=f"zg{f}", slot=1)
        d_x = P.dma("y", "fin", fxt.rearrange("p (tt d) -> p tt d", d=D),
                    x_in[t0:t0 + FT, :].rearrange("(tt p) d -> p tt d", p=128),
                    deps=prev_fin, tag=f"zx{f}", slot=2)
        o_sg = P.op("v", lambda v: v.tensor_mul(fsg[:, :], fs[:, :], fg[:, :]),
                    deps=[d_s, d_g2], tag=f"zsg{f}")
        trevs = []
        for dc in range(16):
            for tt in range(4):
                j = dc * 4 + tt
                bk = j % 2
                tr = P.op("t", lambda t, dc=dc, tt=tt, bk=bk: t.transpose(
                    pbb[bk][:, 0:128],
                    fsg[:, dc * FT + tt * 128:dc * FT + tt * 128 + 128],
                    identb[:, :]), deps=bank_use(bk, [o_sg, c_idb]),
                    tag=f"ztr{f}_{j}")
                ev = P.op("s", lambda s, dc=dc, tt=tt, bk=bk: s.activation(
                    fsgtm[:, tt * D + dc * 128:tt * D + (dc + 1) * 128],
                    pbb[bk][:, 0:128], AF.Copy), deps=[tr], tag=f"zte{f}_{j}")
                bank_release(bk, [ev])
                trevs.append(ev)
        o_out = P.op("v", lambda v: v.tensor_add(fxt[:, :], fxt[:, :],
                                                 fsgtm[:, :]),
                     deps=[trevs[-1], d_x], tag=f"zout{f}")
        d_o = P.dma("y", "out",
                    out_ext[t0:t0 + FT, :].rearrange("(tt p) d -> p tt d", p=128),
                    fxt.rearrange("p (tt d) -> p tt d", d=D),
                    deps=[o_out], tag=f"zo{f}")
        out_dmas.append(d_o)
        prev_fin = [o_out, d_o, o_sg]

    P.op("y", lambda y: y.nop(), deps=out_dmas, tag="end")

    with allow_ici_all_to_all():
        P.emit(nc, stack)
    n_ops = len(P.all)
    return nc, stack, n_ops


_CACHE = {}


def kernel(L0_attn_input, L0_attn_W_dt_in, L0_attn_W_dt_out, L0_attn_b_dt_out,
           L0_attn_W_B, L0_attn_W_C, L0_attn_D, L0_attn_A_log,
           L0_ffn_w_fc, L0_ffn_w_proj):
    import ml_dtypes
    x = np.ascontiguousarray(np.asarray(L0_attn_input, np.float32)
                             .reshape(TTOT, D))
    w_fc = np.ascontiguousarray(np.asarray(L0_ffn_w_fc, np.float32))
    w_proj = np.ascontiguousarray(np.asarray(L0_ffn_w_proj, np.float32))
    wdtin = np.asarray(L0_attn_W_dt_in, np.float32)
    wdtout = np.asarray(L0_attn_W_dt_out, np.float32)
    bdt = np.asarray(L0_attn_b_dt_out, np.float32)
    wB = np.asarray(L0_attn_W_B, np.float32)
    wC = np.asarray(L0_attn_W_C, np.float32)
    Dv = np.asarray(L0_attn_D, np.float32)
    alog = np.asarray(L0_attn_A_log, np.float32)

    wbct = np.ascontiguousarray(np.concatenate([wdtin, wB, wC], axis=0).T)
    ident = np.eye(128, dtype=np.float32)
    identb = ident.astype(ml_dtypes.bfloat16)
    seln = np.zeros((NST, NST * 128), dtype=ml_dtypes.bfloat16)
    for n in range(NST):
        seln[n, n * 128:(n + 1) * 128] = 1.0
    onesb = np.ones((128, 1), dtype=ml_dtypes.bfloat16)
    ones1 = np.ones((1, 128), dtype=np.float32)
    fconst = np.tile(np.array([[EPS1, EPS3, -0.5, 1.0]], np.float32), (128, 1))

    if "nc" not in _CACHE:
        _CACHE["nc"] = build_program()
    nc, _stack, _n = _CACHE["nc"]
    in_maps = make_in_maps(x, w_fc, w_proj, wbct, wdtout, bdt, Dv, alog,
                           ident, identb, seln, onesb, ones1, fconst)
    with allow_ici_all_to_all():
        res = run_bass_kernel_spmd(nc, in_maps, core_ids=list(range(NCORES)),
                                   cleanup_stale_runs=True)
    out = np.concatenate([r["out"] for r in res.results], axis=0)
    return out.reshape(2, TTOT // 2, D)


def make_in_maps(x, w_fc, w_proj, wbct, wdtout, bdt, Dv, alog,
                 ident, identb, seln, onesb, ones1, fconst):
    in_maps = []
    for k in range(NCORES):
        hs = slice(HSL * k, HSL * (k + 1))
        in_maps.append(dict(
            x=np.ascontiguousarray(x[TOKC * k:TOKC * (k + 1)]),
            w1s=np.ascontiguousarray(w_fc[(HID // NCORES) * k:
                                          (HID // NCORES) * (k + 1)]),
            w2s=np.ascontiguousarray(w_proj[(D // NCORES) * k:
                                            (D // NCORES) * (k + 1)]),
            wbct=wbct,
            wdtoutT=np.ascontiguousarray(wdtout[hs, :].T),
            bdts=np.ascontiguousarray(bdt[hs].reshape(2, 128).T),
            dsl=np.ascontiguousarray(Dv[hs].reshape(2, 128).T),
            alogs=np.ascontiguousarray(
                alog[hs, :].reshape(2, 128, NST).transpose(1, 0, 2)
                .reshape(128, 2 * NST)),
            ident=ident, identb=identb, seln=seln, onesb=onesb, ones1=ones1,
            fconst=fconst,
        ))
    return in_maps
